# version 31
# baseline (speedup 1.0000x reference)
import numpy as np
from contextlib import ExitStack

import bass_rust
import concourse.bass as bass
import concourse.mybir as mybir
from concourse.bass_utils import run_bass_kernel_spmd

F32 = mybir.dt.float32
BF16 = mybir.dt.bfloat16
I32 = mybir.dt.int32
AF = mybir.ActivationFunctionType
ALU = mybir.AluOpType
AX = mybir.AxisListType
IOff = bass_rust.IndirectOffsetOnAxis

SAME_ENGINE_SYNC = True

NC = 8
D = 1024
HD = 64
NH = 16
NKV = 2
SEQ = 16384
BATCH = 2
TOK = BATCH * SEQ
REG = 2048
NT_A = 30
NCH_A = 31
RING_KV = 5
RING_X = 4
EXPB = -30.0
EPS = 1e-5
ADDEND = float(np.exp(np.float64(EXPB)))
AG2_ROWS = 512
AG3_ROWS = 256



def _chunk_rows(cid):
    if cid < 128:
        s = cid * 128
    else:
        s = SEQ + (cid - 128) * 128
    return s, s + 128


def _prep(inputs):
    x = np.ascontiguousarray(np.asarray(inputs["L0_attn_x"], dtype=np.float32).reshape(TOK, D))
    ridx = np.asarray(inputs["L0_ffn_router_indices"], dtype=np.int32)
    rw = np.asarray(inputs["L0_ffn_routing_weights"], dtype=np.float32)

    slots = [[] for _ in range(NC)]
    for t in range(REG):
        for k in range(2):
            slots[int(ridx[t, k])].append((t, k))
    counts = [len(s) for s in slots]
    padt = max(1, int(np.ceil(max(counts) / 128.0)))
    pad = padt * 128
    slot_row = {}
    for e in range(NC):
        for i, (t, k) in enumerate(slots[e]):
            slot_row[(t, k)] = e * pad + i

    base_ids = [16 + NT_A * c for c in range(NC)]

    per_core = []
    for c in range(NC):
        base = base_ids[c]
        blocks = []
        for s in range(NCH_A):
            cid = base + s
            if cid <= 255:
                a, b = _chunk_rows(cid)
                blocks.append(x[a:b])
            else:
                blocks.append(np.zeros((128, D), np.float32))
        x_shard = np.ascontiguousarray(np.concatenate(blocks, axis=0))

        x_own = np.ascontiguousarray(x[256 * c:256 * c + 256])

        ha, hb = _chunk_rows(base - 1)
        halo_l = x[ha:hb]
        special = np.zeros((128, D), np.float32)
        if c == 7:
            special = x[2048:2176]
        ag2_extra = np.ascontiguousarray(np.concatenate([halo_l, special], axis=0))

        p = np.arange(128, dtype=np.int32)
        if c == 0:
            a0_idx = (7 * AG2_ROWS + 128 + p).astype(np.int32)
        else:
            a0_idx = (c * AG2_ROWS + 256 + p).astype(np.int32)
        if c == 0:
            b0_idx = (0 * AG2_ROWS + 384 + p).astype(np.int32)
        else:
            tpos = 256 * c - 128 + p
            b0_idx = ((tpos // 256) * AG2_ROWS + tpos % 256).astype(np.int32)
        if c < 7:
            tpos = 256 * c + 256 + p
            b3_idx = ((tpos // 256) * AG2_ROWS + tpos % 256).astype(np.int32)
        else:
            b3_idx = (7 * AG2_ROWS + 384 + p).astype(np.int32)

        ffn1 = np.zeros((128, padt), np.int32)
        ffn3 = np.zeros((128, padt), np.int32)
        for i, (t, k) in enumerate(slots[c]):
            ffn1[i % 128, i // 128] = t
            ffn3[i % 128, i // 128] = (t // 256) * AG3_ROWS + t % 256
        comb = np.zeros((128, 4), np.int32)
        for tt in range(2):
            for k in range(2):
                for pp in range(128):
                    t = 256 * c + 128 * tt + pp
                    comb[pp, 2 * k + tt] = slot_row[(t, k)]

        meta_v = np.ones((128, 3 * 32), np.float32)
        for i in range(NT_A):
            own = base + i
            for cc in range(3):
                kid = own + cc - 1
                ok = (0 <= kid <= 255) and ((kid < 128) == (own < 128))
                if not ok:
                    meta_v[:, 3 * i + cc] = 0.0
        for bt in range(2):
            for cc in range(3):
                j = bt + cc
                pos0 = 256 * c + (j - 1) * 128
                ok = (0 <= pos0) and (pos0 + 128 <= 2176)
                if not ok:
                    meta_v[:, 3 * (30 + bt) + cc] = 0.0
        vev = np.ones((128, 36), np.float32)
        if base + 30 > 255:
            vev[:, 31] = 0.0
        if c == 0:
            vev[:, 32] = 0.0

        rw_own = rw[256 * c:256 * c + 256].reshape(2, 128, 8).transpose(1, 0, 2).reshape(128, 16)
        idx_own = ridx[256 * c:256 * c + 256].reshape(2, 128, 2).transpose(1, 0, 2).reshape(128, 4)

        per_core.append(dict(
            x_shard=x_shard, x_own=x_own, ag2_extra=ag2_extra,
            a0_idx=np.ascontiguousarray(a0_idx.reshape(128, 1)),
            b0_idx=np.ascontiguousarray(b0_idx.reshape(128, 1)),
            b3_idx=np.ascontiguousarray(b3_idx.reshape(128, 1)),
            ffn_idx1=np.ascontiguousarray(ffn1), ffn_idx3=np.ascontiguousarray(ffn3),
            comb_idx=np.ascontiguousarray(comb),
            meta_v=meta_v, vev=vev,
            rw_own=np.ascontiguousarray(rw_own),
            idx_own=np.ascontiguousarray(idx_own.astype(np.int32)),
        ))

    x_region = np.ascontiguousarray(x[0:REG])

    perm = np.concatenate([np.arange(0, 2048, 2), np.arange(1, 2048, 2)])
    gu = np.asarray(inputs["L0_ffn_gate_up_proj"], dtype=np.float32)
    gub = np.asarray(inputs["L0_ffn_gate_up_proj_bias"], dtype=np.float32)
    dn = np.asarray(inputs["L0_ffn_down_proj"], dtype=np.float32)
    dnb = np.asarray(inputs["L0_ffn_down_proj_bias"], dtype=np.float32)
    consts = np.zeros((128, 3), np.float32)
    consts[:, 0] = EXPB
    consts[:, 1] = EPS
    consts[:, 2] = 1.0
    band02 = np.zeros((128, 2048), np.float32)
    w_ = np.arange(128)[:, None]
    i_ = np.arange(128)[None, :]
    b0m = (w_ >= i_).astype(np.float32)
    b2m = (w_ < i_).astype(np.float32)
    for h in range(8):
        band02[:, 128 * h:128 * (h + 1)] = b0m
        band02[:, 1024 + 128 * h:1024 + 128 * (h + 1)] = b2m
    ident = np.eye(128, dtype=np.float32)
    ones_row = np.ones((1, 256), np.float32)

    qmap = np.zeros(D, np.int64)
    for h in range(16):
        g, hw = h // 8, h % 8
        qmap[hw * 128 + 64 * g + np.arange(64)] = h * 64 + np.arange(64)
    wq_p = np.ascontiguousarray(np.asarray(inputs["L0_attn_Wq"], np.float32)[:, qmap])
    bq_p = np.ascontiguousarray(np.asarray(inputs["L0_attn_bq"], np.float32)[qmap].reshape(1, D))
    womap = np.zeros(D, np.int64)
    for j in range(8):
        g, hw4 = j // 4, j % 4
        for half in range(2):
            h = 8 * g + 4 * half + hw4
            womap[j * 128 + half * 64 + np.arange(64)] = h * 64 + np.arange(64)
    wo_p = np.ascontiguousarray(np.asarray(inputs["L0_attn_Wo"], np.float32)[womap, :])

    in_maps = []
    for c in range(NC):
        pc = per_core[c]
        m = dict(
            x_shard=pc["x_shard"], x_own=pc["x_own"], x_region=x_region,
            ag2_extra=pc["ag2_extra"],
            a0_idx=pc["a0_idx"], b0_idx=pc["b0_idx"], b3_idx=pc["b3_idx"],
            ffn_idx1=pc["ffn_idx1"], ffn_idx3=pc["ffn_idx3"], comb_idx=pc["comb_idx"],
            meta_v=pc["meta_v"], vev=pc["vev"],
            rw_own=pc["rw_own"], idx_own=pc["idx_own"],
            consts=consts, band02=band02, ident=ident, ones_row=ones_row,
            wq=wq_p, wk=np.ascontiguousarray(np.asarray(inputs["L0_attn_Wk"], np.float32)),
            wv=np.ascontiguousarray(np.asarray(inputs["L0_attn_Wv"], np.float32)),
            wo=wo_p,
            bq=bq_p,
            bk=np.ascontiguousarray(np.asarray(inputs["L0_attn_bk"], np.float32).reshape(1, 128)),
            bv=np.ascontiguousarray(np.asarray(inputs["L0_attn_bv"], np.float32).reshape(1, 128)),
            bo=np.ascontiguousarray(np.asarray(inputs["L0_attn_bo"], np.float32).reshape(1, D)),
            gu_e=np.ascontiguousarray(gu[c][:, perm]),
            gub_e=np.ascontiguousarray(gub[c][perm].reshape(1, 2048)),
            d_e=np.ascontiguousarray(dn[c]),
            db_e=np.ascontiguousarray(dnb[c].reshape(1, D)),
        )
        in_maps.append(m)
    return in_maps, padt, base_ids



ENGINES = ("pe", "act", "dve", "pool", "sp")


class Sched:
    def __init__(self):
        self.ops = []
        self.lw = {}
        self.readers = {}
        self.channels = {}

    def add(self, engine, fn, reads=(), writes=(), kind="op", ch=None, tag=""):
        i = len(self.ops)
        raw = set()
        for t in reads:
            if t in self.lw:
                raw.add(self.lw[t])
        for t in writes:
            if t in self.lw:
                raw.add(self.lw[t])
            raw |= self.readers.get(t, set())
        raw.discard(i)
        deps = set()
        dma_waits = {}
        for d in raw:
            dop = self.ops[d]
            if dop["kind"] == "dma":
                dma_waits[dop["ch"]] = self.channels[dop["ch"]]
            elif (not SAME_ENGINE_SYNC and kind == "op" and dop["kind"] == "op"
                  and dop["engine"] == engine):
                continue
            else:
                deps.add(d)
        seq = None
        if kind == "dma":
            assert ch is not None
            self.channels[ch] = self.channels.get(ch, 0) + 1
            seq = self.channels[ch]
        self.ops.append(dict(engine=engine, fn=fn, deps=deps, dma_waits=dma_waits,
                             kind=kind, ch=ch, seq=seq, tag=tag))
        for t in reads:
            self.readers.setdefault(t, set()).add(i)
        for t in writes:
            self.lw[t] = i
            self.readers[t] = set()
        return i


def _materialize(nc, sched, es):
    ops = sched.ops
    targets = set()
    for op in ops:
        targets |= op["deps"]
    eng_count = {e: 0 for e in ENGINES}
    milestone = {}
    for i, op in enumerate(ops):
        if op["kind"] == "op" and i in targets:
            eng_count[op["engine"]] += 1
            milestone[i] = eng_count[op["engine"]]
    eng_sem = {e: es.enter_context(nc.semaphore(f"se_{e}")) for e in ENGINES}
    ch_sem = {c: es.enter_context(nc.semaphore(f"sd_{c}")) for c in sched.channels}
    n_coll = sum(1 for op in ops if op["kind"] == "coll")
    coll_sems = [es.enter_context(nc.semaphore(f"scc_{i}")) for i in range(n_coll)]
    coll_idx = {}
    k = 0
    for i, op in enumerate(ops):
        if op["kind"] == "coll":
            coll_idx[i] = k
            k += 1

    def wait_spec(i):
        op = ops[i]
        if op["kind"] == "op":
            return (eng_sem[op["engine"]], milestone.get(i))
        if op["kind"] == "coll":
            return (coll_sems[coll_idx[i]], 1)
        return None

    streams = {e: [] for e in ENGINES}
    for i, op in enumerate(ops):
        streams[op["engine"]].append(i)

    waited = {e: {} for e in ENGINES}

    def emit_stream(eng_obj, eng_name):
        wmap = waited[eng_name]
        for i in streams[eng_name]:
            op = ops[i]
            specs = []
            for d in sorted(op["deps"]):
                ws = wait_spec(d)
                if ws is not None and ws[1] is not None:
                    specs.append(ws)
            for ch, cnt in sorted(op["dma_waits"].items()):
                specs.append((ch_sem[ch], 16 * cnt))
            for sem, val in specs:
                key = id(sem)
                if wmap.get(key, -1) >= val:
                    continue
                eng_obj.wait_ge(sem, val)
                wmap[key] = val
            if op["fn"] is None:
                continue
            instr = op["fn"](eng_obj)
            if op["kind"] == "dma":
                instr.then_inc(ch_sem[op["ch"]], 16)
            elif op["kind"] == "coll":
                instr.then_inc(coll_sems[coll_idx[i]])
            elif i in milestone:
                instr.then_inc(eng_sem[op["engine"]], 1)
        if eng_name == "sp":
            for c, n in sorted(sched.channels.items()):
                sem = ch_sem[c]
                if wmap.get(id(sem), -1) < 16 * n:
                    eng_obj.wait_ge(sem, 16 * n)
            for s in coll_sems:
                eng_obj.wait_ge(s, 1)
            for e in ENGINES:
                if eng_count[e]:
                    eng_obj.wait_ge(eng_sem[e], eng_count[e])

    return eng_sem, ch_sem, coll_sems, emit_stream



def build_program(padt, debug=False):
    nc = bass.Bass("TRN2", target_bir_lowering=False, debug=debug)
    pad = padt * 128

    dri = {}
    for name, shape, dt in [
        ("x_shard", [NCH_A * 128, D], F32), ("x_own", [256, D], F32),
        ("x_region", [REG, D], F32), ("ag2_extra", [256, D], F32),
        ("a0_idx", [128, 1], I32), ("b0_idx", [128, 1], I32), ("b3_idx", [128, 1], I32),
        ("ffn_idx1", [128, padt], I32), ("ffn_idx3", [128, padt], I32),
        ("comb_idx", [128, 4], I32),
        ("meta_v", [128, 96], F32), ("vev", [128, 36], F32),
        ("rw_own", [128, 16], F32), ("idx_own", [128, 4], I32),
        ("consts", [128, 3], F32), ("band02", [128, 2048], F32),
        ("ident", [128, 128], F32), ("ones_row", [1, 256], F32),
        ("wq", [D, D], F32), ("wk", [D, 128], F32), ("wv", [D, 128], F32), ("wo", [D, D], F32),
        ("bq", [1, D], F32), ("bk", [1, 128], F32), ("bv", [1, 128], F32), ("bo", [1, D], F32),
        ("gu_e", [D, 2048], F32), ("gub_e", [1, 2048], F32),
        ("d_e", [D, D], F32), ("db_e", [1, D], F32),
    ]:
        dri[name] = nc.dram_tensor(name, shape, dt, kind="ExternalInput").ap()

    out_shard = nc.dram_tensor("out_shard", [4096, D], F32, kind="ExternalOutput").ap()

    ag1_in = nc.dram_tensor("ag1_in", [pad, D], BF16)
    ag1_out = nc.dram_tensor("ag1_out", [NC * pad, D], BF16, addr_space="Shared")
    ag2_in = nc.dram_tensor("ag2_in", [AG2_ROWS, D], F32)
    ag2_out = nc.dram_tensor("ag2_out", [NC * AG2_ROWS, D], F32, addr_space="Shared")
    ag3_in = nc.dram_tensor("ag3_in", [AG3_ROWS, D], F32)
    ag3_out = nc.dram_tensor("ag3_out", [NC * AG3_ROWS, D], F32, addr_space="Shared")
    ag4_in = nc.dram_tensor("ag4_in", [pad, D], BF16)
    ag4_out = nc.dram_tensor("ag4_out", [NC * pad, D], BF16, addr_space="Shared")

    es = ExitStack()
    E = es.enter_context

    sb = {}

    def SB(name, shape, dt):
        sb[name] = E(nc.sbuf_tensor("sb_" + name, shape, dt))
        return sb[name]

    SB("wq", [128, 8 * 1024], BF16)
    SB("wk", [128, 8 * 128], BF16)
    SB("wv", [128, 8 * 128], BF16)
    SB("wo", [128, 8 * 1024], BF16)
    SB("gu", [128, 8 * 2048], BF16)
    SB("dw", [128, 8 * 1024], BF16)
    SB("bq", [1, D], BF16)
    SB("bk", [1, 128], BF16)
    SB("bv", [1, 128], BF16)
    SB("bo", [1, D], BF16)
    SB("gub", [1, 2048], BF16)
    SB("db", [1, D], BF16)
    SB("ones", [1, 256], BF16)
    SB("ident", [128, 128], BF16)
    SB("band02", [128, 2048], BF16)
    SB("consts", [128, 3], F32)
    SB("meta_v", [128, 96], F32)
    SB("vev", [128, 36], F32)
    SB("ktA", [128, RING_KV * 128], BF16)
    SB("vaA", [128, RING_KV * 130], BF16)
    SB("ktB", [128, 4 * 128], BF16)
    SB("vaB", [128, 4 * 130], BF16)
    SB("xr", [128, RING_X * 1024], F32)
    SB("xb3", [128, 1024], F32)
    SB("xln", [128, 2 * 1024], BF16)
    SB("xlt", [128, 2 * 1024], BF16)
    SB("qt", [128, 3 * 1024], BF16)
    SB("pt", [128, 6 * 1024], BF16)
    SB("attnT", [128, 1024], BF16)
    SB("rsb", [64, 1024], BF16)
    SB("rcp", [1, 1024], BF16)
    SB("lnr", [1, 1024], F32)
    SB("sq", [128, 1024], BF16)
    SB("xo", [128, 1024], F32)
    SB("x1own", [128, 2 * 1024], F32)
    SB("x2own", [128, 2 * 1024], F32)
    SB("ybuf", [128, 1024], BF16)
    SB("wcell", [128, 16], F32)
    SB("lnc", [128, 32], F32)
    SB("xg", [128, 1024], F32)
    SB("swz", [128, 3 * 1024], F32)
    SB("inter", [128, 1024], BF16)
    SB("itT", [128, 1024], BF16)
    SB("iffn1", [128, padt], I32)
    SB("iffn3", [128, padt], I32)
    SB("icomb", [128, 4], I32)
    SB("ia0", [128, 1], I32)
    SB("ib0", [128, 1], I32)
    SB("ib3", [128, 1], I32)
    SB("rw", [128, 16], F32)
    SB("idxo", [128, 4], I32)
    SB("idxf", [128, 4], F32)

    ps = {}
    ps["pAB"] = E(nc.psum_tensor("pAB", [128, 1024], F32))
    ps["pT"] = E(nc.psum_tensor("pT", [128, 1024], F32))
    ps["s0"] = E(nc.psum_tensor("s0", [128, 1024], F32))
    ps["s1"] = E(nc.psum_tensor("s1", [128, 1024], F32))

    S = Sched()

    def dma(engine, ch, out_ap, in_ap, reads, writes, tag=""):
        S.add(engine, lambda e, o=out_ap, i=in_ap: e.dma_start(out=o, in_=i),
              reads=reads, writes=writes, kind="dma", ch=ch, tag=tag)

    for name in ("vaA", "vaB"):
        S.add("pool", lambda e, t=sb[name]: e.memset(t[:, :], 1.0),
              writes=[name], tag="memset")
    S.add("pool", lambda e: e.memset(sb["xb3"][:, :], 0.0), writes=["xb3"])

    small = [
        ("consts", "consts"), ("meta_v", "meta_v"), ("vev", "vev"),
        ("band02", "band02"), ("ident", "ident"), ("ones", "ones_row"),
        ("bq", "bq"), ("bk", "bk"), ("bv", "bv"), ("bo", "bo"),
        ("gub", "gub_e"), ("db", "db_e"),
        ("iffn1", "ffn_idx1"), ("iffn3", "ffn_idx3"), ("icomb", "comb_idx"),
        ("ia0", "a0_idx"), ("ib0", "b0_idx"), ("ib3", "b3_idx"),
        ("rw", "rw_own"), ("idxo", "idx_own"),
    ]
    for sname, dname in small:
        dma("pool", "ci", sb[sname][:, :], dri[dname], reads=[], writes=[sname], tag=f"ld_{sname}")

    def wload(ch, sname, dname, n):
        out_ap = sb[sname][:, :].rearrange("p (dc n) -> p dc n", dc=8)
        in_ap = dri[dname].rearrange("(dc p) n -> p dc n", p=128)
        dma("pool", ch, out_ap, in_ap, reads=[], writes=[sname], tag=f"ld_{sname}")
    wload("cwf", "gu", "gu_e", 2048)
    wload("cwf", "dw", "d_e", 1024)
    wload("cwa", "wq", "wq", 1024)
    wload("cwa", "wk", "wk", 128)
    wload("cwa", "wv", "wv", 128)
    wload("cwa", "wo", "wo", 1024)

    def emit_ln_stats(src_ap, src_tok, ring, tag):
        cells = sb["lnc"]
        c0 = 8 * ring
        ct = f"lnc{ring}"
        S.add("dve", lambda e: e.reduce_sum(cells[:, c0:c0 + 1], src_ap, axis=AX.X),
              reads=[src_tok], writes=[ct, ct + "u"], tag=f"{tag}_sum")
        S.add("act", lambda e: e.activation(sb["sq"][:, :], src_ap, AF.Square,
                                            scale=float(D) ** -0.5,
                                            accum_out=cells[:, c0 + 1:c0 + 2]),
              reads=[src_tok], writes=["sq", ct + "b"], tag=f"{tag}_sq")
        S.add("dve", lambda e: e.tensor_scalar(cells[:, c0 + 2:c0 + 3], cells[:, c0:c0 + 1],
                                               1.0 / D, None, op0=ALU.mult),
              reads=[ct], writes=[ct + "m"], tag=f"{tag}_m")
        S.add("dve", lambda e: e.scalar_tensor_tensor(cells[:, c0 + 4:c0 + 5], cells[:, c0 + 2:c0 + 3],
                                                      cells[:, c0 + 2:c0 + 3], cells[:, c0 + 1:c0 + 2],
                                                      op0=ALU.mult, op1=ALU.subtract),
              reads=[ct + "m", ct + "b"], writes=[ct + "v"], tag=f"{tag}_nv")
        S.add("act", lambda e: e.activation(cells[:, c0 + 5:c0 + 6], cells[:, c0 + 4:c0 + 5],
                                            AF.Ln, bias=sb["consts"][:, 1:2], scale=-1.0),
              reads=[ct + "v", "consts"], writes=[ct + "l"], tag=f"{tag}_ln")
        S.add("act", lambda e: e.activation(cells[:, c0 + 6:c0 + 7], cells[:, c0 + 5:c0 + 6],
                                            AF.Exp, scale=-0.5),
              reads=[ct + "l"], writes=[ct + "r"], tag=f"{tag}_rstd")
        S.add("dve", lambda e: e.scalar_tensor_tensor(cells[:, c0 + 7:c0 + 8], cells[:, c0 + 2:c0 + 3],
                                                      -1.0, cells[:, c0 + 6:c0 + 7],
                                                      op0=ALU.mult, op1=ALU.mult),
              reads=[ct + "m", ct + "r"], writes=[ct + "n"], tag=f"{tag}_nb")
    def emit_ln_apply(src_ap, src_tok, out_ap, out_tok, ring, tag):
        cells = sb["lnc"]
        c0 = 8 * ring
        ct = f"lnc{ring}"
        S.add("dve", lambda e: e.tensor_scalar(
            out_ap, src_ap, cells[:, c0 + 6:c0 + 7], cells[:, c0 + 7:c0 + 8],
            op0=ALU.mult, op1=ALU.add),
            reads=[src_tok, ct + "r", ct + "n"], writes=[out_tok, ct + "u"], tag=f"{tag}_app")

    def emit_ln(src_ap, src_tok, out_ap, out_tok, ring, tag):
        emit_ln_stats(src_ap, src_tok, ring, tag)
        emit_ln_apply(src_ap, src_tok, out_ap, out_tok, ring, tag)

    def emit_tr(src, src_tok, dst, dst_tok, tag):
        def f(e):
            ins = None
            for dc in range(8):
                bank_half = dc * 128
                ins = e.matmul(ps["pT"][:, bank_half:bank_half + 128],
                               src[:, dc * 128:(dc + 1) * 128], sb["ident"][:, :],
                               start=True, stop=True)
            return ins
        S.add("pe", f, reads=[src_tok, "ident"], writes=["pT"], tag=f"{tag}_tr")
        S.add("dve", lambda e: e.tensor_copy(dst, ps["pT"][:, :]),
              reads=["pT"], writes=[dst_tok], tag=f"{tag}_trev")

    def emit_kv_front(xsrc_ap, xsrc_tok, ln_ring, tag, prestats=False):
        bln = ln_ring % 2
        xln_ap = sb["xln"][:, bln * 1024:(bln + 1) * 1024]
        xlt_ap = sb["xlt"][:, bln * 1024:(bln + 1) * 1024]
        if not prestats:
            emit_ln_stats(xsrc_ap, xsrc_tok, ln_ring, tag + "_ln")
        emit_ln_apply(xsrc_ap, xsrc_tok, xln_ap, f"xln{bln}", ln_ring, tag + "_ln")
        emit_tr(xln_ap, f"xln{bln}", xlt_ap, f"xlt{bln}", tag)

    def emit_kv_back(ln_ring, kt_ap, kt_tok, va_ap, va_tok, vev_col,
                     with_q, qt_ring, tag, with_kv=True):
        bln = ln_ring % 2
        xlt_ap = sb["xlt"][:, bln * 1024:(bln + 1) * 1024]
        xlt_tok = f"xlt{bln}"

        if with_q == "single":
            def fq(e):
                ins = None
                for half in range(2):
                    for qcl in range(4):
                        qc = half * 4 + qcl
                        col = half * 512 + qcl * 128
                        ins = e.matmul(ps["pAB"][:, col:col + 128],
                                       sb["bq"][0:1, qc * 128:(qc + 1) * 128],
                                       sb["ones"][0:1, 0:128], start=True, stop=False)
                        for dc in range(8):
                            ins = e.matmul(ps["pAB"][:, col:col + 128],
                                           sb["wq"][:, dc * 1024 + qc * 128:dc * 1024 + (qc + 1) * 128],
                                           xlt_ap[:, dc * 128:(dc + 1) * 128],
                                           start=False, stop=(dc == 7))
                return ins
            S.add("pe", fq, reads=[xlt_tok, "wq", "bq", "ones"], writes=["pA", "pB"],
                  tag=f"{tag}_qt")
            qdst = sb["qt"][:, qt_ring * 1024: (qt_ring + 1) * 1024]
            S.add("dve", lambda e, d=qdst: e.tensor_copy(d, ps["pAB"][:, :]),
                  reads=["pA", "pB"], writes=[f"qt{qt_ring}_0", f"qt{qt_ring}_1"],
                  tag=f"{tag}_qtev")
        elif with_q == "pair":
            xlt2 = sb["xlt"][:, :].rearrange("p (b n) -> p b n", b=2)
            qr0, qr1 = (qt_ring + 2) % 3, qt_ring
            for half in range(2):
                def fqp(e, half=half):
                    ins = None
                    for qcl in range(4):
                        qc = half * 4 + qcl
                        col = qcl * 256
                        ins = e.matmul(ps["pAB"][:, col:col + 256],
                                       sb["bq"][0:1, qc * 128:(qc + 1) * 128],
                                       sb["ones"][0:1, 0:256], start=True, stop=False)
                        for dc in range(8):
                            ins = e.matmul(ps["pAB"][:, col:col + 256],
                                           sb["wq"][:, dc * 1024 + qc * 128:dc * 1024 + (qc + 1) * 128],
                                           xlt2[:, :, dc * 128:(dc + 1) * 128],
                                           start=False, stop=(dc == 7))
                    return ins
                S.add("pe", fqp, reads=["xlt0", "xlt1", "wq", "bq", "ones"],
                      writes=["pA", "pB"], tag=f"{tag}_qtp{half}")
                psrc = ps["pAB"][:, :].rearrange("p (q b n) -> p q b n", q=4, b=2)
                for bi, qr in ((0, qr0), (1, qr1)):
                    qdst = sb["qt"][:, qr * 1024 + half * 512: qr * 1024 + (half + 1) * 512]
                    S.add("dve", lambda e, d=qdst, s=psrc, bi=bi: e.tensor_copy(
                        d.rearrange("p (q n) -> p q n", q=4), s[:, :, bi, :]),
                        reads=["pA", "pB"], writes=[f"qt{qr}_{half}"],
                        tag=f"{tag}_qtpev{half}{bi}")

        if not with_kv:
            return

        def fk(e):
            ins = e.matmul(ps["pAB"][:, 0:128], sb["bk"][0:1, :], sb["ones"][0:1, 0:128],
                           start=True, stop=False)
            for dc in range(8):
                ins = e.matmul(ps["pAB"][:, 0:128],
                               sb["wk"][:, dc * 128:(dc + 1) * 128],
                               xlt_ap[:, dc * 128:(dc + 1) * 128],
                               start=False, stop=(dc == 7))
            return ins
        S.add("pe", fk, reads=[xlt_tok, "wk", "bk", "ones"], writes=["pA"], tag=f"{tag}_kt")
        S.add("dve", lambda e: e.tensor_copy(kt_ap, ps["pAB"][:, 0:128]),
              reads=["pA"], writes=[kt_tok], tag=f"{tag}_ktev")

        def fv(e):
            ins = e.matmul(ps["pAB"][:, 512:640], sb["ones"][0:1, 0:128], sb["bv"][0:1, :],
                           start=True, stop=False)
            for dc in range(8):
                ins = e.matmul(ps["pAB"][:, 512:640],
                               xlt_ap[:, dc * 128:(dc + 1) * 128],
                               sb["wv"][:, dc * 128:(dc + 1) * 128],
                               start=False, stop=(dc == 7))
            return ins
        S.add("pe", fv, reads=[xlt_tok, "wv", "bv", "ones"], writes=["pB"], tag=f"{tag}_v")
        for g in range(2):
            S.add("dve", lambda e, g=g: e.tensor_scalar(
                va_ap[:, g * 65:g * 65 + 64], ps["pAB"][:, 512 + g * 64:512 + (g + 1) * 64],
                sb["vev"][:, vev_col:vev_col + 1], None, op0=ALU.mult),
                reads=["pB", "vev"], writes=[va_tok], tag=f"{tag}_vev{g}")

    def emit_kv(xsrc_ap, xsrc_tok, ln_ring, kt_ap, kt_tok, va_ap, va_tok, vev_col,
                with_q, qt_ring, tag, with_kv=True, prestats=False):
        emit_kv_front(xsrc_ap, xsrc_tok, ln_ring, tag, prestats=prestats)
        emit_kv_back(ln_ring, kt_ap, kt_tok, va_ap, va_tok, vev_col,
                     "single" if with_q else False, qt_ring, tag, with_kv=with_kv)

    BLKMAP = {0: 0, 2: 1, 1: 2}

    def emit_sc_one(g, c, qt_ring, kt_list, meta_tile, tag):
        stn = {0: "s0", 2: "s1", 1: "s0"}
        for c in (c,):
                blk = 3 * g + BLKMAP[c]
                pn = stn[c]
                kt_ap, kt_tok = kt_list[c]

                def fs(e, g=g, c=c, pn=pn, kt_ap=kt_ap):
                    ins = None
                    for half in range(2):
                        rhs = sb["qt"][64 * g:64 * g + 64,
                                       qt_ring * 1024 + half * 512:qt_ring * 1024 + (half + 1) * 512]
                        ins = e.matmul(ps[pn][:, half * 512:(half + 1) * 512],
                                       kt_ap[64 * g:64 * g + 64, :], rhs,
                                       start=True, stop=True)
                    return ins
                S.add("pe", fs, reads=[kt_tok, f"qt{qt_ring}_0", f"qt{qt_ring}_1"],
                      writes=[pn], tag=f"{tag}_sc{g}{c}")
                ptb = sb["pt"][:, blk * 1024:(blk + 1) * 1024]
                mc = meta_tile * 3 + c
                S.add("act", lambda e, pn=pn, ptb=ptb, mc=mc: e.activation(
                    ptb, ps[pn][:, :], AF.Exp, bias=sb["consts"][:, 0:1],
                    scale=sb["meta_v"][:, mc:mc + 1]),
                    reads=[pn, "consts", "meta_v"], writes=[f"pt{blk}"],
                    tag=f"{tag}_exp{g}{c}")
                if c == 2:
                    ptm = sb["pt"][:, (3 * g) * 1024:(3 * g + 2) * 1024]
                    S.add("dve", lambda e, ptm=ptm: e.tensor_tensor(
                        ptm, ptm, sb["band02"][:, :], op=ALU.mult),
                        reads=[f"pt{3 * g}", f"pt{3 * g + 1}", "band02"],
                        writes=[f"pt{3 * g}", f"pt{3 * g + 1}"], tag=f"{tag}_mb{g}")

    def emit_sc_group(g, qt_ring, kt_list, meta_tile, tag):
        for c in (0, 2, 1):
            emit_sc_one(g, c, qt_ring, kt_list, meta_tile, tag)

    def emit_tile_sc(qt_ring, kt_list, meta_tile, tag):
        emit_sc_group(0, qt_ring, kt_list, meta_tile, tag)
        emit_sc_group(1, qt_ring, kt_list, meta_tile, tag)

    def emit_pv_group(g, va_list, tag):
        for par, pn in ((0, "pA"), (1, "pB")):
            def fpv(e, g=g, par=par):
                ins = None
                for c in range(3):
                    va_ap, _ = va_list[c]
                    ins = e.matmul(ps["pAB"][0:65, par * 512:(par + 1) * 512],
                                   va_ap[:, g * 65:(g + 1) * 65],
                                   sb["pt"][:, (3 * g + BLKMAP[c]) * 1024 + par * 512:
                                            (3 * g + BLKMAP[c]) * 1024 + (par + 1) * 512],
                                   start=(c == 0), stop=(c == 2))
                return ins
            S.add("pe", fpv,
                  reads=[va_list[0][1], va_list[1][1], va_list[2][1],
                         f"pt{3 * g + 0}", f"pt{3 * g + 1}", f"pt{3 * g + 2}"],
                  writes=[pn], tag=f"{tag}_pv{g}{par}")
        S.add("act", lambda e: e.activation(
            sb["lnr"][0:1, :], ps["pAB"][64:65, :], AF.Ln),
            reads=["pA", "pB"], writes=["lnr0", "lnr1"], tag=f"{tag}_ln{g}")

        def fex(e):
            with nc.allow_low_precision(reason="bf16 softmax-normalizer operand"):
                return e.activation(sb["rcp"][0:1, :], sb["lnr"][0:1, :],
                                    AF.Exp, scale=-1.0)
        S.add("act", fex, reads=["lnr0", "lnr1"], writes=["rcp0", "rcp1"],
              tag=f"{tag}_rc{g}")

    def emit_rb(g, tag):
        def frb(e):
            ins = None
            for par in range(2):
                ins = e.matmul(ps["pT"][0:64, par * 512:(par + 1) * 512],
                               sb["ones"][0:1, 0:64],
                               sb["rcp"][0:1, par * 512:(par + 1) * 512],
                               start=True, stop=True)
            return ins
        S.add("pe", frb, reads=["rcp0", "rcp1", "ones"], writes=["pT"], tag=f"{tag}_rb{g}")
        S.add("act", lambda e: e.activation(sb["rsb"][:, :], ps["pT"][0:64, :], AF.Identity),
              reads=["pT"], writes=["rsb"], tag=f"{tag}_rev{g}")

    def emit_tnorm(g, tag):
        for par, pn in ((0, "pA"), (1, "pB")):
            dst = sb["attnT"][par * 64:par * 64 + 64, g * 512:(g + 1) * 512]
            S.add("dve", lambda e, par=par, dst=dst: e.tensor_tensor(
                dst, ps["pAB"][0:64, par * 512:(par + 1) * 512],
                sb["rsb"][0:64, par * 512:(par + 1) * 512],
                op=ALU.mult),
                reads=[pn, "rsb"], writes=["attnT"], tag=f"{tag}_tn{g}{par}")

    def emit_norm_group(g, tag):
        emit_rb(g, tag)
        emit_tnorm(g, tag)

    def emit_wo_epi(x_ap, x_tok, dst_kind, dst_info, tag):
        def fwo(e):
            ins = None
            for half in range(2):
                ins = e.matmul(ps["pAB"][:, half * 512:(half + 1) * 512],
                               sb["ones"][0:1, 0:128],
                               sb["bo"][0:1, half * 512:(half + 1) * 512],
                               start=True, stop=False)
            for dc in range(8):
                for half in range(2):
                    ins = e.matmul(ps["pAB"][:, half * 512:(half + 1) * 512],
                                   sb["attnT"][:, dc * 128:(dc + 1) * 128],
                                   sb["wo"][:, dc * 1024 + half * 512:dc * 1024 + (half + 1) * 512],
                                   start=False, stop=(dc == 7))
            return ins
        S.add("pe", fwo, reads=["attnT", "wo", "bo", "ones"], writes=["pA", "pB"],
              tag=f"{tag}_wo")

        if dst_kind == "out":
            row0 = dst_info
            xo_ap = sb["xo"][:, :]
            S.add("dve", lambda e, xo_ap=xo_ap, x_ap=x_ap: e.tensor_tensor(
                xo_ap, ps["pAB"][:, :], x_ap, op=ALU.add),
                reads=["pA", "pB", x_tok], writes=["xo0"], tag=f"{tag}_epi")
            dma("sp", "ost", out_shard[row0:row0 + 128, :], xo_ap,
                reads=["xo0"], writes=[f"ost{row0}"], tag=f"{tag}_st")
        else:
            tt = dst_info
            x2_ap = sb["x2own"][:, tt * 1024:(tt + 1) * 1024]
            S.add("dve", lambda e, x2_ap=x2_ap, x_ap=x_ap: e.tensor_tensor(
                x2_ap, ps["pAB"][:, :], x_ap, op=ALU.add),
                reads=["pA", "pB", x_tok], writes=[f"x2own{tt}"], tag=f"{tag}_epi")
            dma("sp", "a3c", ag3_in.ap()[tt * 128:(tt + 1) * 128, :], x2_ap,
                reads=[f"x2own{tt}"], writes=[f"ag3in{tt}"], tag=f"{tag}_st")

    def emit_tile_pv(va_list, x_ap, x_tok, dst_kind, dst_info, tag):
        emit_pv_group(0, va_list, tag)
        emit_norm_group(0, tag)
        emit_pv_group(1, va_list, tag)
        emit_norm_group(1, tag)
        emit_wo_epi(x_ap, x_tok, dst_kind, dst_info, tag)

    def emit_tile(qt_ring, kt_list, va_list, meta_tile, x_ap, x_tok, dst_kind, dst_info, tag):
        emit_tile_sc(qt_ring, kt_list, meta_tile, tag)
        emit_tile_pv(va_list, x_ap, x_tok, dst_kind, dst_info, tag)

    def emit_ffn_head(call, t, src_dram, idx_sb_name, tag):
        ring = t % 2
        xg_ap = sb["xg"][:, :]
        xg_tok = "xg0"
        S.add("pool", lambda e: e.indirect_dma_start(
            out=xg_ap, out_offset=None, in_=src_dram,
            in_offset=IOff(ap=sb[idx_sb_name][:, t:t + 1], axis=0)),
            reads=[idx_sb_name, f"agsrc{call}"], writes=[xg_tok], kind="dma",
            ch="xg0", tag=f"{tag}_g")
        fln_ap = sb["xln"][:, ring * 1024:(ring + 1) * 1024]
        flt_ap = sb["xlt"][:, ring * 1024:(ring + 1) * 1024]
        emit_ln(xg_ap, xg_tok, fln_ap, f"xln{ring}", 2 + ring, tag + "_ln")
        emit_tr(fln_ap, f"xln{ring}", flt_ap, f"xlt{ring}", tag)
        flt_tok = f"xlt{ring}"

        for half, pn in ((0, "s0"), (1, "s1")):
            def fg(e, half=half, pn=pn, flt_ap=flt_ap):
                ins = None
                for q in range(2):
                    col = half * 1024 + q * 512
                    ins = e.matmul(ps[pn][:, q * 512:(q + 1) * 512],
                                   sb["ones"][0:1, 0:128],
                                   sb["gub"][0:1, col:col + 512], start=True, stop=False)
                    for dc in range(8):
                        ins = e.matmul(ps[pn][:, q * 512:(q + 1) * 512],
                                       flt_ap[:, dc * 128:(dc + 1) * 128],
                                       sb["gu"][:, dc * 2048 + col:dc * 2048 + col + 512],
                                       start=False, stop=(dc == 7))
                return ins
            S.add("pe", fg, reads=[flt_tok, "gu", "gub", "ones"], writes=[pn],
                  tag=f"{tag}_gu{half}")
    def emit_ffn_head2(call, t, tag):
        A = sb["swz"][:, 0:1024]
        C = sb["swz"][:, 2048:3072]
        S.add("dve", lambda e, A=A: e.tensor_scalar(A, ps["s0"][:, :], 7.0, None, op0=ALU.min),
              reads=["s0"], writes=["swzA"], tag=f"{tag}_gmin")
        S.add("dve", lambda e, C=C: e.tensor_scalar(C, ps["s1"][:, :], 7.0, -7.0,
                                                    op0=ALU.min, op1=ALU.max),
              reads=["s1"], writes=["swzC"], tag=f"{tag}_uclip")

    def emit_ffn_tail1(call, t, tag):
        A = sb["swz"][:, 0:1024]
        B = sb["swz"][:, 1024:2048]
        C = sb["swz"][:, 2048:3072]
        S.add("act", lambda e, A=A, B=B: e.activation(B, A, AF.Exp, scale=-1.702),
              reads=["swzA"], writes=["swzB"], tag=f"{tag}_exp")
        S.add("act", lambda e, B=B: e.activation(B, B, AF.Ln, bias=sb["consts"][:, 2:3]),
              reads=["swzB", "consts"], writes=["swzB"], tag=f"{tag}_ln1p")
        S.add("act", lambda e, B=B: e.activation(B, B, AF.Exp, scale=-1.0),
              reads=["swzB"], writes=["swzB"], tag=f"{tag}_sig")
        S.add("dve", lambda e, A=A, B=B: e.tensor_tensor(A, A, B, op=ALU.mult),
              reads=["swzA", "swzB"], writes=["swzA"], tag=f"{tag}_glu")
        S.add("dve", lambda e, A=A, C=C: e.scalar_tensor_tensor(
            sb["inter"][:, :], C, 1.0, A, op0=ALU.add, op1=ALU.mult),
            reads=["swzA", "swzC"], writes=["inter"], tag=f"{tag}_inter")

    def emit_ffn_tail2(call, t, contrib, tag):
        emit_tr(sb["inter"][:, :], "inter", sb["itT"][:, :], "itT", tag + "_it")

        def fd(e):
            ins = None
            for half in range(2):
                ins = e.matmul(ps["pAB"][:, half * 512:(half + 1) * 512],
                               sb["ones"][0:1, 0:128],
                               sb["db"][0:1, half * 512:(half + 1) * 512],
                               start=True, stop=False)
            for ic in range(8):
                for half in range(2):
                    ins = e.matmul(ps["pAB"][:, half * 512:(half + 1) * 512],
                                   sb["itT"][:, ic * 128:(ic + 1) * 128],
                                   sb["dw"][:, ic * 1024 + half * 512:ic * 1024 + (half + 1) * 512],
                                   start=False, stop=(ic == 7))
            return ins
        S.add("pe", fd, reads=["itT", "dw", "db", "ones"], writes=["pA", "pB"],
              tag=f"{tag}_dn")
        S.add("act", lambda e: e.activation(sb["inter"][:, :], ps["pAB"][:, :], AF.Identity),
              reads=["pA", "pB"], writes=["inter"], tag=f"{tag}_ycev")
        dma("sp", contrib[0], contrib[1][t * 128:(t + 1) * 128, :], sb["inter"][:, :],
            reads=["inter"], writes=[f"{contrib[2]}{t}"], tag=f"{tag}_yst")

    def emit_wcompute():
        S.add("dve", lambda e: e.tensor_copy(sb["idxf"][:, :], sb["idxo"][:, :]),
              reads=["idxo"], writes=["idxf"], tag="w_cast")
        for tt in range(2):
            for k in range(2):
                col = 2 * tt + k
                acc_a, acc_b = 4 + 2 * col, 5 + 2 * col
                for ei in range(8):
                    eq = 14
                    S.add("dve", lambda e, col=col, ei=ei, eq=eq: e.tensor_scalar(
                        sb["wcell"][:, eq:eq + 1], sb["idxf"][:, col:col + 1],
                        float(ei), None, op0=ALU.is_equal),
                        reads=["idxf"], writes=["weq"], tag=f"w_eq{col}_{ei}")
                    rcol = 8 * tt + ei
                    if ei == 0:
                        S.add("dve", lambda e, acc_a=acc_a, eq=eq, rcol=rcol: e.tensor_tensor(
                            sb["wcell"][:, acc_a:acc_a + 1], sb["wcell"][:, eq:eq + 1],
                            sb["rw"][:, rcol:rcol + 1], op=ALU.mult),
                            reads=["weq", "rw"], writes=[f"wacc{col}a"], tag=f"w_a{col}_{ei}")
                    else:
                        src_t, dst_t = ((f"wacc{col}a", f"wacc{col}b") if ei % 2 == 1
                                        else (f"wacc{col}b", f"wacc{col}a"))
                        src_c = acc_a if ei % 2 == 1 else acc_b
                        dst_c = acc_b if ei % 2 == 1 else acc_a
                        S.add("dve", lambda e, s=src_c, d=dst_c, eq=eq, rcol=rcol:
                              e.scalar_tensor_tensor(
                                  sb["wcell"][:, d:d + 1], sb["wcell"][:, eq:eq + 1],
                                  sb["rw"][:, rcol:rcol + 1], sb["wcell"][:, s:s + 1],
                                  op0=ALU.mult, op1=ALU.add),
                              reads=["weq", "rw", src_t], writes=[dst_t], tag=f"w_a{col}_{ei}")
                final_c = acc_b if 7 % 2 == 1 else acc_a
                final_t = f"wacc{col}b" if 7 % 2 == 1 else f"wacc{col}a"
                S.add("dve", lambda e, col=col, final_c=final_c: e.tensor_scalar(
                    sb["wcell"][:, col:col + 1], sb["wcell"][:, final_c:final_c + 1],
                    0.5, None, op0=ALU.mult),
                    reads=[final_t], writes=[f"w{col}"], tag=f"w_f{col}")

    def emit_combine(agout, xbase_sb, xbase_tok_fmt, dst_kind, ygroup, tag):
        for tt in range(2):
            A = sb["swz"][:, 0:1024]
            y0 = sb["ybuf"][:, :]
            y1 = sb["ybuf"][:, :]
            S.add("pool", lambda e, col=tt: e.indirect_dma_start(
                out=sb["ybuf"][:, :], out_offset=None,
                in_=agout, in_offset=IOff(ap=sb["icomb"][:, col:col + 1], axis=0)),
                reads=["icomb", f"agdone{tag}"], writes=["ybg0", "ybg1"], kind="dma",
                ch=f"y{ygroup}0", tag=f"{tag}_yg{tt}0")
            xb = xbase_sb[:, tt * 1024:(tt + 1) * 1024]
            S.add("dve", lambda e, A=A, y0=y0, xb=xb, tt=tt: e.scalar_tensor_tensor(
                A, y0, sb["wcell"][:, 2 * tt:2 * tt + 1], xb, op0=ALU.mult, op1=ALU.add),
                reads=["ybg0", f"w{2 * tt}", xbase_tok_fmt.format(tt)],
                writes=["swzA"], tag=f"{tag}_c1_{tt}")
            S.add("pool", lambda e, col=2 + tt: e.indirect_dma_start(
                out=sb["ybuf"][:, :], out_offset=None,
                in_=agout, in_offset=IOff(ap=sb["icomb"][:, col:col + 1], axis=0)),
                reads=["icomb", f"agdone{tag}"], writes=["ybg0", "ybg1"], kind="dma",
                ch=f"y{ygroup}0", tag=f"{tag}_yg{tt}1")
            if dst_kind == "x1":
                dst_ap = sb["x1own"][:, tt * 1024:(tt + 1) * 1024]
                dst_tok = f"x1own{tt}"
            else:
                dst_ap = sb["swz"][:, 2048:3072]
                dst_tok = "swzC"
            S.add("dve", lambda e, dst_ap=dst_ap, y1=y1, A=A, tt=tt: e.scalar_tensor_tensor(
                dst_ap, y1, sb["wcell"][:, 2 * tt + 1:2 * tt + 2], A,
                op0=ALU.mult, op1=ALU.add),
                reads=["ybg1", f"w{2 * tt + 1}", "swzA"], writes=[dst_tok],
                tag=f"{tag}_c2_{tt}")
            if dst_kind == "x1":
                dma("sp", "a2c", ag2_in.ap()[tt * 128:(tt + 1) * 128, :], dst_ap,
                    reads=[dst_tok], writes=[f"ag2in{tt}"], tag=f"{tag}_st{tt}")
            else:
                dma("sp", "ost", out_shard[3840 + tt * 128:3840 + (tt + 1) * 128, :], dst_ap,
                    reads=[dst_tok], writes=["outhbm"], tag=f"{tag}_st{tt}")

    S.add("pool", lambda e: e.engine_nop(), reads=[], writes=["agsrc1"], tag="src1")

    def ffn_h1(call, t):
        if call == 1:
            emit_ffn_head(1, t, dri["x_region"], "iffn1", f"f1t{t}")
        else:
            if t == 0:
                S.add("pool", lambda e: e.engine_nop(), reads=["agdoneC3"],
                      writes=["agsrc3"], tag="src3")
            emit_ffn_head(3, t, ag3_out.ap(), "iffn3", f"f3t{t}")

    def ffn_t2(call, t):
        if call == 1:
            emit_ffn_tail2(1, t, ("a1c", ag1_in.ap(), "ag1in"), f"f1t{t}")
            if t == padt - 1:
                S.add("pool", lambda e: e.collective_compute(
                    "AllGather", ALU.bypass, replica_groups=[list(range(NC))],
                    ins=[ag1_in.ap().opt()], outs=[ag1_out.ap().opt()]),
                    reads=[f"ag1in{tt}" for tt in range(padt)], writes=["agdoneC1"],
                    kind="coll", tag="AG1")
        else:
            emit_ffn_tail2(3, t, ("a4c", ag4_in.ap(), "ag4in"), f"f3t{t}")
            if t == padt - 1:
                S.add("pool", lambda e: e.collective_compute(
                    "AllGather", ALU.bypass, replica_groups=[list(range(NC))],
                    ins=[ag4_in.ap().opt()], outs=[ag4_out.ap().opt()]),
                    reads=[f"ag4in{tt}" for tt in range(padt)], writes=["agdoneC4"],
                    kind="coll", tag="AG4")

    def x_load(slot, ring, tag):
        dma("sp", f"x{ring}", sb["xr"][:, ring * 1024:(ring + 1) * 1024],
            dri["x_shard"][slot * 128:(slot + 1) * 128, :],
            reads=[], writes=[f"xr{ring}"], tag=tag)

    def kvA_front(slot, ring_x, tag):
        x_ap = sb["xr"][:, ring_x * 1024:(ring_x + 1) * 1024]
        emit_kv_front(x_ap, f"xr{ring_x}", slot % 2, tag, prestats=True)

    def kvA_back(slot, with_q, tag):
        pos = (slot + 1) % RING_KV
        kt_ap = sb["ktA"][:, pos * 128:(pos + 1) * 128]
        va_ap = sb["vaA"][:, pos * 130:pos * 130 + 130]
        q = "pair" if (with_q and slot % 2 == 1) else False
        emit_kv_back(slot % 2, kt_ap, f"ktA{pos}", va_ap, f"vaA{pos}",
                     1 + slot, q, (slot % 3), tag)

    def kvA(slot, ring_x, with_q, tag, x_ap=None, x_tok=None, prestats=False):
        pos = (slot + 1) % RING_KV
        kt_ap = sb["ktA"][:, pos * 128:(pos + 1) * 128]
        va_ap = sb["vaA"][:, pos * 130:pos * 130 + 130]
        if x_ap is None:
            x_ap = sb["xr"][:, ring_x * 1024:(ring_x + 1) * 1024]
            x_tok = f"xr{ring_x}"
        emit_kv(x_ap, x_tok, slot % 2, kt_ap, f"ktA{pos}", va_ap, f"vaA{pos}",
                1 + slot, bool(with_q), (slot % 3), tag, prestats=prestats)

    def ktva(slot):
        pos = (slot + 1) % RING_KV
        return ((sb["ktA"][:, pos * 128:(pos + 1) * 128], f"ktA{pos}"),
                (sb["vaA"][:, pos * 130:pos * 130 + 130], f"vaA{pos}"))

    def tileA(i, tag):
        kts, vas = [], []
        for c in range(3):
            kt, va = ktva(i - 1 + c)
            kts.append(kt)
            vas.append(va)
        x_ap = sb["xr"][:, (i % RING_X) * 1024:((i % RING_X) + 1) * 1024]
        emit_tile(i % 3, kts, vas, i, x_ap, f"xr{i % RING_X}", "out",
                  i * 128, tag)

    def tileA_sc(i, tag):
        kts = []
        for c in range(3):
            kt, va = ktva(i - 1 + c)
            kts.append(kt)
        emit_tile_sc(i % 3, kts, i, tag)

    def tile_kts(i):
        kts = []
        for c in range(3):
            kt, va = ktva(i - 1 + c)
            kts.append(kt)
        return kts

    def tile_vas(i):
        vas = []
        for c in range(3):
            kt, va = ktva(i - 1 + c)
            vas.append(va)
        return vas

    def tileA_pvmm(i, tag):
        x_ap = sb["xr"][:, (i % RING_X) * 1024:((i % RING_X) + 1) * 1024]
        S.add("dve", lambda e, x_ap=x_ap: e.tensor_copy(sb["xo"][:, :], x_ap),
              reads=[f"xr{i % RING_X}"], writes=["xo0"], tag=f"{tag}_xcp")
        emit_pv_group(0, tile_vas(i), tag)

    def tileA_tail(i, tag):
        vas = tile_vas(i)
        emit_norm_group(0, tag)
        emit_pv_group(1, vas, tag)
        emit_norm_group(1, tag)
        emit_wo_epi(sb["xo"][:, :], "xo0", "out", i * 128, tag)

    def statsA(slot):
        x_ap = sb["xr"][:, (slot % RING_X) * 1024:((slot % RING_X) + 1) * 1024]
        emit_ln_stats(x_ap, f"xr{slot % RING_X}", slot % 2, f"st{slot}")

    x_load(0, 0, "x0")
    statsA(0)
    x_load(1, 1, "x1")
    statsA(1)
    kvA_front(0, 0, "kv0")
    kvA_back(0, False, "kv0")
    x_load(2, 2, "x2")
    statsA(2)
    kvA_front(1, 1, "kv1")
    kvA_back(1, True, "kv1")
    kvA_front(2, 2, "kv2")
    kvA_back(2, True, "kv2")
    x_load(3, 3, "x3")
    statsA(3)
    kvA_front(3, 3, "kv3")
    kvA_back(3, True, "kv3")
    x_load(4, 0, "x4")
    statsA(4)
    kvA_front(4, 0, "kv4")
    kvA_back(4, True, "kv4")
    stats_done = 4
    kts1 = tile_kts(1)
    emit_sc_group(0, 1, kts1, 1, "tA1")
    emit_sc_group(1, 1, kts1, 1, "tA1")
    tileA_pvmm(1, "tA1")
    tileA_tail(1, "tA1")

    after = {4: "comb1", 7: "segB", 10: "ffn3", 13: "comb3"}

    def ffn_block(call):
        ffn_h1(call, 0)
        emit_ffn_head2(call, 0, f"f{call}t0")
        for t in range(padt):
            if t + 1 < padt:
                ffn_h1(call, t + 1)
            emit_ffn_tail1(call, t, f"f{call}t{t}")
            if t + 1 < padt:
                emit_ffn_head2(call, t + 1, f"f{call}t{t + 1}")
            ffn_t2(call, t)

    ffn_block(1)

    kv_done = 4
    pending = None
    xlt_dirty = [True]
    load_done = 4

    def ensure_load(k):
        nonlocal load_done
        k = min(k, NCH_A - 1)
        while load_done < k:
            load_done += 1
            x_load(load_done, load_done % RING_X, f"x{load_done}")

    def refront(slot):
        x_ap = sb["xr"][:, (slot % RING_X) * 1024:((slot % RING_X) + 1) * 1024]
        x_load(slot, slot % RING_X, f"xr{slot}")
        emit_kv_front(x_ap, f"xr{slot % RING_X}", 2 + slot % 2, f"rf{slot}",
                      prestats=False)
    for i in range(2, NT_A):
        def advance_kv():
            nonlocal kv_done, stats_done
            k = kv_done + 1
            if stats_done < k:
                ensure_load(k)
                statsA(k)
                stats_done = k
            kvA_front(k, k % RING_X, f"kv{k}")
            if k % 2 == 1 and xlt_dirty[0]:
                refront(k - 1)
                xlt_dirty[0] = False
            kvA_back(k, k <= 29, f"kv{k}")
            kv_done = k
        while kv_done < i + 1:
            advance_kv()
        do_kv = (i != 7 and kv_done < min(i + 2, NCH_A - 1))
        if do_kv:
            k = i + 2
            if stats_done < k:
                ensure_load(k)
                statsA(k)
                stats_done = k
            kvA_front(k, k % RING_X, f"kv{k}")
            if k % 2 == 1 and xlt_dirty[0]:
                refront(k - 1)
                xlt_dirty[0] = False
        k3 = stats_done + 1
        if k3 == i + 3 and load_done >= k3 and k3 <= NCH_A - 1:
            statsA(k3)
            stats_done = k3
        kts_i = tile_kts(i)
        emit_sc_group(0, i % 3, kts_i, i, f"tA{i}")
        if pending is not None:
            tileA_tail(pending, f"tA{pending}")
            pending = None
        emit_sc_group(1, i % 3, kts_i, i, f"tA{i}")
        if do_kv:
            k = i + 2
            kvA_back(k, k <= 29, f"kv{k}")
            kv_done = k
        tileA_pvmm(i, f"tA{i}")
        pending = i
        if stats_done < min(kv_done + 1, NCH_A - 1):
            k2 = stats_done + 1
            ensure_load(k2)
            statsA(k2)
            stats_done = k2
        ensure_load(stats_done + 1)
        ph = after.get(i)
        if ph == "ffn3":
            if pending is not None:
                tileA_tail(pending, f"tA{pending}")
                pending = None
            ffn_block(3)
            xlt_dirty[0] = True
        elif ph == "comb1":
            emit_wcompute()
            S.add("pool", lambda e: e.engine_nop(), reads=["agdoneC1"],
                  writes=["agdonecomb1"], tag="agw1")
            for tt in range(2):
                dma("sp", "xoc", sb["x1own"][:, tt * 1024:(tt + 1) * 1024],
                    dri["x_own"][tt * 128:(tt + 1) * 128, :],
                    reads=[], writes=[f"x1own{tt}"], tag=f"xown{tt}")
            emit_combine(ag1_out.ap(), sb["x1own"], "x1own{}", "x1", 0, "comb1")
            dma("sp", "a2c", ag2_in.ap()[256:512, :], dri["ag2_extra"],
                reads=[], writes=["ag2inx"], tag="ag2extra")
            S.add("pool", lambda e: e.collective_compute(
                "AllGather", ALU.bypass, replica_groups=[list(range(NC))],
                ins=[ag2_in.ap().opt()], outs=[ag2_out.ap().opt()]),
                reads=["ag2in0", "ag2in1", "ag2inx"], writes=["agdoneC2"],
                kind="coll", tag="AG2")
        elif ph == "segB":
            S.add("pool", lambda e: e.memset(sb["xg"][:, :], 0.0),
                  reads=[], writes=["xg0"], tag="zb0")
            if pending is not None:
                tileA_tail(pending, f"tA{pending}")
                pending = None
            sring = i % 3
            S.add("pool", lambda e: e.indirect_dma_start(
                out=sb["xg"][:, :], out_offset=None, in_=ag2_out.ap(),
                in_offset=IOff(ap=sb["ib0"][:, 0:1], axis=0)),
                reads=["ib0", "agdoneC2", "xg0"], writes=["xg0"], kind="dma",
                ch="gb0", tag="gb0")
            S.add("pool", lambda e: e.indirect_dma_start(
                out=sb["xb3"][:, :], out_offset=None, in_=ag2_out.ap(),
                in_offset=IOff(ap=sb["ib3"][:, 0:1], axis=0)),
                reads=["ib3", "agdoneC2", "xb3"], writes=["xb3"], kind="dma",
                ch="gb3", tag="gb3")
            bx = [(sb["xg"][:, :], "xg0"),
                  (sb["x1own"][:, 0:1024], "x1own0"),
                  (sb["x1own"][:, 1024:2048], "x1own1"),
                  (sb["xb3"][:, :], "xb3")]
            for j in range(4):
                kt_ap = sb["ktB"][:, j * 128:(j + 1) * 128]
                va_ap = sb["vaB"][:, j * 130:j * 130 + 130]
                emit_kv(bx[j][0], bx[j][1], 2 + j % 2, kt_ap, f"ktB{j}", va_ap, f"vaB{j}",
                        32 + j, j == 1, sring, f"kvB{j}")
            for bt in range(2):
                if bt == 1:
                    emit_kv(bx[2][0], bx[2][1], 2, None, None, None, None,
                            34, True, sring, "kvB2q", with_kv=False)
                kts = [(sb["ktB"][:, (bt + c) * 128:(bt + c + 1) * 128], f"ktB{bt + c}")
                       for c in range(3)]
                vas = [(sb["vaB"][:, (bt + c) * 130:(bt + c) * 130 + 130], f"vaB{bt + c}")
                       for c in range(3)]
                emit_tile(sring, kts, vas, 30 + bt,
                          sb["x1own"][:, bt * 1024:(bt + 1) * 1024], f"x1own{bt}",
                          "x2", bt, f"tB{bt}")
            S.add("pool", lambda e: e.collective_compute(
                "AllGather", ALU.bypass, replica_groups=[list(range(NC))],
                ins=[ag3_in.ap().opt()], outs=[ag3_out.ap().opt()]),
                reads=["ag3in0", "ag3in1"], writes=["agdoneC3"],
                kind="coll", tag="AG3")
            xlt_dirty[0] = True
        elif ph == "comb3":
            S.add("pool", lambda e: e.engine_nop(), reads=["agdoneC4"],
                  writes=["agdonecomb3"], tag="agw4")
            emit_combine(ag4_out.ap(), sb["x2own"], "x2own{}", "out", 1, "comb3")

    if pending is not None:
        tileA_tail(pending, f"tA{pending}")
        pending = None

    S.add("pool", lambda e: e.indirect_dma_start(
        out=sb["xg"][:, :], out_offset=None, in_=ag2_out.ap(),
        in_offset=IOff(ap=sb["ia0"][:, 0:1], axis=0)),
        reads=["ia0", "agdoneC2", "xg0"], writes=["xg0"], kind="dma", ch="ga0", tag="ga0")
    emit_kv(sb["xg"][:, :], "xg0", 2, sb["ktA"][:, 0:128], "ktA0",
            sb["vaA"][:, 0:130], "vaA0", 0, False, 0, "kvm1")
    x_load(0, 0, "x0r")
    emit_kv(sb["xr"][:, 0:1024], "xr0", 3, sb["ktA"][:, 128:256], "ktA1",
            sb["vaA"][:, 130:260], "vaA1", 1, True, 0, "kv0r")
    x_load(1, 1, "x1r")
    emit_kv(sb["xr"][:, 1024:2048], "xr1", 2, sb["ktA"][:, 256:384], "ktA2",
            sb["vaA"][:, 260:390], "vaA2", 2, False, 0, "kv1r")
    tileA(0, "tA0")

    S.add("sp", None, reads=[], writes=[], tag="fence")

    with es:
        with nc.Block() as block:
            eng_sem, ch_sem, coll_sems, emit_stream = _materialize(nc, S, es)

            @block.tensor
            def _(t):
                emit_stream(t, "pe")

            @block.scalar
            def _(s):
                emit_stream(s, "act")

            @block.vector
            def _(v):
                emit_stream(v, "dve")

            @block.gpsimd
            def _(g):
                emit_stream(g, "pool")

            @block.sync
            def _(sy):
                emit_stream(sy, "sp")

    return nc



def kernel(**inputs):
    in_maps, padt, base_ids = _prep(inputs)
    nc = build_program(padt)
    res = run_bass_kernel_spmd(nc, in_maps, core_ids=list(range(NC)))
    out = np.empty((TOK, D), dtype=np.float32)
    for c in range(NC):
        shard = res.results[c]["out_shard"]
        base = base_ids[c]
        for i in range(NT_A):
            a, b = _chunk_rows(base + i)
            out[a:b] = shard[i * 128:(i + 1) * 128]
        out[256 * c:256 * c + 256] = shard[3840:4096]
    return out.reshape(BATCH, SEQ, D)


if __name__ == "__main__":
    import sys
    if "--build" in sys.argv:
        nc = build_program(5)
        n = sum(1 for _ in nc.all_instructions())
        print(f"build OK, {n} instructions")


# version 32
# speedup vs baseline: 1.0125x; 1.0125x over previous
import numpy as np
from contextlib import ExitStack

import bass_rust
import concourse.bass as bass
import concourse.mybir as mybir
from concourse.bass_utils import run_bass_kernel_spmd

F32 = mybir.dt.float32
BF16 = mybir.dt.bfloat16
I32 = mybir.dt.int32
AF = mybir.ActivationFunctionType
ALU = mybir.AluOpType
AX = mybir.AxisListType
IOff = bass_rust.IndirectOffsetOnAxis

SAME_ENGINE_SYNC = True

NC = 8
D = 1024
HD = 64
NH = 16
NKV = 2
SEQ = 16384
BATCH = 2
TOK = BATCH * SEQ
REG = 2048
NT_A = 30
NCH_A = 31
RING_KV = 5
RING_X = 4
EXPB = -30.0
EPS = 1e-5
ADDEND = float(np.exp(np.float64(EXPB)))
AG2_ROWS = 512
AG3_ROWS = 256



def _chunk_rows(cid):
    if cid < 128:
        s = cid * 128
    else:
        s = SEQ + (cid - 128) * 128
    return s, s + 128


def _prep(inputs):
    x = np.ascontiguousarray(np.asarray(inputs["L0_attn_x"], dtype=np.float32).reshape(TOK, D))
    ridx = np.asarray(inputs["L0_ffn_router_indices"], dtype=np.int32)
    rw = np.asarray(inputs["L0_ffn_routing_weights"], dtype=np.float32)

    slots = [[] for _ in range(NC)]
    for t in range(REG):
        for k in range(2):
            slots[int(ridx[t, k])].append((t, k))
    counts = [len(s) for s in slots]
    padt = max(1, int(np.ceil(max(counts) / 128.0)))
    pad = padt * 128
    slot_row = {}
    for e in range(NC):
        for i, (t, k) in enumerate(slots[e]):
            slot_row[(t, k)] = e * pad + i

    base_ids = [16 + NT_A * c for c in range(NC)]

    per_core = []
    for c in range(NC):
        base = base_ids[c]
        blocks = []
        for s in range(NCH_A):
            cid = base + s
            if cid <= 255:
                a, b = _chunk_rows(cid)
                blocks.append(x[a:b])
            else:
                blocks.append(np.zeros((128, D), np.float32))
        x_shard = np.ascontiguousarray(np.concatenate(blocks, axis=0))

        x_own = np.ascontiguousarray(x[256 * c:256 * c + 256])

        ha, hb = _chunk_rows(base - 1)
        halo_l = x[ha:hb]
        special = np.zeros((128, D), np.float32)
        if c == 7:
            special = x[2048:2176]
        ag2_extra = np.ascontiguousarray(np.concatenate([halo_l, special], axis=0))

        p = np.arange(128, dtype=np.int32)
        if c == 0:
            a0_idx = (7 * AG2_ROWS + 128 + p).astype(np.int32)
        else:
            a0_idx = (c * AG2_ROWS + 256 + p).astype(np.int32)
        if c == 0:
            b0_idx = (0 * AG2_ROWS + 384 + p).astype(np.int32)
        else:
            tpos = 256 * c - 128 + p
            b0_idx = ((tpos // 256) * AG2_ROWS + tpos % 256).astype(np.int32)
        if c < 7:
            tpos = 256 * c + 256 + p
            b3_idx = ((tpos // 256) * AG2_ROWS + tpos % 256).astype(np.int32)
        else:
            b3_idx = (7 * AG2_ROWS + 384 + p).astype(np.int32)

        ffn1 = np.zeros((128, padt), np.int32)
        ffn3 = np.zeros((128, padt), np.int32)
        for i, (t, k) in enumerate(slots[c]):
            ffn1[i % 128, i // 128] = t
            ffn3[i % 128, i // 128] = (t // 256) * AG3_ROWS + t % 256
        comb = np.zeros((128, 4), np.int32)
        for tt in range(2):
            for k in range(2):
                for pp in range(128):
                    t = 256 * c + 128 * tt + pp
                    comb[pp, 2 * k + tt] = slot_row[(t, k)]

        meta_v = np.ones((128, 3 * 32), np.float32)
        for i in range(NT_A):
            own = base + i
            for cc in range(3):
                kid = own + cc - 1
                ok = (0 <= kid <= 255) and ((kid < 128) == (own < 128))
                if not ok:
                    meta_v[:, 3 * i + cc] = 0.0
        for bt in range(2):
            for cc in range(3):
                j = bt + cc
                pos0 = 256 * c + (j - 1) * 128
                ok = (0 <= pos0) and (pos0 + 128 <= 2176)
                if not ok:
                    meta_v[:, 3 * (30 + bt) + cc] = 0.0
        vev = np.ones((128, 36), np.float32)
        if base + 30 > 255:
            vev[:, 31] = 0.0
        if c == 0:
            vev[:, 32] = 0.0

        rw_own = rw[256 * c:256 * c + 256].reshape(2, 128, 8).transpose(1, 0, 2).reshape(128, 16)
        idx_own = ridx[256 * c:256 * c + 256].reshape(2, 128, 2).transpose(1, 0, 2).reshape(128, 4)

        per_core.append(dict(
            x_shard=x_shard, x_own=x_own, ag2_extra=ag2_extra,
            a0_idx=np.ascontiguousarray(a0_idx.reshape(128, 1)),
            b0_idx=np.ascontiguousarray(b0_idx.reshape(128, 1)),
            b3_idx=np.ascontiguousarray(b3_idx.reshape(128, 1)),
            ffn_idx1=np.ascontiguousarray(ffn1), ffn_idx3=np.ascontiguousarray(ffn3),
            comb_idx=np.ascontiguousarray(comb),
            meta_v=meta_v, vev=vev,
            rw_own=np.ascontiguousarray(rw_own),
            idx_own=np.ascontiguousarray(idx_own.astype(np.int32)),
        ))

    x_region = np.ascontiguousarray(x[0:REG])

    perm = np.concatenate([np.arange(0, 2048, 2), np.arange(1, 2048, 2)])
    gu = np.asarray(inputs["L0_ffn_gate_up_proj"], dtype=np.float32)
    gub = np.asarray(inputs["L0_ffn_gate_up_proj_bias"], dtype=np.float32)
    dn = np.asarray(inputs["L0_ffn_down_proj"], dtype=np.float32)
    dnb = np.asarray(inputs["L0_ffn_down_proj_bias"], dtype=np.float32)
    consts = np.zeros((128, 3), np.float32)
    consts[:, 0] = EXPB
    consts[:, 1] = EPS
    consts[:, 2] = 1.0
    band02 = np.zeros((128, 2048), np.float32)
    w_ = np.arange(128)[:, None]
    i_ = np.arange(128)[None, :]
    b0m = (w_ >= i_).astype(np.float32)
    b2m = (w_ < i_).astype(np.float32)
    for h in range(8):
        band02[:, 128 * h:128 * (h + 1)] = b0m
        band02[:, 1024 + 128 * h:1024 + 128 * (h + 1)] = b2m
    ident = np.eye(128, dtype=np.float32)
    ones_row = np.ones((1, 256), np.float32)

    qmap = np.zeros(D, np.int64)
    for h in range(16):
        g, hw = h // 8, h % 8
        qmap[hw * 128 + 64 * g + np.arange(64)] = h * 64 + np.arange(64)
    wq_p = np.ascontiguousarray(np.asarray(inputs["L0_attn_Wq"], np.float32)[:, qmap])
    bq_p = np.ascontiguousarray(np.asarray(inputs["L0_attn_bq"], np.float32)[qmap].reshape(1, D))
    womap = np.zeros(D, np.int64)
    for j in range(8):
        g, hw4 = j // 4, j % 4
        for half in range(2):
            h = 8 * g + 4 * half + hw4
            womap[j * 128 + half * 64 + np.arange(64)] = h * 64 + np.arange(64)
    wo_p = np.ascontiguousarray(np.asarray(inputs["L0_attn_Wo"], np.float32)[womap, :])

    in_maps = []
    for c in range(NC):
        pc = per_core[c]
        m = dict(
            x_shard=pc["x_shard"], x_own=pc["x_own"], x_region=x_region,
            ag2_extra=pc["ag2_extra"],
            a0_idx=pc["a0_idx"], b0_idx=pc["b0_idx"], b3_idx=pc["b3_idx"],
            ffn_idx1=pc["ffn_idx1"], ffn_idx3=pc["ffn_idx3"], comb_idx=pc["comb_idx"],
            meta_v=pc["meta_v"], vev=pc["vev"],
            rw_own=pc["rw_own"], idx_own=pc["idx_own"],
            consts=consts, band02=band02, ident=ident, ones_row=ones_row,
            wq=wq_p, wk=np.ascontiguousarray(np.asarray(inputs["L0_attn_Wk"], np.float32)),
            wv=np.ascontiguousarray(np.asarray(inputs["L0_attn_Wv"], np.float32)),
            wo=wo_p,
            bq=bq_p,
            bk=np.ascontiguousarray(np.asarray(inputs["L0_attn_bk"], np.float32).reshape(1, 128)),
            bv=np.ascontiguousarray(np.asarray(inputs["L0_attn_bv"], np.float32).reshape(1, 128)),
            bo=np.ascontiguousarray(np.asarray(inputs["L0_attn_bo"], np.float32).reshape(1, D)),
            gu_e=np.ascontiguousarray(gu[c][:, perm]),
            gub_e=np.ascontiguousarray(gub[c][perm].reshape(1, 2048)),
            d_e=np.ascontiguousarray(dn[c]),
            db_e=np.ascontiguousarray(dnb[c].reshape(1, D)),
        )
        in_maps.append(m)
    return in_maps, padt, base_ids



ENGINES = ("pe", "act", "dve", "pool", "sp")


class Sched:
    def __init__(self):
        self.ops = []
        self.lw = {}
        self.readers = {}
        self.channels = {}

    def add(self, engine, fn, reads=(), writes=(), kind="op", ch=None, tag=""):
        i = len(self.ops)
        raw = set()
        for t in reads:
            if t in self.lw:
                raw.add(self.lw[t])
        for t in writes:
            if t in self.lw:
                raw.add(self.lw[t])
            raw |= self.readers.get(t, set())
        raw.discard(i)
        deps = set()
        dma_waits = {}
        for d in raw:
            dop = self.ops[d]
            if dop["kind"] == "dma":
                dma_waits[dop["ch"]] = self.channels[dop["ch"]]
            elif (not SAME_ENGINE_SYNC and kind == "op" and dop["kind"] == "op"
                  and dop["engine"] == engine):
                continue
            else:
                deps.add(d)
        seq = None
        if kind == "dma":
            assert ch is not None
            self.channels[ch] = self.channels.get(ch, 0) + 1
            seq = self.channels[ch]
        self.ops.append(dict(engine=engine, fn=fn, deps=deps, dma_waits=dma_waits,
                             kind=kind, ch=ch, seq=seq, tag=tag))
        for t in reads:
            self.readers.setdefault(t, set()).add(i)
        for t in writes:
            self.lw[t] = i
            self.readers[t] = set()
        return i


def _materialize(nc, sched, es):
    ops = sched.ops
    targets = set()
    for op in ops:
        targets |= op["deps"]
    eng_count = {e: 0 for e in ENGINES}
    milestone = {}
    for i, op in enumerate(ops):
        if op["kind"] == "op" and i in targets:
            eng_count[op["engine"]] += 1
            milestone[i] = eng_count[op["engine"]]
    eng_sem = {e: es.enter_context(nc.semaphore(f"se_{e}")) for e in ENGINES}
    ch_sem = {c: es.enter_context(nc.semaphore(f"sd_{c}")) for c in sched.channels}
    n_coll = sum(1 for op in ops if op["kind"] == "coll")
    coll_sems = [es.enter_context(nc.semaphore(f"scc_{i}")) for i in range(n_coll)]
    coll_idx = {}
    k = 0
    for i, op in enumerate(ops):
        if op["kind"] == "coll":
            coll_idx[i] = k
            k += 1

    def wait_spec(i):
        op = ops[i]
        if op["kind"] == "op":
            return (eng_sem[op["engine"]], milestone.get(i))
        if op["kind"] == "coll":
            return (coll_sems[coll_idx[i]], 1)
        return None

    streams = {e: [] for e in ENGINES}
    for i, op in enumerate(ops):
        streams[op["engine"]].append(i)

    waited = {e: {} for e in ENGINES}

    def emit_stream(eng_obj, eng_name):
        wmap = waited[eng_name]
        for i in streams[eng_name]:
            op = ops[i]
            specs = []
            for d in sorted(op["deps"]):
                ws = wait_spec(d)
                if ws is not None and ws[1] is not None:
                    specs.append(ws)
            for ch, cnt in sorted(op["dma_waits"].items()):
                specs.append((ch_sem[ch], 16 * cnt))
            for sem, val in specs:
                key = id(sem)
                if wmap.get(key, -1) >= val:
                    continue
                eng_obj.wait_ge(sem, val)
                wmap[key] = val
            if op["fn"] is None:
                continue
            instr = op["fn"](eng_obj)
            if op["kind"] == "dma":
                instr.then_inc(ch_sem[op["ch"]], 16)
            elif op["kind"] == "coll":
                instr.then_inc(coll_sems[coll_idx[i]])
            elif i in milestone:
                instr.then_inc(eng_sem[op["engine"]], 1)
        if eng_name == "sp":
            for c, n in sorted(sched.channels.items()):
                sem = ch_sem[c]
                if wmap.get(id(sem), -1) < 16 * n:
                    eng_obj.wait_ge(sem, 16 * n)
            for s in coll_sems:
                eng_obj.wait_ge(s, 1)
            for e in ENGINES:
                if eng_count[e]:
                    eng_obj.wait_ge(eng_sem[e], eng_count[e])

    return eng_sem, ch_sem, coll_sems, emit_stream



def build_program(padt, debug=False):
    nc = bass.Bass("TRN2", target_bir_lowering=False, debug=debug)
    pad = padt * 128

    dri = {}
    for name, shape, dt in [
        ("x_shard", [NCH_A * 128, D], F32), ("x_own", [256, D], F32),
        ("x_region", [REG, D], F32), ("ag2_extra", [256, D], F32),
        ("a0_idx", [128, 1], I32), ("b0_idx", [128, 1], I32), ("b3_idx", [128, 1], I32),
        ("ffn_idx1", [128, padt], I32), ("ffn_idx3", [128, padt], I32),
        ("comb_idx", [128, 4], I32),
        ("meta_v", [128, 96], F32), ("vev", [128, 36], F32),
        ("rw_own", [128, 16], F32), ("idx_own", [128, 4], I32),
        ("consts", [128, 3], F32), ("band02", [128, 2048], F32),
        ("ident", [128, 128], F32), ("ones_row", [1, 256], F32),
        ("wq", [D, D], F32), ("wk", [D, 128], F32), ("wv", [D, 128], F32), ("wo", [D, D], F32),
        ("bq", [1, D], F32), ("bk", [1, 128], F32), ("bv", [1, 128], F32), ("bo", [1, D], F32),
        ("gu_e", [D, 2048], F32), ("gub_e", [1, 2048], F32),
        ("d_e", [D, D], F32), ("db_e", [1, D], F32),
    ]:
        dri[name] = nc.dram_tensor(name, shape, dt, kind="ExternalInput").ap()

    out_shard = nc.dram_tensor("out_shard", [4096, D], F32, kind="ExternalOutput").ap()

    ag1_in = nc.dram_tensor("ag1_in", [pad, D], BF16)
    ag1_out = nc.dram_tensor("ag1_out", [NC * pad, D], BF16, addr_space="Shared")
    ag2_in = nc.dram_tensor("ag2_in", [AG2_ROWS, D], F32)
    ag2_out = nc.dram_tensor("ag2_out", [NC * AG2_ROWS, D], F32, addr_space="Shared")
    ag3_in = nc.dram_tensor("ag3_in", [AG3_ROWS, D], F32)
    ag3_out = nc.dram_tensor("ag3_out", [NC * AG3_ROWS, D], F32, addr_space="Shared")
    ag4_in = nc.dram_tensor("ag4_in", [pad, D], BF16)
    ag4_out = nc.dram_tensor("ag4_out", [NC * pad, D], BF16, addr_space="Shared")

    es = ExitStack()
    E = es.enter_context

    sb = {}

    def SB(name, shape, dt):
        sb[name] = E(nc.sbuf_tensor("sb_" + name, shape, dt))
        return sb[name]

    SB("wq", [128, 8 * 1024], BF16)
    SB("wk", [128, 8 * 128], BF16)
    SB("wv", [128, 8 * 128], BF16)
    SB("wo", [128, 8 * 1024], BF16)
    SB("gu", [128, 8 * 2048], BF16)
    SB("dw", [128, 8 * 1024], BF16)
    SB("bq", [1, D], BF16)
    SB("bk", [1, 128], BF16)
    SB("bv", [1, 128], BF16)
    SB("bo", [1, D], BF16)
    SB("gub", [1, 2048], BF16)
    SB("db", [1, D], BF16)
    SB("ones", [1, 256], BF16)
    SB("ident", [128, 128], BF16)
    SB("band02", [128, 2048], BF16)
    SB("consts", [128, 3], F32)
    SB("meta_v", [128, 96], F32)
    SB("vev", [128, 36], F32)
    SB("ktA", [128, RING_KV * 128], BF16)
    SB("vaA", [128, RING_KV * 130], BF16)
    SB("ktB", [128, 4 * 128], BF16)
    SB("vaB", [128, 4 * 130], BF16)
    SB("xr", [128, RING_X * 1024], F32)
    SB("xb3", [128, 1024], F32)
    SB("xln", [128, 2 * 1024], BF16)
    SB("xlt", [128, 2 * 1024], BF16)
    SB("qt", [128, 3 * 1024], BF16)
    SB("pt", [128, 6 * 1024], BF16)
    SB("attnT", [128, 1024], BF16)
    SB("rsb", [64, 1024], BF16)
    SB("rcp", [1, 1024], BF16)
    SB("lnr", [1, 1024], F32)
    SB("bnt", [128, 48], F32)
    SB("xo", [128, 1024], F32)
    SB("x1own", [128, 2 * 1024], F32)
    SB("x2own", [128, 2 * 1024], F32)
    SB("ybuf", [128, 1024], BF16)
    SB("wcell", [128, 16], F32)
    SB("lnc", [128, 32], F32)
    SB("xg", [128, 1024], F32)
    SB("swz", [128, 3 * 1024], F32)
    SB("inter", [128, 1024], BF16)
    SB("itT", [128, 1024], BF16)
    SB("iffn1", [128, padt], I32)
    SB("iffn3", [128, padt], I32)
    SB("icomb", [128, 4], I32)
    SB("ia0", [128, 1], I32)
    SB("ib0", [128, 1], I32)
    SB("ib3", [128, 1], I32)
    SB("rw", [128, 16], F32)
    SB("idxo", [128, 4], I32)
    SB("idxf", [128, 4], F32)

    ps = {}
    ps["pAB"] = E(nc.psum_tensor("pAB", [128, 1024], F32))
    ps["pT"] = E(nc.psum_tensor("pT", [128, 1024], F32))
    ps["s0"] = E(nc.psum_tensor("s0", [128, 1024], F32))
    ps["s1"] = E(nc.psum_tensor("s1", [128, 1024], F32))

    S = Sched()

    def dma(engine, ch, out_ap, in_ap, reads, writes, tag=""):
        S.add(engine, lambda e, o=out_ap, i=in_ap: e.dma_start(out=o, in_=i),
              reads=reads, writes=writes, kind="dma", ch=ch, tag=tag)

    for name in ("vaA", "vaB"):
        S.add("pool", lambda e, t=sb[name]: e.memset(t[:, :], 1.0),
              writes=[name], tag="memset")
    S.add("pool", lambda e: e.memset(sb["xb3"][:, :], 0.0), writes=["xb3"])

    small = [
        ("consts", "consts"), ("meta_v", "meta_v"), ("vev", "vev"),
        ("band02", "band02"), ("ident", "ident"), ("ones", "ones_row"),
        ("bq", "bq"), ("bk", "bk"), ("bv", "bv"), ("bo", "bo"),
        ("gub", "gub_e"), ("db", "db_e"),
        ("iffn1", "ffn_idx1"), ("iffn3", "ffn_idx3"), ("icomb", "comb_idx"),
        ("ia0", "a0_idx"), ("ib0", "b0_idx"), ("ib3", "b3_idx"),
        ("rw", "rw_own"), ("idxo", "idx_own"),
    ]
    for sname, dname in small:
        dma("pool", "ci", sb[sname][:, :], dri[dname], reads=[], writes=[sname], tag=f"ld_{sname}")

    def wload(ch, sname, dname, n):
        out_ap = sb[sname][:, :].rearrange("p (dc n) -> p dc n", dc=8)
        in_ap = dri[dname].rearrange("(dc p) n -> p dc n", p=128)
        dma("pool", ch, out_ap, in_ap, reads=[], writes=[sname], tag=f"ld_{sname}")
    wload("cwf", "gu", "gu_e", 2048)
    wload("cwf", "dw", "d_e", 1024)
    wload("cwa", "wq", "wq", 1024)
    wload("cwa", "wk", "wk", 128)
    wload("cwa", "wv", "wv", 128)
    wload("cwa", "wo", "wo", 1024)

    def emit_ln_stats(src_ap, src_tok, ring, tag):
        cells = sb["lnc"]
        c0 = 8 * ring
        ct = f"lnc{ring}"
        bt = sb["bnt"][:, ring * 12:ring * 12 + 12]
        btok = f"bnt{ring}"
        S.add("dve", lambda e: e.bn_stats(bt[:, 0:6], src_ap[:, 0:512]),
              reads=[src_tok], writes=[btok, ct + "u"], tag=f"{tag}_bs0")
        S.add("dve", lambda e: e.bn_stats(bt[:, 6:12], src_ap[:, 512:1024]),
              reads=[src_tok, btok], writes=[btok], tag=f"{tag}_bs1")
        S.add("dve", lambda e: e.bn_aggr(cells[:, c0 + 2:c0 + 4], bt[:, 0:12]),
              reads=[btok], writes=[ct + "m", ct + "v"], tag=f"{tag}_agg")
        S.add("act", lambda e: e.activation(cells[:, c0 + 5:c0 + 6], cells[:, c0 + 3:c0 + 4],
                                            AF.Ln, bias=sb["consts"][:, 1:2], scale=1.0),
              reads=[ct + "v", "consts"], writes=[ct + "l"], tag=f"{tag}_ln")
        S.add("act", lambda e: e.activation(cells[:, c0 + 6:c0 + 7], cells[:, c0 + 5:c0 + 6],
                                            AF.Exp, scale=-0.5),
              reads=[ct + "l"], writes=[ct + "r"], tag=f"{tag}_rstd")
        S.add("dve", lambda e: e.scalar_tensor_tensor(cells[:, c0 + 7:c0 + 8], cells[:, c0 + 2:c0 + 3],
                                                      -1.0, cells[:, c0 + 6:c0 + 7],
                                                      op0=ALU.mult, op1=ALU.mult),
              reads=[ct + "m", ct + "r"], writes=[ct + "n"], tag=f"{tag}_nb")
    def emit_ln_apply(src_ap, src_tok, out_ap, out_tok, ring, tag):
        cells = sb["lnc"]
        c0 = 8 * ring
        ct = f"lnc{ring}"
        S.add("dve", lambda e: e.tensor_scalar(
            out_ap, src_ap, cells[:, c0 + 6:c0 + 7], cells[:, c0 + 7:c0 + 8],
            op0=ALU.mult, op1=ALU.add),
            reads=[src_tok, ct + "r", ct + "n"], writes=[out_tok, ct + "u"], tag=f"{tag}_app")

    def emit_ln(src_ap, src_tok, out_ap, out_tok, ring, tag):
        emit_ln_stats(src_ap, src_tok, ring, tag)
        emit_ln_apply(src_ap, src_tok, out_ap, out_tok, ring, tag)

    def emit_tr(src, src_tok, dst, dst_tok, tag):
        def f(e):
            ins = None
            for dc in range(8):
                bank_half = dc * 128
                ins = e.matmul(ps["pT"][:, bank_half:bank_half + 128],
                               src[:, dc * 128:(dc + 1) * 128], sb["ident"][:, :],
                               start=True, stop=True)
            return ins
        S.add("pe", f, reads=[src_tok, "ident"], writes=["pT"], tag=f"{tag}_tr")
        S.add("dve", lambda e: e.tensor_copy(dst, ps["pT"][:, :]),
              reads=["pT"], writes=[dst_tok], tag=f"{tag}_trev")

    def emit_kv_front(xsrc_ap, xsrc_tok, ln_ring, tag, prestats=False):
        bln = ln_ring % 2
        xln_ap = sb["xln"][:, bln * 1024:(bln + 1) * 1024]
        xlt_ap = sb["xlt"][:, bln * 1024:(bln + 1) * 1024]
        if not prestats:
            emit_ln_stats(xsrc_ap, xsrc_tok, ln_ring, tag + "_ln")
        emit_ln_apply(xsrc_ap, xsrc_tok, xln_ap, f"xln{bln}", ln_ring, tag + "_ln")
        emit_tr(xln_ap, f"xln{bln}", xlt_ap, f"xlt{bln}", tag)

    def emit_kv_back(ln_ring, kt_ap, kt_tok, va_ap, va_tok, vev_col,
                     with_q, qt_ring, tag, with_kv=True):
        bln = ln_ring % 2
        xlt_ap = sb["xlt"][:, bln * 1024:(bln + 1) * 1024]
        xlt_tok = f"xlt{bln}"

        if with_q == "single":
            def fq(e):
                ins = None
                for half in range(2):
                    for qcl in range(4):
                        qc = half * 4 + qcl
                        col = half * 512 + qcl * 128
                        ins = e.matmul(ps["pAB"][:, col:col + 128],
                                       sb["bq"][0:1, qc * 128:(qc + 1) * 128],
                                       sb["ones"][0:1, 0:128], start=True, stop=False)
                        for dc in range(8):
                            ins = e.matmul(ps["pAB"][:, col:col + 128],
                                           sb["wq"][:, dc * 1024 + qc * 128:dc * 1024 + (qc + 1) * 128],
                                           xlt_ap[:, dc * 128:(dc + 1) * 128],
                                           start=False, stop=(dc == 7))
                return ins
            S.add("pe", fq, reads=[xlt_tok, "wq", "bq", "ones"], writes=["pA", "pB"],
                  tag=f"{tag}_qt")
            qdst = sb["qt"][:, qt_ring * 1024: (qt_ring + 1) * 1024]
            S.add("dve", lambda e, d=qdst: e.tensor_copy(d, ps["pAB"][:, :]),
                  reads=["pA", "pB"], writes=[f"qt{qt_ring}_0", f"qt{qt_ring}_1"],
                  tag=f"{tag}_qtev")
        elif with_q == "pair":
            xlt2 = sb["xlt"][:, :].rearrange("p (b n) -> p b n", b=2)
            qr0, qr1 = (qt_ring + 2) % 3, qt_ring
            for half in range(2):
                def fqp(e, half=half):
                    ins = None
                    for qcl in range(4):
                        qc = half * 4 + qcl
                        col = qcl * 256
                        ins = e.matmul(ps["pAB"][:, col:col + 256],
                                       sb["bq"][0:1, qc * 128:(qc + 1) * 128],
                                       sb["ones"][0:1, 0:256], start=True, stop=False)
                        for dc in range(8):
                            ins = e.matmul(ps["pAB"][:, col:col + 256],
                                           sb["wq"][:, dc * 1024 + qc * 128:dc * 1024 + (qc + 1) * 128],
                                           xlt2[:, :, dc * 128:(dc + 1) * 128],
                                           start=False, stop=(dc == 7))
                    return ins
                S.add("pe", fqp, reads=["xlt0", "xlt1", "wq", "bq", "ones"],
                      writes=["pA", "pB"], tag=f"{tag}_qtp{half}")
                psrc = ps["pAB"][:, :].rearrange("p (q b n) -> p q b n", q=4, b=2)
                for bi, qr in ((0, qr0), (1, qr1)):
                    qdst = sb["qt"][:, qr * 1024 + half * 512: qr * 1024 + (half + 1) * 512]
                    S.add("dve", lambda e, d=qdst, s=psrc, bi=bi: e.tensor_copy(
                        d.rearrange("p (q n) -> p q n", q=4), s[:, :, bi, :]),
                        reads=["pA", "pB"], writes=[f"qt{qr}_{half}"],
                        tag=f"{tag}_qtpev{half}{bi}")

        if not with_kv:
            return

        def fk(e):
            ins = e.matmul(ps["pAB"][:, 0:128], sb["bk"][0:1, :], sb["ones"][0:1, 0:128],
                           start=True, stop=False)
            for dc in range(8):
                ins = e.matmul(ps["pAB"][:, 0:128],
                               sb["wk"][:, dc * 128:(dc + 1) * 128],
                               xlt_ap[:, dc * 128:(dc + 1) * 128],
                               start=False, stop=(dc == 7))
            return ins
        S.add("pe", fk, reads=[xlt_tok, "wk", "bk", "ones"], writes=["pA"], tag=f"{tag}_kt")
        S.add("dve", lambda e: e.tensor_copy(kt_ap, ps["pAB"][:, 0:128]),
              reads=["pA"], writes=[kt_tok], tag=f"{tag}_ktev")

        def fv(e):
            ins = e.matmul(ps["pAB"][:, 512:640], sb["ones"][0:1, 0:128], sb["bv"][0:1, :],
                           start=True, stop=False)
            for dc in range(8):
                ins = e.matmul(ps["pAB"][:, 512:640],
                               xlt_ap[:, dc * 128:(dc + 1) * 128],
                               sb["wv"][:, dc * 128:(dc + 1) * 128],
                               start=False, stop=(dc == 7))
            return ins
        S.add("pe", fv, reads=[xlt_tok, "wv", "bv", "ones"], writes=["pB"], tag=f"{tag}_v")
        for g in range(2):
            S.add("dve", lambda e, g=g: e.tensor_scalar(
                va_ap[:, g * 65:g * 65 + 64], ps["pAB"][:, 512 + g * 64:512 + (g + 1) * 64],
                sb["vev"][:, vev_col:vev_col + 1], None, op0=ALU.mult),
                reads=["pB", "vev"], writes=[va_tok], tag=f"{tag}_vev{g}")

    def emit_kv(xsrc_ap, xsrc_tok, ln_ring, kt_ap, kt_tok, va_ap, va_tok, vev_col,
                with_q, qt_ring, tag, with_kv=True, prestats=False):
        emit_kv_front(xsrc_ap, xsrc_tok, ln_ring, tag, prestats=prestats)
        emit_kv_back(ln_ring, kt_ap, kt_tok, va_ap, va_tok, vev_col,
                     "single" if with_q else False, qt_ring, tag, with_kv=with_kv)

    BLKMAP = {0: 0, 2: 1, 1: 2}

    def emit_sc_one(g, c, qt_ring, kt_list, meta_tile, tag):
        stn = {0: "s0", 2: "s1", 1: "s0"}
        for c in (c,):
                blk = 3 * g + BLKMAP[c]
                pn = stn[c]
                kt_ap, kt_tok = kt_list[c]

                def fs(e, g=g, c=c, pn=pn, kt_ap=kt_ap):
                    ins = None
                    for half in range(2):
                        rhs = sb["qt"][64 * g:64 * g + 64,
                                       qt_ring * 1024 + half * 512:qt_ring * 1024 + (half + 1) * 512]
                        ins = e.matmul(ps[pn][:, half * 512:(half + 1) * 512],
                                       kt_ap[64 * g:64 * g + 64, :], rhs,
                                       start=True, stop=True)
                    return ins
                S.add("pe", fs, reads=[kt_tok, f"qt{qt_ring}_0", f"qt{qt_ring}_1"],
                      writes=[pn], tag=f"{tag}_sc{g}{c}")
                ptb = sb["pt"][:, blk * 1024:(blk + 1) * 1024]
                mc = meta_tile * 3 + c
                S.add("act", lambda e, pn=pn, ptb=ptb, mc=mc: e.activation(
                    ptb, ps[pn][:, :], AF.Exp, bias=sb["consts"][:, 0:1],
                    scale=sb["meta_v"][:, mc:mc + 1]),
                    reads=[pn, "consts", "meta_v"], writes=[f"pt{blk}"],
                    tag=f"{tag}_exp{g}{c}")
                if c == 2:
                    ptm = sb["pt"][:, (3 * g) * 1024:(3 * g + 2) * 1024]
                    S.add("dve", lambda e, ptm=ptm: e.tensor_tensor(
                        ptm, ptm, sb["band02"][:, :], op=ALU.mult),
                        reads=[f"pt{3 * g}", f"pt{3 * g + 1}", "band02"],
                        writes=[f"pt{3 * g}", f"pt{3 * g + 1}"], tag=f"{tag}_mb{g}")

    def emit_sc_group(g, qt_ring, kt_list, meta_tile, tag):
        for c in (0, 2, 1):
            emit_sc_one(g, c, qt_ring, kt_list, meta_tile, tag)

    def emit_tile_sc(qt_ring, kt_list, meta_tile, tag):
        emit_sc_group(0, qt_ring, kt_list, meta_tile, tag)
        emit_sc_group(1, qt_ring, kt_list, meta_tile, tag)

    def emit_pv_group(g, va_list, tag):
        for par, pn in ((0, "pA"), (1, "pB")):
            def fpv(e, g=g, par=par):
                ins = None
                for c in range(3):
                    va_ap, _ = va_list[c]
                    ins = e.matmul(ps["pAB"][0:65, par * 512:(par + 1) * 512],
                                   va_ap[:, g * 65:(g + 1) * 65],
                                   sb["pt"][:, (3 * g + BLKMAP[c]) * 1024 + par * 512:
                                            (3 * g + BLKMAP[c]) * 1024 + (par + 1) * 512],
                                   start=(c == 0), stop=(c == 2))
                return ins
            S.add("pe", fpv,
                  reads=[va_list[0][1], va_list[1][1], va_list[2][1],
                         f"pt{3 * g + 0}", f"pt{3 * g + 1}", f"pt{3 * g + 2}"],
                  writes=[pn], tag=f"{tag}_pv{g}{par}")
        S.add("act", lambda e: e.activation(
            sb["lnr"][0:1, :], ps["pAB"][64:65, :], AF.Ln),
            reads=["pA", "pB"], writes=["lnr0", "lnr1"], tag=f"{tag}_ln{g}")

        def fex(e):
            with nc.allow_low_precision(reason="bf16 softmax-normalizer operand"):
                return e.activation(sb["rcp"][0:1, :], sb["lnr"][0:1, :],
                                    AF.Exp, scale=-1.0)
        S.add("act", fex, reads=["lnr0", "lnr1"], writes=["rcp0", "rcp1"],
              tag=f"{tag}_rc{g}")

    def emit_rb(g, tag):
        def frb(e):
            ins = None
            for par in range(2):
                ins = e.matmul(ps["pT"][0:64, par * 512:(par + 1) * 512],
                               sb["ones"][0:1, 0:64],
                               sb["rcp"][0:1, par * 512:(par + 1) * 512],
                               start=True, stop=True)
            return ins
        S.add("pe", frb, reads=["rcp0", "rcp1", "ones"], writes=["pT"], tag=f"{tag}_rb{g}")
        S.add("act", lambda e: e.activation(sb["rsb"][:, :], ps["pT"][0:64, :], AF.Identity),
              reads=["pT"], writes=["rsb"], tag=f"{tag}_rev{g}")

    def emit_tnorm(g, tag):
        for par, pn in ((0, "pA"), (1, "pB")):
            dst = sb["attnT"][par * 64:par * 64 + 64, g * 512:(g + 1) * 512]
            S.add("dve", lambda e, par=par, dst=dst: e.tensor_tensor(
                dst, ps["pAB"][0:64, par * 512:(par + 1) * 512],
                sb["rsb"][0:64, par * 512:(par + 1) * 512],
                op=ALU.mult),
                reads=[pn, "rsb"], writes=["attnT"], tag=f"{tag}_tn{g}{par}")

    def emit_norm_group(g, tag):
        emit_rb(g, tag)
        emit_tnorm(g, tag)

    def emit_wo_epi(x_ap, x_tok, dst_kind, dst_info, tag):
        def fwo(e):
            ins = None
            for half in range(2):
                ins = e.matmul(ps["pAB"][:, half * 512:(half + 1) * 512],
                               sb["ones"][0:1, 0:128],
                               sb["bo"][0:1, half * 512:(half + 1) * 512],
                               start=True, stop=False)
            for dc in range(8):
                for half in range(2):
                    ins = e.matmul(ps["pAB"][:, half * 512:(half + 1) * 512],
                                   sb["attnT"][:, dc * 128:(dc + 1) * 128],
                                   sb["wo"][:, dc * 1024 + half * 512:dc * 1024 + (half + 1) * 512],
                                   start=False, stop=(dc == 7))
            return ins
        S.add("pe", fwo, reads=["attnT", "wo", "bo", "ones"], writes=["pA", "pB"],
              tag=f"{tag}_wo")

        if dst_kind == "out":
            row0 = dst_info
            xo_ap = sb["xo"][:, :]
            S.add("dve", lambda e, xo_ap=xo_ap, x_ap=x_ap: e.tensor_tensor(
                xo_ap, ps["pAB"][:, :], x_ap, op=ALU.add),
                reads=["pA", "pB", x_tok], writes=["xo0"], tag=f"{tag}_epi")
            dma("sp", "ost", out_shard[row0:row0 + 128, :], xo_ap,
                reads=["xo0"], writes=[f"ost{row0}"], tag=f"{tag}_st")
        else:
            tt = dst_info
            x2_ap = sb["x2own"][:, tt * 1024:(tt + 1) * 1024]
            S.add("dve", lambda e, x2_ap=x2_ap, x_ap=x_ap: e.tensor_tensor(
                x2_ap, ps["pAB"][:, :], x_ap, op=ALU.add),
                reads=["pA", "pB", x_tok], writes=[f"x2own{tt}"], tag=f"{tag}_epi")
            dma("sp", "a3c", ag3_in.ap()[tt * 128:(tt + 1) * 128, :], x2_ap,
                reads=[f"x2own{tt}"], writes=[f"ag3in{tt}"], tag=f"{tag}_st")

    def emit_tile_pv(va_list, x_ap, x_tok, dst_kind, dst_info, tag):
        emit_pv_group(0, va_list, tag)
        emit_norm_group(0, tag)
        emit_pv_group(1, va_list, tag)
        emit_norm_group(1, tag)
        emit_wo_epi(x_ap, x_tok, dst_kind, dst_info, tag)

    def emit_tile(qt_ring, kt_list, va_list, meta_tile, x_ap, x_tok, dst_kind, dst_info, tag):
        emit_tile_sc(qt_ring, kt_list, meta_tile, tag)
        emit_tile_pv(va_list, x_ap, x_tok, dst_kind, dst_info, tag)

    def emit_ffn_head(call, t, src_dram, idx_sb_name, tag):
        ring = t % 2
        xg_ap = sb["xg"][:, :]
        xg_tok = "xg0"
        S.add("pool", lambda e: e.indirect_dma_start(
            out=xg_ap, out_offset=None, in_=src_dram,
            in_offset=IOff(ap=sb[idx_sb_name][:, t:t + 1], axis=0)),
            reads=[idx_sb_name, f"agsrc{call}"], writes=[xg_tok], kind="dma",
            ch="xg0", tag=f"{tag}_g")
        fln_ap = sb["xln"][:, ring * 1024:(ring + 1) * 1024]
        flt_ap = sb["xlt"][:, ring * 1024:(ring + 1) * 1024]
        emit_ln(xg_ap, xg_tok, fln_ap, f"xln{ring}", 2 + ring, tag + "_ln")
        emit_tr(fln_ap, f"xln{ring}", flt_ap, f"xlt{ring}", tag)
        flt_tok = f"xlt{ring}"

        for half, pn in ((0, "s0"), (1, "s1")):
            def fg(e, half=half, pn=pn, flt_ap=flt_ap):
                ins = None
                for q in range(2):
                    col = half * 1024 + q * 512
                    ins = e.matmul(ps[pn][:, q * 512:(q + 1) * 512],
                                   sb["ones"][0:1, 0:128],
                                   sb["gub"][0:1, col:col + 512], start=True, stop=False)
                    for dc in range(8):
                        ins = e.matmul(ps[pn][:, q * 512:(q + 1) * 512],
                                       flt_ap[:, dc * 128:(dc + 1) * 128],
                                       sb["gu"][:, dc * 2048 + col:dc * 2048 + col + 512],
                                       start=False, stop=(dc == 7))
                return ins
            S.add("pe", fg, reads=[flt_tok, "gu", "gub", "ones"], writes=[pn],
                  tag=f"{tag}_gu{half}")
    def emit_ffn_head2(call, t, tag):
        A = sb["swz"][:, 0:1024]
        C = sb["swz"][:, 2048:3072]
        S.add("dve", lambda e, A=A: e.tensor_scalar(A, ps["s0"][:, :], 7.0, None, op0=ALU.min),
              reads=["s0"], writes=["swzA"], tag=f"{tag}_gmin")
        S.add("dve", lambda e, C=C: e.tensor_scalar(C, ps["s1"][:, :], 7.0, -7.0,
                                                    op0=ALU.min, op1=ALU.max),
              reads=["s1"], writes=["swzC"], tag=f"{tag}_uclip")

    def emit_ffn_tail1(call, t, tag):
        A = sb["swz"][:, 0:1024]
        B = sb["swz"][:, 1024:2048]
        C = sb["swz"][:, 2048:3072]
        S.add("act", lambda e, A=A, B=B: e.activation(B, A, AF.Exp, scale=-1.702),
              reads=["swzA"], writes=["swzB"], tag=f"{tag}_exp")
        S.add("act", lambda e, B=B: e.activation(B, B, AF.Ln, bias=sb["consts"][:, 2:3]),
              reads=["swzB", "consts"], writes=["swzB"], tag=f"{tag}_ln1p")
        S.add("act", lambda e, B=B: e.activation(B, B, AF.Exp, scale=-1.0),
              reads=["swzB"], writes=["swzB"], tag=f"{tag}_sig")
        S.add("dve", lambda e, A=A, B=B: e.tensor_tensor(A, A, B, op=ALU.mult),
              reads=["swzA", "swzB"], writes=["swzA"], tag=f"{tag}_glu")
        S.add("dve", lambda e, A=A, C=C: e.scalar_tensor_tensor(
            sb["inter"][:, :], C, 1.0, A, op0=ALU.add, op1=ALU.mult),
            reads=["swzA", "swzC"], writes=["inter"], tag=f"{tag}_inter")

    def emit_ffn_tail2(call, t, contrib, tag):
        emit_tr(sb["inter"][:, :], "inter", sb["itT"][:, :], "itT", tag + "_it")

        def fd(e):
            ins = None
            for half in range(2):
                ins = e.matmul(ps["pAB"][:, half * 512:(half + 1) * 512],
                               sb["ones"][0:1, 0:128],
                               sb["db"][0:1, half * 512:(half + 1) * 512],
                               start=True, stop=False)
            for ic in range(8):
                for half in range(2):
                    ins = e.matmul(ps["pAB"][:, half * 512:(half + 1) * 512],
                                   sb["itT"][:, ic * 128:(ic + 1) * 128],
                                   sb["dw"][:, ic * 1024 + half * 512:ic * 1024 + (half + 1) * 512],
                                   start=False, stop=(ic == 7))
            return ins
        S.add("pe", fd, reads=["itT", "dw", "db", "ones"], writes=["pA", "pB"],
              tag=f"{tag}_dn")
        S.add("act", lambda e: e.activation(sb["inter"][:, :], ps["pAB"][:, :], AF.Identity),
              reads=["pA", "pB"], writes=["inter"], tag=f"{tag}_ycev")
        dma("sp", contrib[0], contrib[1][t * 128:(t + 1) * 128, :], sb["inter"][:, :],
            reads=["inter"], writes=[f"{contrib[2]}{t}"], tag=f"{tag}_yst")

    def emit_wcompute():
        S.add("dve", lambda e: e.tensor_copy(sb["idxf"][:, :], sb["idxo"][:, :]),
              reads=["idxo"], writes=["idxf"], tag="w_cast")
        for tt in range(2):
            for k in range(2):
                col = 2 * tt + k
                acc_a, acc_b = 4 + 2 * col, 5 + 2 * col
                for ei in range(8):
                    eq = 14
                    S.add("dve", lambda e, col=col, ei=ei, eq=eq: e.tensor_scalar(
                        sb["wcell"][:, eq:eq + 1], sb["idxf"][:, col:col + 1],
                        float(ei), None, op0=ALU.is_equal),
                        reads=["idxf"], writes=["weq"], tag=f"w_eq{col}_{ei}")
                    rcol = 8 * tt + ei
                    if ei == 0:
                        S.add("dve", lambda e, acc_a=acc_a, eq=eq, rcol=rcol: e.tensor_tensor(
                            sb["wcell"][:, acc_a:acc_a + 1], sb["wcell"][:, eq:eq + 1],
                            sb["rw"][:, rcol:rcol + 1], op=ALU.mult),
                            reads=["weq", "rw"], writes=[f"wacc{col}a"], tag=f"w_a{col}_{ei}")
                    else:
                        src_t, dst_t = ((f"wacc{col}a", f"wacc{col}b") if ei % 2 == 1
                                        else (f"wacc{col}b", f"wacc{col}a"))
                        src_c = acc_a if ei % 2 == 1 else acc_b
                        dst_c = acc_b if ei % 2 == 1 else acc_a
                        S.add("dve", lambda e, s=src_c, d=dst_c, eq=eq, rcol=rcol:
                              e.scalar_tensor_tensor(
                                  sb["wcell"][:, d:d + 1], sb["wcell"][:, eq:eq + 1],
                                  sb["rw"][:, rcol:rcol + 1], sb["wcell"][:, s:s + 1],
                                  op0=ALU.mult, op1=ALU.add),
                              reads=["weq", "rw", src_t], writes=[dst_t], tag=f"w_a{col}_{ei}")
                final_c = acc_b if 7 % 2 == 1 else acc_a
                final_t = f"wacc{col}b" if 7 % 2 == 1 else f"wacc{col}a"
                S.add("dve", lambda e, col=col, final_c=final_c: e.tensor_scalar(
                    sb["wcell"][:, col:col + 1], sb["wcell"][:, final_c:final_c + 1],
                    0.5, None, op0=ALU.mult),
                    reads=[final_t], writes=[f"w{col}"], tag=f"w_f{col}")

    def emit_combine(agout, xbase_sb, xbase_tok_fmt, dst_kind, ygroup, tag):
        for tt in range(2):
            A = sb["swz"][:, 0:1024]
            y0 = sb["ybuf"][:, :]
            y1 = sb["ybuf"][:, :]
            S.add("pool", lambda e, col=tt: e.indirect_dma_start(
                out=sb["ybuf"][:, :], out_offset=None,
                in_=agout, in_offset=IOff(ap=sb["icomb"][:, col:col + 1], axis=0)),
                reads=["icomb", f"agdone{tag}"], writes=["ybg0", "ybg1"], kind="dma",
                ch=f"y{ygroup}0", tag=f"{tag}_yg{tt}0")
            xb = xbase_sb[:, tt * 1024:(tt + 1) * 1024]
            S.add("dve", lambda e, A=A, y0=y0, xb=xb, tt=tt: e.scalar_tensor_tensor(
                A, y0, sb["wcell"][:, 2 * tt:2 * tt + 1], xb, op0=ALU.mult, op1=ALU.add),
                reads=["ybg0", f"w{2 * tt}", xbase_tok_fmt.format(tt)],
                writes=["swzA"], tag=f"{tag}_c1_{tt}")
            S.add("pool", lambda e, col=2 + tt: e.indirect_dma_start(
                out=sb["ybuf"][:, :], out_offset=None,
                in_=agout, in_offset=IOff(ap=sb["icomb"][:, col:col + 1], axis=0)),
                reads=["icomb", f"agdone{tag}"], writes=["ybg0", "ybg1"], kind="dma",
                ch=f"y{ygroup}0", tag=f"{tag}_yg{tt}1")
            if dst_kind == "x1":
                dst_ap = sb["x1own"][:, tt * 1024:(tt + 1) * 1024]
                dst_tok = f"x1own{tt}"
            else:
                dst_ap = sb["swz"][:, 2048:3072]
                dst_tok = "swzC"
            S.add("dve", lambda e, dst_ap=dst_ap, y1=y1, A=A, tt=tt: e.scalar_tensor_tensor(
                dst_ap, y1, sb["wcell"][:, 2 * tt + 1:2 * tt + 2], A,
                op0=ALU.mult, op1=ALU.add),
                reads=["ybg1", f"w{2 * tt + 1}", "swzA"], writes=[dst_tok],
                tag=f"{tag}_c2_{tt}")
            if dst_kind == "x1":
                dma("sp", "a2c", ag2_in.ap()[tt * 128:(tt + 1) * 128, :], dst_ap,
                    reads=[dst_tok], writes=[f"ag2in{tt}"], tag=f"{tag}_st{tt}")
            else:
                dma("sp", "ost", out_shard[3840 + tt * 128:3840 + (tt + 1) * 128, :], dst_ap,
                    reads=[dst_tok], writes=["outhbm"], tag=f"{tag}_st{tt}")

    S.add("pool", lambda e: e.engine_nop(), reads=[], writes=["agsrc1"], tag="src1")

    def ffn_h1(call, t):
        if call == 1:
            emit_ffn_head(1, t, dri["x_region"], "iffn1", f"f1t{t}")
        else:
            if t == 0:
                S.add("pool", lambda e: e.engine_nop(), reads=["agdoneC3"],
                      writes=["agsrc3"], tag="src3")
            emit_ffn_head(3, t, ag3_out.ap(), "iffn3", f"f3t{t}")

    def ffn_t2(call, t):
        if call == 1:
            emit_ffn_tail2(1, t, ("a1c", ag1_in.ap(), "ag1in"), f"f1t{t}")
            if t == padt - 1:
                S.add("pool", lambda e: e.collective_compute(
                    "AllGather", ALU.bypass, replica_groups=[list(range(NC))],
                    ins=[ag1_in.ap().opt()], outs=[ag1_out.ap().opt()]),
                    reads=[f"ag1in{tt}" for tt in range(padt)], writes=["agdoneC1"],
                    kind="coll", tag="AG1")
        else:
            emit_ffn_tail2(3, t, ("a4c", ag4_in.ap(), "ag4in"), f"f3t{t}")
            if t == padt - 1:
                S.add("pool", lambda e: e.collective_compute(
                    "AllGather", ALU.bypass, replica_groups=[list(range(NC))],
                    ins=[ag4_in.ap().opt()], outs=[ag4_out.ap().opt()]),
                    reads=[f"ag4in{tt}" for tt in range(padt)], writes=["agdoneC4"],
                    kind="coll", tag="AG4")

    def x_load(slot, ring, tag):
        dma("sp", f"x{ring}", sb["xr"][:, ring * 1024:(ring + 1) * 1024],
            dri["x_shard"][slot * 128:(slot + 1) * 128, :],
            reads=[], writes=[f"xr{ring}"], tag=tag)

    def kvA_front(slot, ring_x, tag):
        x_ap = sb["xr"][:, ring_x * 1024:(ring_x + 1) * 1024]
        emit_kv_front(x_ap, f"xr{ring_x}", slot % 2, tag, prestats=True)

    def kvA_back(slot, with_q, tag):
        pos = (slot + 1) % RING_KV
        kt_ap = sb["ktA"][:, pos * 128:(pos + 1) * 128]
        va_ap = sb["vaA"][:, pos * 130:pos * 130 + 130]
        q = "pair" if (with_q and slot % 2 == 1) else False
        emit_kv_back(slot % 2, kt_ap, f"ktA{pos}", va_ap, f"vaA{pos}",
                     1 + slot, q, (slot % 3), tag)

    def kvA(slot, ring_x, with_q, tag, x_ap=None, x_tok=None, prestats=False):
        pos = (slot + 1) % RING_KV
        kt_ap = sb["ktA"][:, pos * 128:(pos + 1) * 128]
        va_ap = sb["vaA"][:, pos * 130:pos * 130 + 130]
        if x_ap is None:
            x_ap = sb["xr"][:, ring_x * 1024:(ring_x + 1) * 1024]
            x_tok = f"xr{ring_x}"
        emit_kv(x_ap, x_tok, slot % 2, kt_ap, f"ktA{pos}", va_ap, f"vaA{pos}",
                1 + slot, bool(with_q), (slot % 3), tag, prestats=prestats)

    def ktva(slot):
        pos = (slot + 1) % RING_KV
        return ((sb["ktA"][:, pos * 128:(pos + 1) * 128], f"ktA{pos}"),
                (sb["vaA"][:, pos * 130:pos * 130 + 130], f"vaA{pos}"))

    def tileA(i, tag):
        kts, vas = [], []
        for c in range(3):
            kt, va = ktva(i - 1 + c)
            kts.append(kt)
            vas.append(va)
        x_ap = sb["xr"][:, (i % RING_X) * 1024:((i % RING_X) + 1) * 1024]
        emit_tile(i % 3, kts, vas, i, x_ap, f"xr{i % RING_X}", "out",
                  i * 128, tag)

    def tileA_sc(i, tag):
        kts = []
        for c in range(3):
            kt, va = ktva(i - 1 + c)
            kts.append(kt)
        emit_tile_sc(i % 3, kts, i, tag)

    def tile_kts(i):
        kts = []
        for c in range(3):
            kt, va = ktva(i - 1 + c)
            kts.append(kt)
        return kts

    def tile_vas(i):
        vas = []
        for c in range(3):
            kt, va = ktva(i - 1 + c)
            vas.append(va)
        return vas

    def tileA_pvmm(i, tag):
        x_ap = sb["xr"][:, (i % RING_X) * 1024:((i % RING_X) + 1) * 1024]
        S.add("dve", lambda e, x_ap=x_ap: e.tensor_copy(sb["xo"][:, :], x_ap),
              reads=[f"xr{i % RING_X}"], writes=["xo0"], tag=f"{tag}_xcp")
        emit_pv_group(0, tile_vas(i), tag)

    def tileA_tail(i, tag):
        vas = tile_vas(i)
        emit_norm_group(0, tag)
        emit_pv_group(1, vas, tag)
        emit_norm_group(1, tag)
        emit_wo_epi(sb["xo"][:, :], "xo0", "out", i * 128, tag)

    def statsA(slot):
        x_ap = sb["xr"][:, (slot % RING_X) * 1024:((slot % RING_X) + 1) * 1024]
        emit_ln_stats(x_ap, f"xr{slot % RING_X}", slot % 2, f"st{slot}")

    x_load(0, 0, "x0")
    statsA(0)
    x_load(1, 1, "x1")
    statsA(1)
    kvA_front(0, 0, "kv0")
    kvA_back(0, False, "kv0")
    x_load(2, 2, "x2")
    statsA(2)
    kvA_front(1, 1, "kv1")
    kvA_back(1, True, "kv1")
    kvA_front(2, 2, "kv2")
    kvA_back(2, True, "kv2")
    stats_done = 2

    after = {4: "comb1", 7: "segB", 10: "ffn3", 13: "comb3"}

    def ffn_block(call):
        ffn_h1(call, 0)
        emit_ffn_head2(call, 0, f"f{call}t0")
        for t in range(padt):
            if t + 1 < padt:
                ffn_h1(call, t + 1)
            emit_ffn_tail1(call, t, f"f{call}t{t}")
            if t + 1 < padt:
                emit_ffn_head2(call, t + 1, f"f{call}t{t + 1}")
            ffn_t2(call, t)

    ffn_block(1)

    kv_done = 2
    pending = None
    xlt_dirty = [True]
    load_done = 2

    def ensure_load(k):
        nonlocal load_done
        k = min(k, NCH_A - 1)
        while load_done < k:
            load_done += 1
            x_load(load_done, load_done % RING_X, f"x{load_done}")

    def refront(slot):
        x_ap = sb["xr"][:, (slot % RING_X) * 1024:((slot % RING_X) + 1) * 1024]
        x_load(slot, slot % RING_X, f"xr{slot}")
        emit_kv_front(x_ap, f"xr{slot % RING_X}", 2 + slot % 2, f"rf{slot}",
                      prestats=False)
    for i in range(1, NT_A):
        def advance_kv():
            nonlocal kv_done, stats_done
            k = kv_done + 1
            if stats_done < k:
                ensure_load(k)
                statsA(k)
                stats_done = k
            kvA_front(k, k % RING_X, f"kv{k}")
            if k % 2 == 1 and xlt_dirty[0]:
                refront(k - 1)
                xlt_dirty[0] = False
            kvA_back(k, k <= 29, f"kv{k}")
            kv_done = k
        while kv_done < i + 1:
            advance_kv()
        do_kv = (i != 7 and kv_done < min(i + 2, NCH_A - 1))
        if do_kv:
            k = i + 2
            if stats_done < k:
                ensure_load(k)
                statsA(k)
                stats_done = k
            kvA_front(k, k % RING_X, f"kv{k}")
            if k % 2 == 1 and xlt_dirty[0]:
                refront(k - 1)
                xlt_dirty[0] = False
        k3 = stats_done + 1
        if k3 == i + 3 and load_done >= k3 and k3 <= NCH_A - 1:
            statsA(k3)
            stats_done = k3
        kts_i = tile_kts(i)
        emit_sc_group(0, i % 3, kts_i, i, f"tA{i}")
        if pending is not None:
            tileA_tail(pending, f"tA{pending}")
            pending = None
        emit_sc_group(1, i % 3, kts_i, i, f"tA{i}")
        if do_kv:
            k = i + 2
            kvA_back(k, k <= 29, f"kv{k}")
            kv_done = k
        tileA_pvmm(i, f"tA{i}")
        pending = i
        if stats_done < min(kv_done + 1, NCH_A - 1):
            k2 = stats_done + 1
            ensure_load(k2)
            statsA(k2)
            stats_done = k2
        ensure_load(stats_done + 1)
        ph = after.get(i)
        if ph == "ffn3":
            if pending is not None:
                tileA_tail(pending, f"tA{pending}")
                pending = None
            ffn_block(3)
            xlt_dirty[0] = True
        elif ph == "comb1":
            emit_wcompute()
            S.add("pool", lambda e: e.engine_nop(), reads=["agdoneC1"],
                  writes=["agdonecomb1"], tag="agw1")
            for tt in range(2):
                dma("sp", "xoc", sb["x1own"][:, tt * 1024:(tt + 1) * 1024],
                    dri["x_own"][tt * 128:(tt + 1) * 128, :],
                    reads=[], writes=[f"x1own{tt}"], tag=f"xown{tt}")
            emit_combine(ag1_out.ap(), sb["x1own"], "x1own{}", "x1", 0, "comb1")
            dma("sp", "a2c", ag2_in.ap()[256:512, :], dri["ag2_extra"],
                reads=[], writes=["ag2inx"], tag="ag2extra")
            S.add("pool", lambda e: e.collective_compute(
                "AllGather", ALU.bypass, replica_groups=[list(range(NC))],
                ins=[ag2_in.ap().opt()], outs=[ag2_out.ap().opt()]),
                reads=["ag2in0", "ag2in1", "ag2inx"], writes=["agdoneC2"],
                kind="coll", tag="AG2")
        elif ph == "segB":
            S.add("pool", lambda e: e.memset(sb["xg"][:, :], 0.0),
                  reads=[], writes=["xg0"], tag="zb0")
            if pending is not None:
                tileA_tail(pending, f"tA{pending}")
                pending = None
            sring = i % 3
            S.add("pool", lambda e: e.indirect_dma_start(
                out=sb["xg"][:, :], out_offset=None, in_=ag2_out.ap(),
                in_offset=IOff(ap=sb["ib0"][:, 0:1], axis=0)),
                reads=["ib0", "agdoneC2", "xg0"], writes=["xg0"], kind="dma",
                ch="gb0", tag="gb0")
            S.add("pool", lambda e: e.indirect_dma_start(
                out=sb["xb3"][:, :], out_offset=None, in_=ag2_out.ap(),
                in_offset=IOff(ap=sb["ib3"][:, 0:1], axis=0)),
                reads=["ib3", "agdoneC2", "xb3"], writes=["xb3"], kind="dma",
                ch="gb3", tag="gb3")
            bx = [(sb["xg"][:, :], "xg0"),
                  (sb["x1own"][:, 0:1024], "x1own0"),
                  (sb["x1own"][:, 1024:2048], "x1own1"),
                  (sb["xb3"][:, :], "xb3")]
            for j in range(4):
                kt_ap = sb["ktB"][:, j * 128:(j + 1) * 128]
                va_ap = sb["vaB"][:, j * 130:j * 130 + 130]
                emit_kv(bx[j][0], bx[j][1], 2 + j % 2, kt_ap, f"ktB{j}", va_ap, f"vaB{j}",
                        32 + j, j == 1, sring, f"kvB{j}")
            for bt in range(2):
                if bt == 1:
                    emit_kv(bx[2][0], bx[2][1], 2, None, None, None, None,
                            34, True, sring, "kvB2q", with_kv=False)
                kts = [(sb["ktB"][:, (bt + c) * 128:(bt + c + 1) * 128], f"ktB{bt + c}")
                       for c in range(3)]
                vas = [(sb["vaB"][:, (bt + c) * 130:(bt + c) * 130 + 130], f"vaB{bt + c}")
                       for c in range(3)]
                emit_tile(sring, kts, vas, 30 + bt,
                          sb["x1own"][:, bt * 1024:(bt + 1) * 1024], f"x1own{bt}",
                          "x2", bt, f"tB{bt}")
            S.add("pool", lambda e: e.collective_compute(
                "AllGather", ALU.bypass, replica_groups=[list(range(NC))],
                ins=[ag3_in.ap().opt()], outs=[ag3_out.ap().opt()]),
                reads=["ag3in0", "ag3in1"], writes=["agdoneC3"],
                kind="coll", tag="AG3")
            xlt_dirty[0] = True
        elif ph == "comb3":
            S.add("pool", lambda e: e.engine_nop(), reads=["agdoneC4"],
                  writes=["agdonecomb3"], tag="agw4")
            emit_combine(ag4_out.ap(), sb["x2own"], "x2own{}", "out", 1, "comb3")

    if pending is not None:
        tileA_tail(pending, f"tA{pending}")
        pending = None

    S.add("pool", lambda e: e.indirect_dma_start(
        out=sb["xg"][:, :], out_offset=None, in_=ag2_out.ap(),
        in_offset=IOff(ap=sb["ia0"][:, 0:1], axis=0)),
        reads=["ia0", "agdoneC2", "xg0"], writes=["xg0"], kind="dma", ch="ga0", tag="ga0")
    emit_kv(sb["xg"][:, :], "xg0", 2, sb["ktA"][:, 0:128], "ktA0",
            sb["vaA"][:, 0:130], "vaA0", 0, False, 0, "kvm1")
    x_load(0, 0, "x0r")
    emit_kv(sb["xr"][:, 0:1024], "xr0", 3, sb["ktA"][:, 128:256], "ktA1",
            sb["vaA"][:, 130:260], "vaA1", 1, True, 0, "kv0r")
    x_load(1, 1, "x1r")
    emit_kv(sb["xr"][:, 1024:2048], "xr1", 2, sb["ktA"][:, 256:384], "ktA2",
            sb["vaA"][:, 260:390], "vaA2", 2, False, 0, "kv1r")
    tileA(0, "tA0")

    S.add("sp", None, reads=[], writes=[], tag="fence")

    with es:
        with nc.Block() as block:
            eng_sem, ch_sem, coll_sems, emit_stream = _materialize(nc, S, es)

            @block.tensor
            def _(t):
                emit_stream(t, "pe")

            @block.scalar
            def _(s):
                emit_stream(s, "act")

            @block.vector
            def _(v):
                emit_stream(v, "dve")

            @block.gpsimd
            def _(g):
                emit_stream(g, "pool")

            @block.sync
            def _(sy):
                emit_stream(sy, "sp")

    return nc



def kernel(**inputs):
    in_maps, padt, base_ids = _prep(inputs)
    nc = build_program(padt)
    res = run_bass_kernel_spmd(nc, in_maps, core_ids=list(range(NC)))
    out = np.empty((TOK, D), dtype=np.float32)
    for c in range(NC):
        shard = res.results[c]["out_shard"]
        base = base_ids[c]
        for i in range(NT_A):
            a, b = _chunk_rows(base + i)
            out[a:b] = shard[i * 128:(i + 1) * 128]
        out[256 * c:256 * c + 256] = shard[3840:4096]
    return out.reshape(BATCH, SEQ, D)


if __name__ == "__main__":
    import sys
    if "--build" in sys.argv:
        nc = build_program(5)
        n = sum(1 for _ in nc.all_instructions())
        print(f"build OK, {n} instructions")


# version 33 (final)
# speedup vs baseline: 1.0176x; 1.0050x over previous
import numpy as np
from contextlib import ExitStack

import bass_rust
import concourse.bass as bass
import concourse.mybir as mybir
from concourse.bass_utils import run_bass_kernel_spmd

F32 = mybir.dt.float32
BF16 = mybir.dt.bfloat16
I32 = mybir.dt.int32
AF = mybir.ActivationFunctionType
ALU = mybir.AluOpType
AX = mybir.AxisListType
IOff = bass_rust.IndirectOffsetOnAxis

SAME_ENGINE_SYNC = True

NC = 8
D = 1024
HD = 64
NH = 16
NKV = 2
SEQ = 16384
BATCH = 2
TOK = BATCH * SEQ
REG = 2048
NT_A = 30
NCH_A = 31
RING_KV = 5
RING_X = 4
EXPB = -30.0
EPS = 1e-5
ADDEND = float(np.exp(np.float64(EXPB)))
AG2_ROWS = 512
AG3_ROWS = 256



def _chunk_rows(cid):
    if cid < 128:
        s = cid * 128
    else:
        s = SEQ + (cid - 128) * 128
    return s, s + 128


def _prep(inputs):
    x = np.ascontiguousarray(np.asarray(inputs["L0_attn_x"], dtype=np.float32).reshape(TOK, D))
    ridx = np.asarray(inputs["L0_ffn_router_indices"], dtype=np.int32)
    rw = np.asarray(inputs["L0_ffn_routing_weights"], dtype=np.float32)

    slots = [[] for _ in range(NC)]
    for t in range(REG):
        for k in range(2):
            slots[int(ridx[t, k])].append((t, k))
    counts = [len(s) for s in slots]
    padt = max(1, int(np.ceil(max(counts) / 128.0)))
    pad = padt * 128
    slot_row = {}
    for e in range(NC):
        for i, (t, k) in enumerate(slots[e]):
            slot_row[(t, k)] = e * pad + i

    base_ids = [16 + NT_A * c for c in range(NC)]

    per_core = []
    for c in range(NC):
        base = base_ids[c]
        blocks = []
        for s in range(NCH_A):
            cid = base + s
            if cid <= 255:
                a, b = _chunk_rows(cid)
                blocks.append(x[a:b])
            else:
                blocks.append(np.zeros((128, D), np.float32))
        x_shard = np.ascontiguousarray(np.concatenate(blocks, axis=0))

        x_own = np.ascontiguousarray(x[256 * c:256 * c + 256])

        ha, hb = _chunk_rows(base - 1)
        halo_l = x[ha:hb]
        special = np.zeros((128, D), np.float32)
        if c == 7:
            special = x[2048:2176]
        ag2_extra = np.ascontiguousarray(np.concatenate([halo_l, special], axis=0))

        p = np.arange(128, dtype=np.int32)
        if c == 0:
            a0_idx = (7 * AG2_ROWS + 128 + p).astype(np.int32)
        else:
            a0_idx = (c * AG2_ROWS + 256 + p).astype(np.int32)
        if c == 0:
            b0_idx = (0 * AG2_ROWS + 384 + p).astype(np.int32)
        else:
            tpos = 256 * c - 128 + p
            b0_idx = ((tpos // 256) * AG2_ROWS + tpos % 256).astype(np.int32)
        if c < 7:
            tpos = 256 * c + 256 + p
            b3_idx = ((tpos // 256) * AG2_ROWS + tpos % 256).astype(np.int32)
        else:
            b3_idx = (7 * AG2_ROWS + 384 + p).astype(np.int32)

        ffn1 = np.zeros((128, padt), np.int32)
        ffn3 = np.zeros((128, padt), np.int32)
        for i, (t, k) in enumerate(slots[c]):
            ffn1[i % 128, i // 128] = t
            ffn3[i % 128, i // 128] = (t // 256) * AG3_ROWS + t % 256
        comb = np.zeros((128, 4), np.int32)
        for tt in range(2):
            for k in range(2):
                for pp in range(128):
                    t = 256 * c + 128 * tt + pp
                    comb[pp, 2 * k + tt] = slot_row[(t, k)]

        meta_v = np.ones((128, 3 * 32), np.float32)
        for i in range(NT_A):
            own = base + i
            for cc in range(3):
                kid = own + cc - 1
                ok = (0 <= kid <= 255) and ((kid < 128) == (own < 128))
                if not ok:
                    meta_v[:, 3 * i + cc] = 0.0
        for bt in range(2):
            for cc in range(3):
                j = bt + cc
                pos0 = 256 * c + (j - 1) * 128
                ok = (0 <= pos0) and (pos0 + 128 <= 2176)
                if not ok:
                    meta_v[:, 3 * (30 + bt) + cc] = 0.0
        vev = np.ones((128, 36), np.float32)
        if base + 30 > 255:
            vev[:, 31] = 0.0
        if c == 0:
            vev[:, 32] = 0.0

        rw_own = rw[256 * c:256 * c + 256].reshape(2, 128, 8).transpose(1, 0, 2).reshape(128, 16)
        idx_own = ridx[256 * c:256 * c + 256].reshape(2, 128, 2).transpose(1, 0, 2).reshape(128, 4)

        per_core.append(dict(
            x_shard=x_shard, x_own=x_own, ag2_extra=ag2_extra,
            a0_idx=np.ascontiguousarray(a0_idx.reshape(128, 1)),
            b0_idx=np.ascontiguousarray(b0_idx.reshape(128, 1)),
            b3_idx=np.ascontiguousarray(b3_idx.reshape(128, 1)),
            ffn_idx1=np.ascontiguousarray(ffn1), ffn_idx3=np.ascontiguousarray(ffn3),
            comb_idx=np.ascontiguousarray(comb),
            meta_v=meta_v, vev=vev,
            rw_own=np.ascontiguousarray(rw_own),
            idx_own=np.ascontiguousarray(idx_own.astype(np.int32)),
        ))

    x_region = np.ascontiguousarray(x[0:REG])

    perm = np.concatenate([np.arange(0, 2048, 2), np.arange(1, 2048, 2)])
    gu = np.asarray(inputs["L0_ffn_gate_up_proj"], dtype=np.float32)
    gub = np.asarray(inputs["L0_ffn_gate_up_proj_bias"], dtype=np.float32)
    dn = np.asarray(inputs["L0_ffn_down_proj"], dtype=np.float32)
    dnb = np.asarray(inputs["L0_ffn_down_proj_bias"], dtype=np.float32)
    consts = np.zeros((128, 3), np.float32)
    consts[:, 0] = EXPB
    consts[:, 1] = EPS
    consts[:, 2] = 1.0
    band02 = np.zeros((128, 2048), np.float32)
    w_ = np.arange(128)[:, None]
    i_ = np.arange(128)[None, :]
    b0m = (w_ >= i_).astype(np.float32)
    b2m = (w_ < i_).astype(np.float32)
    for h in range(8):
        band02[:, 128 * h:128 * (h + 1)] = b0m
        band02[:, 1024 + 128 * h:1024 + 128 * (h + 1)] = b2m
    ident = np.eye(128, dtype=np.float32)
    ones_row = np.ones((1, 256), np.float32)

    qmap = np.zeros(D, np.int64)
    for h in range(16):
        g, hw = h // 8, h % 8
        qmap[hw * 128 + 64 * g + np.arange(64)] = h * 64 + np.arange(64)
    wq_p = np.ascontiguousarray(np.asarray(inputs["L0_attn_Wq"], np.float32)[:, qmap])
    bq_p = np.ascontiguousarray(np.asarray(inputs["L0_attn_bq"], np.float32)[qmap].reshape(1, D))
    womap = np.zeros(D, np.int64)
    for j in range(8):
        g, hw4 = j // 4, j % 4
        for half in range(2):
            h = 8 * g + 4 * half + hw4
            womap[j * 128 + half * 64 + np.arange(64)] = h * 64 + np.arange(64)
    wo_p = np.ascontiguousarray(np.asarray(inputs["L0_attn_Wo"], np.float32)[womap, :])

    in_maps = []
    for c in range(NC):
        pc = per_core[c]
        m = dict(
            x_shard=pc["x_shard"], x_own=pc["x_own"], x_region=x_region,
            ag2_extra=pc["ag2_extra"],
            a0_idx=pc["a0_idx"], b0_idx=pc["b0_idx"], b3_idx=pc["b3_idx"],
            ffn_idx1=pc["ffn_idx1"], ffn_idx3=pc["ffn_idx3"], comb_idx=pc["comb_idx"],
            meta_v=pc["meta_v"], vev=pc["vev"],
            rw_own=pc["rw_own"], idx_own=pc["idx_own"],
            consts=consts, band02=band02, ident=ident, ones_row=ones_row,
            wq=wq_p, wk=np.ascontiguousarray(np.asarray(inputs["L0_attn_Wk"], np.float32)),
            wv=np.ascontiguousarray(np.asarray(inputs["L0_attn_Wv"], np.float32)),
            wo=wo_p,
            bq=bq_p,
            bk=np.ascontiguousarray(np.asarray(inputs["L0_attn_bk"], np.float32).reshape(1, 128)),
            bv=np.ascontiguousarray(np.asarray(inputs["L0_attn_bv"], np.float32).reshape(1, 128)),
            bo=np.ascontiguousarray(np.asarray(inputs["L0_attn_bo"], np.float32).reshape(1, D)),
            gu_e=np.ascontiguousarray(gu[c][:, perm]),
            gub_e=np.ascontiguousarray(gub[c][perm].reshape(1, 2048)),
            d_e=np.ascontiguousarray(dn[c]),
            db_e=np.ascontiguousarray(dnb[c].reshape(1, D)),
        )
        in_maps.append(m)
    return in_maps, padt, base_ids



ENGINES = ("pe", "act", "dve", "pool", "sp")


class Sched:
    def __init__(self):
        self.ops = []
        self.lw = {}
        self.readers = {}
        self.channels = {}

    def add(self, engine, fn, reads=(), writes=(), kind="op", ch=None, tag=""):
        i = len(self.ops)
        raw = set()
        for t in reads:
            if t in self.lw:
                raw.add(self.lw[t])
        for t in writes:
            if t in self.lw:
                raw.add(self.lw[t])
            raw |= self.readers.get(t, set())
        raw.discard(i)
        deps = set()
        dma_waits = {}
        for d in raw:
            dop = self.ops[d]
            if dop["kind"] == "dma":
                dma_waits[dop["ch"]] = self.channels[dop["ch"]]
            elif (not SAME_ENGINE_SYNC and kind == "op" and dop["kind"] == "op"
                  and dop["engine"] == engine):
                continue
            else:
                deps.add(d)
        seq = None
        if kind == "dma":
            assert ch is not None
            self.channels[ch] = self.channels.get(ch, 0) + 1
            seq = self.channels[ch]
        self.ops.append(dict(engine=engine, fn=fn, deps=deps, dma_waits=dma_waits,
                             kind=kind, ch=ch, seq=seq, tag=tag))
        for t in reads:
            self.readers.setdefault(t, set()).add(i)
        for t in writes:
            self.lw[t] = i
            self.readers[t] = set()
        return i


def _materialize(nc, sched, es):
    ops = sched.ops
    targets = set()
    for op in ops:
        targets |= op["deps"]
    eng_count = {e: 0 for e in ENGINES}
    milestone = {}
    for i, op in enumerate(ops):
        if op["kind"] == "op" and i in targets:
            eng_count[op["engine"]] += 1
            milestone[i] = eng_count[op["engine"]]
    eng_sem = {e: es.enter_context(nc.semaphore(f"se_{e}")) for e in ENGINES}
    ch_sem = {c: es.enter_context(nc.semaphore(f"sd_{c}")) for c in sched.channels}
    n_coll = sum(1 for op in ops if op["kind"] == "coll")
    coll_sems = [es.enter_context(nc.semaphore(f"scc_{i}")) for i in range(n_coll)]
    coll_idx = {}
    k = 0
    for i, op in enumerate(ops):
        if op["kind"] == "coll":
            coll_idx[i] = k
            k += 1

    def wait_spec(i):
        op = ops[i]
        if op["kind"] == "op":
            return (eng_sem[op["engine"]], milestone.get(i))
        if op["kind"] == "coll":
            return (coll_sems[coll_idx[i]], 1)
        return None

    streams = {e: [] for e in ENGINES}
    for i, op in enumerate(ops):
        streams[op["engine"]].append(i)

    waited = {e: {} for e in ENGINES}

    def emit_stream(eng_obj, eng_name):
        wmap = waited[eng_name]
        for i in streams[eng_name]:
            op = ops[i]
            specs = []
            for d in sorted(op["deps"]):
                ws = wait_spec(d)
                if ws is not None and ws[1] is not None:
                    specs.append(ws)
            for ch, cnt in sorted(op["dma_waits"].items()):
                specs.append((ch_sem[ch], 16 * cnt))
            for sem, val in specs:
                key = id(sem)
                if wmap.get(key, -1) >= val:
                    continue
                eng_obj.wait_ge(sem, val)
                wmap[key] = val
            if op["fn"] is None:
                continue
            instr = op["fn"](eng_obj)
            if op["kind"] == "dma":
                instr.then_inc(ch_sem[op["ch"]], 16)
            elif op["kind"] == "coll":
                instr.then_inc(coll_sems[coll_idx[i]])
            elif i in milestone:
                instr.then_inc(eng_sem[op["engine"]], 1)
        if eng_name == "sp":
            for c, n in sorted(sched.channels.items()):
                sem = ch_sem[c]
                if wmap.get(id(sem), -1) < 16 * n:
                    eng_obj.wait_ge(sem, 16 * n)
            for s in coll_sems:
                eng_obj.wait_ge(s, 1)
            for e in ENGINES:
                if eng_count[e]:
                    eng_obj.wait_ge(eng_sem[e], eng_count[e])

    return eng_sem, ch_sem, coll_sems, emit_stream



def build_program(padt, debug=False):
    nc = bass.Bass("TRN2", target_bir_lowering=False, debug=debug)
    pad = padt * 128

    dri = {}
    for name, shape, dt in [
        ("x_shard", [NCH_A * 128, D], F32), ("x_own", [256, D], F32),
        ("x_region", [REG, D], F32), ("ag2_extra", [256, D], F32),
        ("a0_idx", [128, 1], I32), ("b0_idx", [128, 1], I32), ("b3_idx", [128, 1], I32),
        ("ffn_idx1", [128, padt], I32), ("ffn_idx3", [128, padt], I32),
        ("comb_idx", [128, 4], I32),
        ("meta_v", [128, 96], F32), ("vev", [128, 36], F32),
        ("rw_own", [128, 16], F32), ("idx_own", [128, 4], I32),
        ("consts", [128, 3], F32), ("band02", [128, 2048], F32),
        ("ident", [128, 128], F32), ("ones_row", [1, 256], F32),
        ("wq", [D, D], F32), ("wk", [D, 128], F32), ("wv", [D, 128], F32), ("wo", [D, D], F32),
        ("bq", [1, D], F32), ("bk", [1, 128], F32), ("bv", [1, 128], F32), ("bo", [1, D], F32),
        ("gu_e", [D, 2048], F32), ("gub_e", [1, 2048], F32),
        ("d_e", [D, D], F32), ("db_e", [1, D], F32),
    ]:
        dri[name] = nc.dram_tensor(name, shape, dt, kind="ExternalInput").ap()

    out_shard = nc.dram_tensor("out_shard", [4096, D], F32, kind="ExternalOutput").ap()

    ag1_in = nc.dram_tensor("ag1_in", [pad, D], BF16)
    ag1_out = nc.dram_tensor("ag1_out", [NC * pad, D], BF16, addr_space="Shared")
    ag2_in = nc.dram_tensor("ag2_in", [AG2_ROWS, D], F32)
    ag2_out = nc.dram_tensor("ag2_out", [NC * AG2_ROWS, D], F32, addr_space="Shared")
    ag3_in = nc.dram_tensor("ag3_in", [AG3_ROWS, D], F32)
    ag3_out = nc.dram_tensor("ag3_out", [NC * AG3_ROWS, D], F32, addr_space="Shared")
    ag4_in = nc.dram_tensor("ag4_in", [pad, D], BF16)
    ag4_out = nc.dram_tensor("ag4_out", [NC * pad, D], BF16, addr_space="Shared")

    es = ExitStack()
    E = es.enter_context

    sb = {}

    def SB(name, shape, dt):
        sb[name] = E(nc.sbuf_tensor("sb_" + name, shape, dt))
        return sb[name]

    SB("wq", [128, 8 * 1024], BF16)
    SB("wk", [128, 8 * 128], BF16)
    SB("wv", [128, 8 * 128], BF16)
    SB("wo", [128, 8 * 1024], BF16)
    SB("gu", [128, 8 * 2048], BF16)
    SB("dw", [128, 8 * 1024], BF16)
    SB("bq", [1, D], BF16)
    SB("bk", [1, 128], BF16)
    SB("bv", [1, 128], BF16)
    SB("bo", [1, D], BF16)
    SB("gub", [1, 2048], BF16)
    SB("db", [1, D], BF16)
    SB("ones", [1, 256], BF16)
    SB("ident", [128, 128], BF16)
    SB("band02", [128, 2048], BF16)
    SB("consts", [128, 3], F32)
    SB("meta_v", [128, 96], F32)
    SB("vev", [128, 36], F32)
    SB("ktA", [128, RING_KV * 128], BF16)
    SB("vaA", [128, RING_KV * 130], BF16)
    SB("ktB", [128, 4 * 128], BF16)
    SB("vaB", [128, 4 * 130], BF16)
    SB("xr", [128, RING_X * 1024], F32)
    SB("xb3", [128, 1024], F32)
    SB("xln", [128, 2 * 1024], BF16)
    SB("xlt", [128, 2 * 1024], BF16)
    SB("qt", [128, 3 * 1024], BF16)
    SB("pt", [128, 6 * 1024], BF16)
    SB("attnT", [128, 1024], BF16)
    SB("rsb", [64, 1024], BF16)
    SB("rcp", [1, 1024], BF16)
    SB("lnr", [1, 1024], F32)
    SB("bnt", [128, 48], F32)
    SB("xo", [128, 1024], F32)
    SB("x1own", [128, 2 * 1024], F32)
    SB("x2own", [128, 2 * 1024], F32)
    SB("ybuf", [128, 1024], BF16)
    SB("ybuf2", [128, 1024], BF16)
    SB("wcell", [128, 16], F32)
    SB("lnc", [128, 32], F32)
    SB("xg", [128, 1024], F32)
    SB("swz", [128, 3 * 1024], F32)
    SB("inter", [128, 1024], BF16)
    SB("itT", [128, 1024], BF16)
    SB("iffn1", [128, padt], I32)
    SB("iffn3", [128, padt], I32)
    SB("icomb", [128, 4], I32)
    SB("ia0", [128, 1], I32)
    SB("ib0", [128, 1], I32)
    SB("ib3", [128, 1], I32)
    SB("rw", [128, 16], F32)
    SB("idxo", [128, 4], I32)
    SB("idxf", [128, 4], F32)

    ps = {}
    ps["pAB"] = E(nc.psum_tensor("pAB", [128, 1024], F32))
    ps["pT"] = E(nc.psum_tensor("pT", [128, 1024], F32))
    ps["s0"] = E(nc.psum_tensor("s0", [128, 1024], F32))
    ps["s1"] = E(nc.psum_tensor("s1", [128, 1024], F32))

    S = Sched()

    def dma(engine, ch, out_ap, in_ap, reads, writes, tag=""):
        S.add(engine, lambda e, o=out_ap, i=in_ap: e.dma_start(out=o, in_=i),
              reads=reads, writes=writes, kind="dma", ch=ch, tag=tag)

    for name in ("vaA", "vaB"):
        S.add("pool", lambda e, t=sb[name]: e.memset(t[:, :], 1.0),
              writes=[name], tag="memset")
    S.add("pool", lambda e: e.memset(sb["xb3"][:, :], 0.0), writes=["xb3"])

    small = [
        ("consts", "consts"), ("meta_v", "meta_v"), ("vev", "vev"),
        ("band02", "band02"), ("ident", "ident"), ("ones", "ones_row"),
        ("bq", "bq"), ("bk", "bk"), ("bv", "bv"), ("bo", "bo"),
        ("gub", "gub_e"), ("db", "db_e"),
        ("iffn1", "ffn_idx1"), ("iffn3", "ffn_idx3"), ("icomb", "comb_idx"),
        ("ia0", "a0_idx"), ("ib0", "b0_idx"), ("ib3", "b3_idx"),
        ("rw", "rw_own"), ("idxo", "idx_own"),
    ]
    for sname, dname in small:
        dma("pool", "ci", sb[sname][:, :], dri[dname], reads=[], writes=[sname], tag=f"ld_{sname}")

    def wload(ch, sname, dname, n):
        out_ap = sb[sname][:, :].rearrange("p (dc n) -> p dc n", dc=8)
        in_ap = dri[dname].rearrange("(dc p) n -> p dc n", p=128)
        dma("pool", ch, out_ap, in_ap, reads=[], writes=[sname], tag=f"ld_{sname}")
    wload("cwf", "gu", "gu_e", 2048)
    wload("cwf", "dw", "d_e", 1024)
    wload("cwa", "wq", "wq", 1024)
    wload("cwa", "wk", "wk", 128)
    wload("cwa", "wv", "wv", 128)
    wload("cwa", "wo", "wo", 1024)

    def emit_ln_stats(src_ap, src_tok, ring, tag):
        cells = sb["lnc"]
        c0 = 8 * ring
        ct = f"lnc{ring}"
        bt = sb["bnt"][:, ring * 12:ring * 12 + 12]
        btok = f"bnt{ring}"
        S.add("dve", lambda e: e.bn_stats(bt[:, 0:6], src_ap[:, 0:512]),
              reads=[src_tok], writes=[btok, ct + "u"], tag=f"{tag}_bs0")
        S.add("dve", lambda e: e.bn_stats(bt[:, 6:12], src_ap[:, 512:1024]),
              reads=[src_tok, btok], writes=[btok], tag=f"{tag}_bs1")
        S.add("dve", lambda e: e.bn_aggr(cells[:, c0 + 2:c0 + 4], bt[:, 0:12]),
              reads=[btok], writes=[ct + "m", ct + "v"], tag=f"{tag}_agg")
        S.add("act", lambda e: e.activation(cells[:, c0 + 5:c0 + 6], cells[:, c0 + 3:c0 + 4],
                                            AF.Ln, bias=sb["consts"][:, 1:2], scale=1.0),
              reads=[ct + "v", "consts"], writes=[ct + "l"], tag=f"{tag}_ln")
        S.add("act", lambda e: e.activation(cells[:, c0 + 6:c0 + 7], cells[:, c0 + 5:c0 + 6],
                                            AF.Exp, scale=-0.5),
              reads=[ct + "l"], writes=[ct + "r"], tag=f"{tag}_rstd")
        S.add("dve", lambda e: e.scalar_tensor_tensor(cells[:, c0 + 7:c0 + 8], cells[:, c0 + 2:c0 + 3],
                                                      -1.0, cells[:, c0 + 6:c0 + 7],
                                                      op0=ALU.mult, op1=ALU.mult),
              reads=[ct + "m", ct + "r"], writes=[ct + "n"], tag=f"{tag}_nb")
    def emit_ln_apply(src_ap, src_tok, out_ap, out_tok, ring, tag):
        cells = sb["lnc"]
        c0 = 8 * ring
        ct = f"lnc{ring}"
        S.add("dve", lambda e: e.tensor_scalar(
            out_ap, src_ap, cells[:, c0 + 6:c0 + 7], cells[:, c0 + 7:c0 + 8],
            op0=ALU.mult, op1=ALU.add),
            reads=[src_tok, ct + "r", ct + "n"], writes=[out_tok, ct + "u"], tag=f"{tag}_app")

    def emit_ln(src_ap, src_tok, out_ap, out_tok, ring, tag):
        emit_ln_stats(src_ap, src_tok, ring, tag)
        emit_ln_apply(src_ap, src_tok, out_ap, out_tok, ring, tag)

    def emit_tr(src, src_tok, dst, dst_tok, tag):
        def f(e):
            ins = None
            for dc in range(8):
                bank_half = dc * 128
                ins = e.matmul(ps["pT"][:, bank_half:bank_half + 128],
                               src[:, dc * 128:(dc + 1) * 128], sb["ident"][:, :],
                               start=True, stop=True)
            return ins
        S.add("pe", f, reads=[src_tok, "ident"], writes=["pT"], tag=f"{tag}_tr")
        S.add("dve", lambda e: e.tensor_copy(dst, ps["pT"][:, :]),
              reads=["pT"], writes=[dst_tok], tag=f"{tag}_trev")

    def emit_kv_front(xsrc_ap, xsrc_tok, ln_ring, tag, prestats=False):
        bln = ln_ring % 2
        xln_ap = sb["xln"][:, bln * 1024:(bln + 1) * 1024]
        xlt_ap = sb["xlt"][:, bln * 1024:(bln + 1) * 1024]
        if not prestats:
            emit_ln_stats(xsrc_ap, xsrc_tok, ln_ring, tag + "_ln")
        emit_ln_apply(xsrc_ap, xsrc_tok, xln_ap, f"xln{bln}", ln_ring, tag + "_ln")
        emit_tr(xln_ap, f"xln{bln}", xlt_ap, f"xlt{bln}", tag)

    def emit_kv_back(ln_ring, kt_ap, kt_tok, va_ap, va_tok, vev_col,
                     with_q, qt_ring, tag, with_kv=True):
        bln = ln_ring % 2
        xlt_ap = sb["xlt"][:, bln * 1024:(bln + 1) * 1024]
        xlt_tok = f"xlt{bln}"

        if with_q == "single":
            def fq(e):
                ins = None
                for half in range(2):
                    for qcl in range(4):
                        qc = half * 4 + qcl
                        col = half * 512 + qcl * 128
                        ins = e.matmul(ps["pAB"][:, col:col + 128],
                                       sb["bq"][0:1, qc * 128:(qc + 1) * 128],
                                       sb["ones"][0:1, 0:128], start=True, stop=False)
                        for dc in range(8):
                            ins = e.matmul(ps["pAB"][:, col:col + 128],
                                           sb["wq"][:, dc * 1024 + qc * 128:dc * 1024 + (qc + 1) * 128],
                                           xlt_ap[:, dc * 128:(dc + 1) * 128],
                                           start=False, stop=(dc == 7))
                return ins
            S.add("pe", fq, reads=[xlt_tok, "wq", "bq", "ones"], writes=["pA", "pB"],
                  tag=f"{tag}_qt")
            qdst = sb["qt"][:, qt_ring * 1024: (qt_ring + 1) * 1024]
            S.add("dve", lambda e, d=qdst: e.tensor_copy(d, ps["pAB"][:, :]),
                  reads=["pA", "pB"], writes=[f"qt{qt_ring}_0", f"qt{qt_ring}_1"],
                  tag=f"{tag}_qtev")
        elif with_q == "pair":
            xlt2 = sb["xlt"][:, :].rearrange("p (b n) -> p b n", b=2)
            qr0, qr1 = (qt_ring + 2) % 3, qt_ring
            for half in range(2):
                def fqp(e, half=half):
                    ins = None
                    for qcl in range(4):
                        qc = half * 4 + qcl
                        col = qcl * 256
                        ins = e.matmul(ps["pAB"][:, col:col + 256],
                                       sb["bq"][0:1, qc * 128:(qc + 1) * 128],
                                       sb["ones"][0:1, 0:256], start=True, stop=False)
                        for dc in range(8):
                            ins = e.matmul(ps["pAB"][:, col:col + 256],
                                           sb["wq"][:, dc * 1024 + qc * 128:dc * 1024 + (qc + 1) * 128],
                                           xlt2[:, :, dc * 128:(dc + 1) * 128],
                                           start=False, stop=(dc == 7))
                    return ins
                S.add("pe", fqp, reads=["xlt0", "xlt1", "wq", "bq", "ones"],
                      writes=["pA", "pB"], tag=f"{tag}_qtp{half}")
                psrc = ps["pAB"][:, :].rearrange("p (q b n) -> p q b n", q=4, b=2)
                for bi, qr in ((0, qr0), (1, qr1)):
                    qdst = sb["qt"][:, qr * 1024 + half * 512: qr * 1024 + (half + 1) * 512]
                    S.add("dve", lambda e, d=qdst, s=psrc, bi=bi: e.tensor_copy(
                        d.rearrange("p (q n) -> p q n", q=4), s[:, :, bi, :]),
                        reads=["pA", "pB"], writes=[f"qt{qr}_{half}"],
                        tag=f"{tag}_qtpev{half}{bi}")

        if not with_kv:
            return

        def fk(e):
            ins = e.matmul(ps["pAB"][:, 0:128], sb["bk"][0:1, :], sb["ones"][0:1, 0:128],
                           start=True, stop=False)
            for dc in range(8):
                ins = e.matmul(ps["pAB"][:, 0:128],
                               sb["wk"][:, dc * 128:(dc + 1) * 128],
                               xlt_ap[:, dc * 128:(dc + 1) * 128],
                               start=False, stop=(dc == 7))
            return ins
        S.add("pe", fk, reads=[xlt_tok, "wk", "bk", "ones"], writes=["pA"], tag=f"{tag}_kt")
        S.add("dve", lambda e: e.tensor_copy(kt_ap, ps["pAB"][:, 0:128]),
              reads=["pA"], writes=[kt_tok], tag=f"{tag}_ktev")

        def fv(e):
            ins = e.matmul(ps["pAB"][:, 512:640], sb["ones"][0:1, 0:128], sb["bv"][0:1, :],
                           start=True, stop=False)
            for dc in range(8):
                ins = e.matmul(ps["pAB"][:, 512:640],
                               xlt_ap[:, dc * 128:(dc + 1) * 128],
                               sb["wv"][:, dc * 128:(dc + 1) * 128],
                               start=False, stop=(dc == 7))
            return ins
        S.add("pe", fv, reads=[xlt_tok, "wv", "bv", "ones"], writes=["pB"], tag=f"{tag}_v")
        for g in range(2):
            S.add("dve", lambda e, g=g: e.tensor_scalar(
                va_ap[:, g * 65:g * 65 + 64], ps["pAB"][:, 512 + g * 64:512 + (g + 1) * 64],
                sb["vev"][:, vev_col:vev_col + 1], None, op0=ALU.mult),
                reads=["pB", "vev"], writes=[va_tok], tag=f"{tag}_vev{g}")

    def emit_kv(xsrc_ap, xsrc_tok, ln_ring, kt_ap, kt_tok, va_ap, va_tok, vev_col,
                with_q, qt_ring, tag, with_kv=True, prestats=False):
        emit_kv_front(xsrc_ap, xsrc_tok, ln_ring, tag, prestats=prestats)
        emit_kv_back(ln_ring, kt_ap, kt_tok, va_ap, va_tok, vev_col,
                     "single" if with_q else False, qt_ring, tag, with_kv=with_kv)

    BLKMAP = {0: 0, 2: 1, 1: 2}

    def emit_sc_one(g, c, qt_ring, kt_list, meta_tile, tag):
        stn = {0: "s0", 2: "s1", 1: "s0"}
        for c in (c,):
                blk = 3 * g + BLKMAP[c]
                pn = stn[c]
                kt_ap, kt_tok = kt_list[c]

                def fs(e, g=g, c=c, pn=pn, kt_ap=kt_ap):
                    ins = None
                    for half in range(2):
                        rhs = sb["qt"][64 * g:64 * g + 64,
                                       qt_ring * 1024 + half * 512:qt_ring * 1024 + (half + 1) * 512]
                        ins = e.matmul(ps[pn][:, half * 512:(half + 1) * 512],
                                       kt_ap[64 * g:64 * g + 64, :], rhs,
                                       start=True, stop=True)
                    return ins
                S.add("pe", fs, reads=[kt_tok, f"qt{qt_ring}_0", f"qt{qt_ring}_1"],
                      writes=[pn], tag=f"{tag}_sc{g}{c}")
                ptb = sb["pt"][:, blk * 1024:(blk + 1) * 1024]
                mc = meta_tile * 3 + c
                S.add("act", lambda e, pn=pn, ptb=ptb, mc=mc: e.activation(
                    ptb, ps[pn][:, :], AF.Exp, bias=sb["consts"][:, 0:1],
                    scale=sb["meta_v"][:, mc:mc + 1]),
                    reads=[pn, "consts", "meta_v"], writes=[f"pt{blk}"],
                    tag=f"{tag}_exp{g}{c}")
                if c == 2:
                    ptm = sb["pt"][:, (3 * g) * 1024:(3 * g + 2) * 1024]
                    S.add("dve", lambda e, ptm=ptm: e.tensor_tensor(
                        ptm, ptm, sb["band02"][:, :], op=ALU.mult),
                        reads=[f"pt{3 * g}", f"pt{3 * g + 1}", "band02"],
                        writes=[f"pt{3 * g}", f"pt{3 * g + 1}"], tag=f"{tag}_mb{g}")

    def emit_sc_group(g, qt_ring, kt_list, meta_tile, tag):
        for c in (0, 2, 1):
            emit_sc_one(g, c, qt_ring, kt_list, meta_tile, tag)

    def emit_tile_sc(qt_ring, kt_list, meta_tile, tag):
        emit_sc_group(0, qt_ring, kt_list, meta_tile, tag)
        emit_sc_group(1, qt_ring, kt_list, meta_tile, tag)

    def emit_pv_group(g, va_list, tag):
        for par, pn in ((0, "pA"), (1, "pB")):
            def fpv(e, g=g, par=par):
                ins = None
                for c in range(3):
                    va_ap, _ = va_list[c]
                    ins = e.matmul(ps["pAB"][0:65, par * 512:(par + 1) * 512],
                                   va_ap[:, g * 65:(g + 1) * 65],
                                   sb["pt"][:, (3 * g + BLKMAP[c]) * 1024 + par * 512:
                                            (3 * g + BLKMAP[c]) * 1024 + (par + 1) * 512],
                                   start=(c == 0), stop=(c == 2))
                return ins
            S.add("pe", fpv,
                  reads=[va_list[0][1], va_list[1][1], va_list[2][1],
                         f"pt{3 * g + 0}", f"pt{3 * g + 1}", f"pt{3 * g + 2}"],
                  writes=[pn], tag=f"{tag}_pv{g}{par}")
        S.add("act", lambda e: e.activation(
            sb["lnr"][0:1, :], ps["pAB"][64:65, :], AF.Ln),
            reads=["pA", "pB"], writes=["lnr0", "lnr1"], tag=f"{tag}_ln{g}")

        def fex(e):
            with nc.allow_low_precision(reason="bf16 softmax-normalizer operand"):
                return e.activation(sb["rcp"][0:1, :], sb["lnr"][0:1, :],
                                    AF.Exp, scale=-1.0)
        S.add("act", fex, reads=["lnr0", "lnr1"], writes=["rcp0", "rcp1"],
              tag=f"{tag}_rc{g}")

    def emit_rb(g, tag):
        def frb(e):
            ins = None
            for par in range(2):
                ins = e.matmul(ps["pT"][0:64, par * 512:(par + 1) * 512],
                               sb["ones"][0:1, 0:64],
                               sb["rcp"][0:1, par * 512:(par + 1) * 512],
                               start=True, stop=True)
            return ins
        S.add("pe", frb, reads=["rcp0", "rcp1", "ones"], writes=["pT"], tag=f"{tag}_rb{g}")
        S.add("act", lambda e: e.activation(sb["rsb"][:, :], ps["pT"][0:64, :], AF.Identity),
              reads=["pT"], writes=["rsb"], tag=f"{tag}_rev{g}")

    def emit_tnorm(g, tag):
        for par, pn in ((0, "pA"), (1, "pB")):
            dst = sb["attnT"][par * 64:par * 64 + 64, g * 512:(g + 1) * 512]
            S.add("dve", lambda e, par=par, dst=dst: e.tensor_tensor(
                dst, ps["pAB"][0:64, par * 512:(par + 1) * 512],
                sb["rsb"][0:64, par * 512:(par + 1) * 512],
                op=ALU.mult),
                reads=[pn, "rsb"], writes=["attnT"], tag=f"{tag}_tn{g}{par}")

    def emit_norm_group(g, tag):
        emit_rb(g, tag)
        emit_tnorm(g, tag)

    def emit_wo_epi(x_ap, x_tok, dst_kind, dst_info, tag):
        def fwo(e):
            ins = None
            for half in range(2):
                ins = e.matmul(ps["pAB"][:, half * 512:(half + 1) * 512],
                               sb["ones"][0:1, 0:128],
                               sb["bo"][0:1, half * 512:(half + 1) * 512],
                               start=True, stop=False)
            for dc in range(8):
                for half in range(2):
                    ins = e.matmul(ps["pAB"][:, half * 512:(half + 1) * 512],
                                   sb["attnT"][:, dc * 128:(dc + 1) * 128],
                                   sb["wo"][:, dc * 1024 + half * 512:dc * 1024 + (half + 1) * 512],
                                   start=False, stop=(dc == 7))
            return ins
        S.add("pe", fwo, reads=["attnT", "wo", "bo", "ones"], writes=["pA", "pB"],
              tag=f"{tag}_wo")

        if dst_kind == "out":
            row0 = dst_info
            xo_ap = sb["xo"][:, :]
            S.add("dve", lambda e, xo_ap=xo_ap, x_ap=x_ap: e.tensor_tensor(
                xo_ap, ps["pAB"][:, :], x_ap, op=ALU.add),
                reads=["pA", "pB", x_tok], writes=["xo0"], tag=f"{tag}_epi")
            dma("sp", "ost", out_shard[row0:row0 + 128, :], xo_ap,
                reads=["xo0"], writes=[f"ost{row0}"], tag=f"{tag}_st")
        else:
            tt = dst_info
            x2_ap = sb["x2own"][:, tt * 1024:(tt + 1) * 1024]
            S.add("dve", lambda e, x2_ap=x2_ap, x_ap=x_ap: e.tensor_tensor(
                x2_ap, ps["pAB"][:, :], x_ap, op=ALU.add),
                reads=["pA", "pB", x_tok], writes=[f"x2own{tt}"], tag=f"{tag}_epi")
            dma("sp", "a3c", ag3_in.ap()[tt * 128:(tt + 1) * 128, :], x2_ap,
                reads=[f"x2own{tt}"], writes=[f"ag3in{tt}"], tag=f"{tag}_st")

    def emit_tile_pv(va_list, x_ap, x_tok, dst_kind, dst_info, tag):
        emit_pv_group(0, va_list, tag)
        emit_norm_group(0, tag)
        emit_pv_group(1, va_list, tag)
        emit_norm_group(1, tag)
        emit_wo_epi(x_ap, x_tok, dst_kind, dst_info, tag)

    def emit_tile(qt_ring, kt_list, va_list, meta_tile, x_ap, x_tok, dst_kind, dst_info, tag):
        emit_tile_sc(qt_ring, kt_list, meta_tile, tag)
        emit_tile_pv(va_list, x_ap, x_tok, dst_kind, dst_info, tag)

    def emit_ffn_head(call, t, src_dram, idx_sb_name, tag):
        ring = t % 2
        xg_ap = sb["xg"][:, :]
        xg_tok = "xg0"
        S.add("pool", lambda e: e.indirect_dma_start(
            out=xg_ap, out_offset=None, in_=src_dram,
            in_offset=IOff(ap=sb[idx_sb_name][:, t:t + 1], axis=0)),
            reads=[idx_sb_name, f"agsrc{call}"], writes=[xg_tok], kind="dma",
            ch="xg0", tag=f"{tag}_g")
        fln_ap = sb["xln"][:, ring * 1024:(ring + 1) * 1024]
        flt_ap = sb["xlt"][:, ring * 1024:(ring + 1) * 1024]
        emit_ln(xg_ap, xg_tok, fln_ap, f"xln{ring}", 2 + ring, tag + "_ln")
        emit_tr(fln_ap, f"xln{ring}", flt_ap, f"xlt{ring}", tag)
        flt_tok = f"xlt{ring}"

        for half, pn in ((0, "s0"), (1, "s1")):
            def fg(e, half=half, pn=pn, flt_ap=flt_ap):
                ins = None
                for q in range(2):
                    col = half * 1024 + q * 512
                    ins = e.matmul(ps[pn][:, q * 512:(q + 1) * 512],
                                   sb["ones"][0:1, 0:128],
                                   sb["gub"][0:1, col:col + 512], start=True, stop=False)
                    for dc in range(8):
                        ins = e.matmul(ps[pn][:, q * 512:(q + 1) * 512],
                                       flt_ap[:, dc * 128:(dc + 1) * 128],
                                       sb["gu"][:, dc * 2048 + col:dc * 2048 + col + 512],
                                       start=False, stop=(dc == 7))
                return ins
            S.add("pe", fg, reads=[flt_tok, "gu", "gub", "ones"], writes=[pn],
                  tag=f"{tag}_gu{half}")
    def emit_ffn_head2(call, t, tag):
        A = sb["swz"][:, 0:1024]
        C = sb["swz"][:, 2048:3072]
        S.add("dve", lambda e, A=A: e.tensor_scalar(A, ps["s0"][:, :], 7.0, None, op0=ALU.min),
              reads=["s0"], writes=["swzA"], tag=f"{tag}_gmin")
        S.add("dve", lambda e, C=C: e.tensor_scalar(C, ps["s1"][:, :], 7.0, -7.0,
                                                    op0=ALU.min, op1=ALU.max),
              reads=["s1"], writes=["swzC"], tag=f"{tag}_uclip")

    def emit_ffn_tail1(call, t, tag):
        A = sb["swz"][:, 0:1024]
        B = sb["swz"][:, 1024:2048]
        C = sb["swz"][:, 2048:3072]
        S.add("act", lambda e, A=A, B=B: e.activation(B, A, AF.Exp, scale=-1.702),
              reads=["swzA"], writes=["swzB"], tag=f"{tag}_exp")
        S.add("act", lambda e, B=B: e.activation(B, B, AF.Ln, bias=sb["consts"][:, 2:3]),
              reads=["swzB", "consts"], writes=["swzB"], tag=f"{tag}_ln1p")
        S.add("act", lambda e, B=B: e.activation(B, B, AF.Exp, scale=-1.0),
              reads=["swzB"], writes=["swzB"], tag=f"{tag}_sig")
        S.add("dve", lambda e, A=A, B=B: e.tensor_tensor(A, A, B, op=ALU.mult),
              reads=["swzA", "swzB"], writes=["swzA"], tag=f"{tag}_glu")
        S.add("dve", lambda e, A=A, C=C: e.scalar_tensor_tensor(
            sb["inter"][:, :], C, 1.0, A, op0=ALU.add, op1=ALU.mult),
            reads=["swzA", "swzC"], writes=["inter"], tag=f"{tag}_inter")

    def emit_ffn_tail2(call, t, contrib, tag):
        emit_tr(sb["inter"][:, :], "inter", sb["itT"][:, :], "itT", tag + "_it")

        def fd(e):
            ins = None
            for half in range(2):
                ins = e.matmul(ps["pAB"][:, half * 512:(half + 1) * 512],
                               sb["ones"][0:1, 0:128],
                               sb["db"][0:1, half * 512:(half + 1) * 512],
                               start=True, stop=False)
            for ic in range(8):
                for half in range(2):
                    ins = e.matmul(ps["pAB"][:, half * 512:(half + 1) * 512],
                                   sb["itT"][:, ic * 128:(ic + 1) * 128],
                                   sb["dw"][:, ic * 1024 + half * 512:ic * 1024 + (half + 1) * 512],
                                   start=False, stop=(ic == 7))
            return ins
        S.add("pe", fd, reads=["itT", "dw", "db", "ones"], writes=["pA", "pB"],
              tag=f"{tag}_dn")
        S.add("act", lambda e: e.activation(sb["inter"][:, :], ps["pAB"][:, :], AF.Identity),
              reads=["pA", "pB"], writes=["inter"], tag=f"{tag}_ycev")
        dma("sp", contrib[0], contrib[1][t * 128:(t + 1) * 128, :], sb["inter"][:, :],
            reads=["inter"], writes=[f"{contrib[2]}{t}"], tag=f"{tag}_yst")

    def emit_wcompute():
        S.add("dve", lambda e: e.tensor_copy(sb["idxf"][:, :], sb["idxo"][:, :]),
              reads=["idxo"], writes=["idxf"], tag="w_cast")
        for tt in range(2):
            for k in range(2):
                col = 2 * tt + k
                acc_a, acc_b = 4 + 2 * col, 5 + 2 * col
                for ei in range(8):
                    eq = 14
                    S.add("dve", lambda e, col=col, ei=ei, eq=eq: e.tensor_scalar(
                        sb["wcell"][:, eq:eq + 1], sb["idxf"][:, col:col + 1],
                        float(ei), None, op0=ALU.is_equal),
                        reads=["idxf"], writes=["weq"], tag=f"w_eq{col}_{ei}")
                    rcol = 8 * tt + ei
                    if ei == 0:
                        S.add("dve", lambda e, acc_a=acc_a, eq=eq, rcol=rcol: e.tensor_tensor(
                            sb["wcell"][:, acc_a:acc_a + 1], sb["wcell"][:, eq:eq + 1],
                            sb["rw"][:, rcol:rcol + 1], op=ALU.mult),
                            reads=["weq", "rw"], writes=[f"wacc{col}a"], tag=f"w_a{col}_{ei}")
                    else:
                        src_t, dst_t = ((f"wacc{col}a", f"wacc{col}b") if ei % 2 == 1
                                        else (f"wacc{col}b", f"wacc{col}a"))
                        src_c = acc_a if ei % 2 == 1 else acc_b
                        dst_c = acc_b if ei % 2 == 1 else acc_a
                        S.add("dve", lambda e, s=src_c, d=dst_c, eq=eq, rcol=rcol:
                              e.scalar_tensor_tensor(
                                  sb["wcell"][:, d:d + 1], sb["wcell"][:, eq:eq + 1],
                                  sb["rw"][:, rcol:rcol + 1], sb["wcell"][:, s:s + 1],
                                  op0=ALU.mult, op1=ALU.add),
                              reads=["weq", "rw", src_t], writes=[dst_t], tag=f"w_a{col}_{ei}")
                final_c = acc_b if 7 % 2 == 1 else acc_a
                final_t = f"wacc{col}b" if 7 % 2 == 1 else f"wacc{col}a"
                S.add("dve", lambda e, col=col, final_c=final_c: e.tensor_scalar(
                    sb["wcell"][:, col:col + 1], sb["wcell"][:, final_c:final_c + 1],
                    0.5, None, op0=ALU.mult),
                    reads=[final_t], writes=[f"w{col}"], tag=f"w_f{col}")

    def emit_combine(agout, xbase_sb, xbase_tok_fmt, dst_kind, ygroup, tag):
        for tt in range(2):
            A = sb["swz"][:, 0:1024]
            y0 = sb["ybuf"][:, :]
            y1 = sb["ybuf2"][:, :]
            S.add("pool", lambda e, col=tt: e.indirect_dma_start(
                out=sb["ybuf"][:, :], out_offset=None,
                in_=agout, in_offset=IOff(ap=sb["icomb"][:, col:col + 1], axis=0)),
                reads=["icomb", f"agdone{tag}"], writes=["ybg0"], kind="dma",
                ch=f"y{ygroup}0", tag=f"{tag}_yg{tt}0")
            S.add("pool", lambda e, col=2 + tt: e.indirect_dma_start(
                out=sb["ybuf2"][:, :], out_offset=None,
                in_=agout, in_offset=IOff(ap=sb["icomb"][:, col:col + 1], axis=0)),
                reads=["icomb", f"agdone{tag}"], writes=["ybg1"], kind="dma",
                ch=f"y{ygroup}0", tag=f"{tag}_yg{tt}1")
            xb = xbase_sb[:, tt * 1024:(tt + 1) * 1024]
            S.add("dve", lambda e, A=A, y0=y0, xb=xb, tt=tt: e.scalar_tensor_tensor(
                A, y0, sb["wcell"][:, 2 * tt:2 * tt + 1], xb, op0=ALU.mult, op1=ALU.add),
                reads=["ybg0", f"w{2 * tt}", xbase_tok_fmt.format(tt)],
                writes=["swzA"], tag=f"{tag}_c1_{tt}")
            if dst_kind == "x1":
                dst_ap = sb["x1own"][:, tt * 1024:(tt + 1) * 1024]
                dst_tok = f"x1own{tt}"
            else:
                dst_ap = sb["swz"][:, 2048:3072]
                dst_tok = "swzC"
            S.add("dve", lambda e, dst_ap=dst_ap, y1=y1, A=A, tt=tt: e.scalar_tensor_tensor(
                dst_ap, y1, sb["wcell"][:, 2 * tt + 1:2 * tt + 2], A,
                op0=ALU.mult, op1=ALU.add),
                reads=["ybg1", f"w{2 * tt + 1}", "swzA"], writes=[dst_tok],
                tag=f"{tag}_c2_{tt}")
            if dst_kind == "x1":
                dma("sp", "a2c", ag2_in.ap()[tt * 128:(tt + 1) * 128, :], dst_ap,
                    reads=[dst_tok], writes=[f"ag2in{tt}"], tag=f"{tag}_st{tt}")
            else:
                dma("sp", "ost", out_shard[3840 + tt * 128:3840 + (tt + 1) * 128, :], dst_ap,
                    reads=[dst_tok], writes=["outhbm"], tag=f"{tag}_st{tt}")

    S.add("pool", lambda e: e.engine_nop(), reads=[], writes=["agsrc1"], tag="src1")

    def ffn_h1(call, t):
        if call == 1:
            emit_ffn_head(1, t, dri["x_region"], "iffn1", f"f1t{t}")
        else:
            if t == 0:
                S.add("pool", lambda e: e.engine_nop(), reads=["agdoneC3"],
                      writes=["agsrc3"], tag="src3")
            emit_ffn_head(3, t, ag3_out.ap(), "iffn3", f"f3t{t}")

    def ffn_t2(call, t):
        if call == 1:
            emit_ffn_tail2(1, t, ("a1c", ag1_in.ap(), "ag1in"), f"f1t{t}")
            if t == padt - 1:
                S.add("pool", lambda e: e.collective_compute(
                    "AllGather", ALU.bypass, replica_groups=[list(range(NC))],
                    ins=[ag1_in.ap().opt()], outs=[ag1_out.ap().opt()]),
                    reads=[f"ag1in{tt}" for tt in range(padt)], writes=["agdoneC1"],
                    kind="coll", tag="AG1")
        else:
            emit_ffn_tail2(3, t, ("a4c", ag4_in.ap(), "ag4in"), f"f3t{t}")
            if t == padt - 1:
                S.add("pool", lambda e: e.collective_compute(
                    "AllGather", ALU.bypass, replica_groups=[list(range(NC))],
                    ins=[ag4_in.ap().opt()], outs=[ag4_out.ap().opt()]),
                    reads=[f"ag4in{tt}" for tt in range(padt)], writes=["agdoneC4"],
                    kind="coll", tag="AG4")

    def x_load(slot, ring, tag):
        dma("sp", f"x{ring}", sb["xr"][:, ring * 1024:(ring + 1) * 1024],
            dri["x_shard"][slot * 128:(slot + 1) * 128, :],
            reads=[], writes=[f"xr{ring}"], tag=tag)

    def kvA_front(slot, ring_x, tag):
        x_ap = sb["xr"][:, ring_x * 1024:(ring_x + 1) * 1024]
        emit_kv_front(x_ap, f"xr{ring_x}", slot % 2, tag, prestats=True)

    def kvA_back(slot, with_q, tag):
        pos = (slot + 1) % RING_KV
        kt_ap = sb["ktA"][:, pos * 128:(pos + 1) * 128]
        va_ap = sb["vaA"][:, pos * 130:pos * 130 + 130]
        q = "pair" if (with_q and slot % 2 == 1) else False
        emit_kv_back(slot % 2, kt_ap, f"ktA{pos}", va_ap, f"vaA{pos}",
                     1 + slot, q, (slot % 3), tag)

    def kvA(slot, ring_x, with_q, tag, x_ap=None, x_tok=None, prestats=False):
        pos = (slot + 1) % RING_KV
        kt_ap = sb["ktA"][:, pos * 128:(pos + 1) * 128]
        va_ap = sb["vaA"][:, pos * 130:pos * 130 + 130]
        if x_ap is None:
            x_ap = sb["xr"][:, ring_x * 1024:(ring_x + 1) * 1024]
            x_tok = f"xr{ring_x}"
        emit_kv(x_ap, x_tok, slot % 2, kt_ap, f"ktA{pos}", va_ap, f"vaA{pos}",
                1 + slot, bool(with_q), (slot % 3), tag, prestats=prestats)

    def ktva(slot):
        pos = (slot + 1) % RING_KV
        return ((sb["ktA"][:, pos * 128:(pos + 1) * 128], f"ktA{pos}"),
                (sb["vaA"][:, pos * 130:pos * 130 + 130], f"vaA{pos}"))

    def tileA(i, tag):
        kts, vas = [], []
        for c in range(3):
            kt, va = ktva(i - 1 + c)
            kts.append(kt)
            vas.append(va)
        x_ap = sb["xr"][:, (i % RING_X) * 1024:((i % RING_X) + 1) * 1024]
        emit_tile(i % 3, kts, vas, i, x_ap, f"xr{i % RING_X}", "out",
                  i * 128, tag)

    def tileA_sc(i, tag):
        kts = []
        for c in range(3):
            kt, va = ktva(i - 1 + c)
            kts.append(kt)
        emit_tile_sc(i % 3, kts, i, tag)

    def tile_kts(i):
        kts = []
        for c in range(3):
            kt, va = ktva(i - 1 + c)
            kts.append(kt)
        return kts

    def tile_vas(i):
        vas = []
        for c in range(3):
            kt, va = ktva(i - 1 + c)
            vas.append(va)
        return vas

    def tileA_pvmm(i, tag):
        x_ap = sb["xr"][:, (i % RING_X) * 1024:((i % RING_X) + 1) * 1024]
        S.add("dve", lambda e, x_ap=x_ap: e.tensor_copy(sb["xo"][:, :], x_ap),
              reads=[f"xr{i % RING_X}"], writes=["xo0"], tag=f"{tag}_xcp")
        emit_pv_group(0, tile_vas(i), tag)

    def tileA_tail(i, tag):
        vas = tile_vas(i)
        emit_norm_group(0, tag)
        emit_pv_group(1, vas, tag)
        emit_norm_group(1, tag)
        emit_wo_epi(sb["xo"][:, :], "xo0", "out", i * 128, tag)

    def statsA(slot):
        x_ap = sb["xr"][:, (slot % RING_X) * 1024:((slot % RING_X) + 1) * 1024]
        emit_ln_stats(x_ap, f"xr{slot % RING_X}", slot % 2, f"st{slot}")

    x_load(0, 0, "x0")
    statsA(0)
    x_load(1, 1, "x1")
    statsA(1)
    kvA_front(0, 0, "kv0")
    kvA_back(0, False, "kv0")
    x_load(2, 2, "x2")
    statsA(2)
    kvA_front(1, 1, "kv1")
    kvA_back(1, True, "kv1")
    kvA_front(2, 2, "kv2")
    kvA_back(2, True, "kv2")
    stats_done = 2

    after = {4: "comb1", 7: "segB", 10: "ffn3", 13: "comb3"}

    def ffn_block(call):
        ffn_h1(call, 0)
        emit_ffn_head2(call, 0, f"f{call}t0")
        for t in range(padt):
            if t + 1 < padt:
                ffn_h1(call, t + 1)
            emit_ffn_tail1(call, t, f"f{call}t{t}")
            if t + 1 < padt:
                emit_ffn_head2(call, t + 1, f"f{call}t{t + 1}")
            ffn_t2(call, t)

    ffn_block(1)

    kv_done = 2
    pending = None
    xlt_dirty = [True]
    load_done = 2

    def ensure_load(k):
        nonlocal load_done
        k = min(k, NCH_A - 1)
        while load_done < k:
            load_done += 1
            x_load(load_done, load_done % RING_X, f"x{load_done}")

    def refront(slot):
        x_ap = sb["xr"][:, (slot % RING_X) * 1024:((slot % RING_X) + 1) * 1024]
        x_load(slot, slot % RING_X, f"xr{slot}")
        emit_kv_front(x_ap, f"xr{slot % RING_X}", 2 + slot % 2, f"rf{slot}",
                      prestats=False)
    for i in range(1, NT_A):
        def advance_kv():
            nonlocal kv_done, stats_done
            k = kv_done + 1
            if stats_done < k:
                ensure_load(k)
                statsA(k)
                stats_done = k
            kvA_front(k, k % RING_X, f"kv{k}")
            if k % 2 == 1 and xlt_dirty[0]:
                refront(k - 1)
                xlt_dirty[0] = False
            kvA_back(k, k <= 29, f"kv{k}")
            kv_done = k
        while kv_done < i + 1:
            advance_kv()
        do_kv = (i != 7 and kv_done < min(i + 2, NCH_A - 1))
        if do_kv:
            k = i + 2
            if stats_done < k:
                ensure_load(k)
                statsA(k)
                stats_done = k
            kvA_front(k, k % RING_X, f"kv{k}")
            if k % 2 == 1 and xlt_dirty[0]:
                refront(k - 1)
                xlt_dirty[0] = False
        k3 = stats_done + 1
        if k3 == i + 3 and load_done >= k3 and k3 <= NCH_A - 1:
            statsA(k3)
            stats_done = k3
        kts_i = tile_kts(i)
        emit_sc_group(0, i % 3, kts_i, i, f"tA{i}")
        if pending is not None:
            tileA_tail(pending, f"tA{pending}")
            pending = None
        emit_sc_group(1, i % 3, kts_i, i, f"tA{i}")
        if do_kv:
            k = i + 2
            kvA_back(k, k <= 29, f"kv{k}")
            kv_done = k
        tileA_pvmm(i, f"tA{i}")
        pending = i
        if stats_done < min(kv_done + 1, NCH_A - 1):
            k2 = stats_done + 1
            ensure_load(k2)
            statsA(k2)
            stats_done = k2
        ensure_load(stats_done + 1)
        ph = after.get(i)
        if ph == "ffn3":
            if pending is not None:
                tileA_tail(pending, f"tA{pending}")
                pending = None
            ffn_block(3)
            xlt_dirty[0] = True
        elif ph == "comb1":
            emit_wcompute()
            S.add("pool", lambda e: e.engine_nop(), reads=["agdoneC1"],
                  writes=["agdonecomb1"], tag="agw1")
            for tt in range(2):
                dma("sp", "xoc", sb["x1own"][:, tt * 1024:(tt + 1) * 1024],
                    dri["x_own"][tt * 128:(tt + 1) * 128, :],
                    reads=[], writes=[f"x1own{tt}"], tag=f"xown{tt}")
            emit_combine(ag1_out.ap(), sb["x1own"], "x1own{}", "x1", 0, "comb1")
            dma("sp", "a2c", ag2_in.ap()[256:512, :], dri["ag2_extra"],
                reads=[], writes=["ag2inx"], tag="ag2extra")
            S.add("pool", lambda e: e.collective_compute(
                "AllGather", ALU.bypass, replica_groups=[list(range(NC))],
                ins=[ag2_in.ap().opt()], outs=[ag2_out.ap().opt()]),
                reads=["ag2in0", "ag2in1", "ag2inx"], writes=["agdoneC2"],
                kind="coll", tag="AG2")
        elif ph == "segB":
            S.add("pool", lambda e: e.memset(sb["xg"][:, :], 0.0),
                  reads=[], writes=["xg0"], tag="zb0")
            if pending is not None:
                tileA_tail(pending, f"tA{pending}")
                pending = None
            sring = i % 3
            S.add("pool", lambda e: e.indirect_dma_start(
                out=sb["xg"][:, :], out_offset=None, in_=ag2_out.ap(),
                in_offset=IOff(ap=sb["ib0"][:, 0:1], axis=0)),
                reads=["ib0", "agdoneC2", "xg0"], writes=["xg0"], kind="dma",
                ch="gb0", tag="gb0")
            S.add("pool", lambda e: e.indirect_dma_start(
                out=sb["xb3"][:, :], out_offset=None, in_=ag2_out.ap(),
                in_offset=IOff(ap=sb["ib3"][:, 0:1], axis=0)),
                reads=["ib3", "agdoneC2", "xb3"], writes=["xb3"], kind="dma",
                ch="gb3", tag="gb3")
            bx = [(sb["xg"][:, :], "xg0"),
                  (sb["x1own"][:, 0:1024], "x1own0"),
                  (sb["x1own"][:, 1024:2048], "x1own1"),
                  (sb["xb3"][:, :], "xb3")]
            for j in range(4):
                kt_ap = sb["ktB"][:, j * 128:(j + 1) * 128]
                va_ap = sb["vaB"][:, j * 130:j * 130 + 130]
                emit_kv(bx[j][0], bx[j][1], 2 + j % 2, kt_ap, f"ktB{j}", va_ap, f"vaB{j}",
                        32 + j, j == 1, sring, f"kvB{j}")
            for bt in range(2):
                if bt == 1:
                    emit_kv(bx[2][0], bx[2][1], 2, None, None, None, None,
                            34, True, sring, "kvB2q", with_kv=False)
                kts = [(sb["ktB"][:, (bt + c) * 128:(bt + c + 1) * 128], f"ktB{bt + c}")
                       for c in range(3)]
                vas = [(sb["vaB"][:, (bt + c) * 130:(bt + c) * 130 + 130], f"vaB{bt + c}")
                       for c in range(3)]
                emit_tile(sring, kts, vas, 30 + bt,
                          sb["x1own"][:, bt * 1024:(bt + 1) * 1024], f"x1own{bt}",
                          "x2", bt, f"tB{bt}")
            S.add("pool", lambda e: e.collective_compute(
                "AllGather", ALU.bypass, replica_groups=[list(range(NC))],
                ins=[ag3_in.ap().opt()], outs=[ag3_out.ap().opt()]),
                reads=["ag3in0", "ag3in1"], writes=["agdoneC3"],
                kind="coll", tag="AG3")
            xlt_dirty[0] = True
        elif ph == "comb3":
            S.add("pool", lambda e: e.engine_nop(), reads=["agdoneC4"],
                  writes=["agdonecomb3"], tag="agw4")
            emit_combine(ag4_out.ap(), sb["x2own"], "x2own{}", "out", 1, "comb3")

    if pending is not None:
        tileA_tail(pending, f"tA{pending}")
        pending = None

    S.add("pool", lambda e: e.indirect_dma_start(
        out=sb["xg"][:, :], out_offset=None, in_=ag2_out.ap(),
        in_offset=IOff(ap=sb["ia0"][:, 0:1], axis=0)),
        reads=["ia0", "agdoneC2", "xg0"], writes=["xg0"], kind="dma", ch="ga0", tag="ga0")
    emit_kv(sb["xg"][:, :], "xg0", 2, sb["ktA"][:, 0:128], "ktA0",
            sb["vaA"][:, 0:130], "vaA0", 0, False, 0, "kvm1")
    x_load(0, 0, "x0r")
    emit_kv(sb["xr"][:, 0:1024], "xr0", 3, sb["ktA"][:, 128:256], "ktA1",
            sb["vaA"][:, 130:260], "vaA1", 1, True, 0, "kv0r")
    x_load(1, 1, "x1r")
    emit_kv(sb["xr"][:, 1024:2048], "xr1", 2, sb["ktA"][:, 256:384], "ktA2",
            sb["vaA"][:, 260:390], "vaA2", 2, False, 0, "kv1r")
    tileA(0, "tA0")

    S.add("sp", None, reads=[], writes=[], tag="fence")

    with es:
        with nc.Block() as block:
            eng_sem, ch_sem, coll_sems, emit_stream = _materialize(nc, S, es)

            @block.tensor
            def _(t):
                emit_stream(t, "pe")

            @block.scalar
            def _(s):
                emit_stream(s, "act")

            @block.vector
            def _(v):
                emit_stream(v, "dve")

            @block.gpsimd
            def _(g):
                emit_stream(g, "pool")

            @block.sync
            def _(sy):
                emit_stream(sy, "sp")

    return nc



def kernel(**inputs):
    in_maps, padt, base_ids = _prep(inputs)
    nc = build_program(padt)
    res = run_bass_kernel_spmd(nc, in_maps, core_ids=list(range(NC)))
    out = np.empty((TOK, D), dtype=np.float32)
    for c in range(NC):
        shard = res.results[c]["out_shard"]
        base = base_ids[c]
        for i in range(NT_A):
            a, b = _chunk_rows(base + i)
            out[a:b] = shard[i * 128:(i + 1) * 128]
        out[256 * c:256 * c + 256] = shard[3840:4096]
    return out.reshape(BATCH, SEQ, D)


if __name__ == "__main__":
    import sys
    if "--build" in sys.argv:
        nc = build_program(5)
        n = sum(1 for _ in nc.all_instructions())
        print(f"build OK, {n} instructions")


# version 34
# speedup vs baseline: 1.0184x; 1.0008x over previous
import numpy as np
from contextlib import ExitStack

import bass_rust
import concourse.bass as bass
import concourse.mybir as mybir
from concourse.bass_utils import run_bass_kernel_spmd

F32 = mybir.dt.float32
BF16 = mybir.dt.bfloat16
I32 = mybir.dt.int32
AF = mybir.ActivationFunctionType
ALU = mybir.AluOpType
AX = mybir.AxisListType
IOff = bass_rust.IndirectOffsetOnAxis

SAME_ENGINE_SYNC = True

NC = 8
D = 1024
HD = 64
NH = 16
NKV = 2
SEQ = 16384
BATCH = 2
TOK = BATCH * SEQ
REG = 2048
NT_A = 30
NCH_A = 31
RING_KV = 5
RING_X = 4
EXPB = -30.0
EPS = 1e-5
ADDEND = float(np.exp(np.float64(EXPB)))
AG2_ROWS = 512
AG3_ROWS = 256



def _chunk_rows(cid):
    if cid < 128:
        s = cid * 128
    else:
        s = SEQ + (cid - 128) * 128
    return s, s + 128


def _prep(inputs):
    x = np.ascontiguousarray(np.asarray(inputs["L0_attn_x"], dtype=np.float32).reshape(TOK, D))
    ridx = np.asarray(inputs["L0_ffn_router_indices"], dtype=np.int32)
    rw = np.asarray(inputs["L0_ffn_routing_weights"], dtype=np.float32)

    slots = [[] for _ in range(NC)]
    for t in range(REG):
        for k in range(2):
            slots[int(ridx[t, k])].append((t, k))
    counts = [len(s) for s in slots]
    padt = max(1, int(np.ceil(max(counts) / 128.0)))
    pad = padt * 128
    slot_row = {}
    for e in range(NC):
        for i, (t, k) in enumerate(slots[e]):
            slot_row[(t, k)] = e * pad + i

    base_ids = [16 + NT_A * c for c in range(NC)]

    per_core = []
    for c in range(NC):
        base = base_ids[c]
        blocks = []
        for s in range(NCH_A):
            cid = base + s
            if cid <= 255:
                a, b = _chunk_rows(cid)
                blocks.append(x[a:b])
            else:
                blocks.append(np.zeros((128, D), np.float32))
        x_shard = np.ascontiguousarray(np.concatenate(blocks, axis=0))

        x_own = np.ascontiguousarray(x[256 * c:256 * c + 256])

        ha, hb = _chunk_rows(base - 1)
        halo_l = x[ha:hb]
        special = np.zeros((128, D), np.float32)
        if c == 7:
            special = x[2048:2176]
        ag2_extra = np.ascontiguousarray(np.concatenate([halo_l, special], axis=0))

        p = np.arange(128, dtype=np.int32)
        if c == 0:
            a0_idx = (7 * AG2_ROWS + 128 + p).astype(np.int32)
        else:
            a0_idx = (c * AG2_ROWS + 256 + p).astype(np.int32)
        if c == 0:
            b0_idx = (0 * AG2_ROWS + 384 + p).astype(np.int32)
        else:
            tpos = 256 * c - 128 + p
            b0_idx = ((tpos // 256) * AG2_ROWS + tpos % 256).astype(np.int32)
        if c < 7:
            tpos = 256 * c + 256 + p
            b3_idx = ((tpos // 256) * AG2_ROWS + tpos % 256).astype(np.int32)
        else:
            b3_idx = (7 * AG2_ROWS + 384 + p).astype(np.int32)

        ffn1 = np.zeros((128, padt), np.int32)
        ffn3 = np.zeros((128, padt), np.int32)
        for i, (t, k) in enumerate(slots[c]):
            ffn1[i % 128, i // 128] = t
            ffn3[i % 128, i // 128] = (t // 256) * AG3_ROWS + t % 256
        comb = np.zeros((128, 4), np.int32)
        for tt in range(2):
            for k in range(2):
                for pp in range(128):
                    t = 256 * c + 128 * tt + pp
                    comb[pp, 2 * k + tt] = slot_row[(t, k)]

        meta_v = np.ones((128, 3 * 32), np.float32)
        for i in range(NT_A):
            own = base + i
            for cc in range(3):
                kid = own + cc - 1
                ok = (0 <= kid <= 255) and ((kid < 128) == (own < 128))
                if not ok:
                    meta_v[:, 3 * i + cc] = 0.0
        for bt in range(2):
            for cc in range(3):
                j = bt + cc
                pos0 = 256 * c + (j - 1) * 128
                ok = (0 <= pos0) and (pos0 + 128 <= 2176)
                if not ok:
                    meta_v[:, 3 * (30 + bt) + cc] = 0.0
        vev = np.ones((128, 36), np.float32)
        if base + 30 > 255:
            vev[:, 31] = 0.0
        if c == 0:
            vev[:, 32] = 0.0

        rw_own = rw[256 * c:256 * c + 256].reshape(2, 128, 8).transpose(1, 0, 2).reshape(128, 16)
        idx_own = ridx[256 * c:256 * c + 256].reshape(2, 128, 2).transpose(1, 0, 2).reshape(128, 4)

        per_core.append(dict(
            x_shard=x_shard, x_own=x_own, ag2_extra=ag2_extra,
            a0_idx=np.ascontiguousarray(a0_idx.reshape(128, 1)),
            b0_idx=np.ascontiguousarray(b0_idx.reshape(128, 1)),
            b3_idx=np.ascontiguousarray(b3_idx.reshape(128, 1)),
            ffn_idx1=np.ascontiguousarray(ffn1), ffn_idx3=np.ascontiguousarray(ffn3),
            comb_idx=np.ascontiguousarray(comb),
            meta_v=meta_v, vev=vev,
            rw_own=np.ascontiguousarray(rw_own),
            idx_own=np.ascontiguousarray(idx_own.astype(np.int32)),
        ))

    x_region = np.ascontiguousarray(x[0:REG])

    perm = np.concatenate([np.arange(0, 2048, 2), np.arange(1, 2048, 2)])
    gu = np.asarray(inputs["L0_ffn_gate_up_proj"], dtype=np.float32)
    gub = np.asarray(inputs["L0_ffn_gate_up_proj_bias"], dtype=np.float32)
    dn = np.asarray(inputs["L0_ffn_down_proj"], dtype=np.float32)
    dnb = np.asarray(inputs["L0_ffn_down_proj_bias"], dtype=np.float32)
    consts = np.zeros((128, 3), np.float32)
    consts[:, 0] = EXPB
    consts[:, 1] = EPS
    consts[:, 2] = 1.0
    band02 = np.zeros((128, 2048), np.float32)
    w_ = np.arange(128)[:, None]
    i_ = np.arange(128)[None, :]
    b0m = (w_ >= i_).astype(np.float32)
    b2m = (w_ < i_).astype(np.float32)
    for h in range(8):
        band02[:, 128 * h:128 * (h + 1)] = b0m
        band02[:, 1024 + 128 * h:1024 + 128 * (h + 1)] = b2m
    ident = np.eye(128, dtype=np.float32)
    ones_row = np.ones((1, 256), np.float32)

    qmap = np.zeros(D, np.int64)
    for h in range(16):
        g, hw = h // 8, h % 8
        qmap[hw * 128 + 64 * g + np.arange(64)] = h * 64 + np.arange(64)
    wq_p = np.ascontiguousarray(np.asarray(inputs["L0_attn_Wq"], np.float32)[:, qmap])
    bq_p = np.ascontiguousarray(np.asarray(inputs["L0_attn_bq"], np.float32)[qmap].reshape(1, D))
    womap = np.zeros(D, np.int64)
    for j in range(8):
        g, hw4 = j // 4, j % 4
        for half in range(2):
            h = 8 * g + 4 * half + hw4
            womap[j * 128 + half * 64 + np.arange(64)] = h * 64 + np.arange(64)
    wo_p = np.ascontiguousarray(np.asarray(inputs["L0_attn_Wo"], np.float32)[womap, :])

    in_maps = []
    for c in range(NC):
        pc = per_core[c]
        m = dict(
            x_shard=pc["x_shard"], x_own=pc["x_own"], x_region=x_region,
            ag2_extra=pc["ag2_extra"],
            a0_idx=pc["a0_idx"], b0_idx=pc["b0_idx"], b3_idx=pc["b3_idx"],
            ffn_idx1=pc["ffn_idx1"], ffn_idx3=pc["ffn_idx3"], comb_idx=pc["comb_idx"],
            meta_v=pc["meta_v"], vev=pc["vev"],
            rw_own=pc["rw_own"], idx_own=pc["idx_own"],
            consts=consts, band02=band02, ident=ident, ones_row=ones_row,
            wq=wq_p, wk=np.ascontiguousarray(np.asarray(inputs["L0_attn_Wk"], np.float32)),
            wv=np.ascontiguousarray(np.asarray(inputs["L0_attn_Wv"], np.float32)),
            wo=wo_p,
            bq=bq_p,
            bk=np.ascontiguousarray(np.asarray(inputs["L0_attn_bk"], np.float32).reshape(1, 128)),
            bv=np.ascontiguousarray(np.asarray(inputs["L0_attn_bv"], np.float32).reshape(1, 128)),
            bo=np.ascontiguousarray(np.asarray(inputs["L0_attn_bo"], np.float32).reshape(1, D)),
            gu_e=np.ascontiguousarray(gu[c][:, perm]),
            gub_e=np.ascontiguousarray(gub[c][perm].reshape(1, 2048)),
            d_e=np.ascontiguousarray(dn[c]),
            db_e=np.ascontiguousarray(dnb[c].reshape(1, D)),
        )
        in_maps.append(m)
    return in_maps, padt, base_ids



ENGINES = ("pe", "act", "dve", "pool", "sp")


class Sched:
    def __init__(self):
        self.ops = []
        self.lw = {}
        self.readers = {}
        self.channels = {}

    def add(self, engine, fn, reads=(), writes=(), kind="op", ch=None, tag=""):
        i = len(self.ops)
        raw = set()
        for t in reads:
            if t in self.lw:
                raw.add(self.lw[t])
        for t in writes:
            if t in self.lw:
                raw.add(self.lw[t])
            raw |= self.readers.get(t, set())
        raw.discard(i)
        deps = set()
        dma_waits = {}
        for d in raw:
            dop = self.ops[d]
            if dop["kind"] == "dma":
                dma_waits[dop["ch"]] = self.channels[dop["ch"]]
            elif (not SAME_ENGINE_SYNC and kind == "op" and dop["kind"] == "op"
                  and dop["engine"] == engine):
                continue
            else:
                deps.add(d)
        seq = None
        if kind == "dma":
            assert ch is not None
            self.channels[ch] = self.channels.get(ch, 0) + 1
            seq = self.channels[ch]
        self.ops.append(dict(engine=engine, fn=fn, deps=deps, dma_waits=dma_waits,
                             kind=kind, ch=ch, seq=seq, tag=tag))
        for t in reads:
            self.readers.setdefault(t, set()).add(i)
        for t in writes:
            self.lw[t] = i
            self.readers[t] = set()
        return i


def _materialize(nc, sched, es):
    ops = sched.ops
    targets = set()
    for op in ops:
        targets |= op["deps"]
    eng_count = {e: 0 for e in ENGINES}
    milestone = {}
    for i, op in enumerate(ops):
        if op["kind"] == "op" and i in targets:
            eng_count[op["engine"]] += 1
            milestone[i] = eng_count[op["engine"]]
    eng_sem = {e: es.enter_context(nc.semaphore(f"se_{e}")) for e in ENGINES}
    ch_sem = {c: es.enter_context(nc.semaphore(f"sd_{c}")) for c in sched.channels}
    n_coll = sum(1 for op in ops if op["kind"] == "coll")
    coll_sems = [es.enter_context(nc.semaphore(f"scc_{i}")) for i in range(n_coll)]
    coll_idx = {}
    k = 0
    for i, op in enumerate(ops):
        if op["kind"] == "coll":
            coll_idx[i] = k
            k += 1

    def wait_spec(i):
        op = ops[i]
        if op["kind"] == "op":
            return (eng_sem[op["engine"]], milestone.get(i))
        if op["kind"] == "coll":
            return (coll_sems[coll_idx[i]], 1)
        return None

    streams = {e: [] for e in ENGINES}
    for i, op in enumerate(ops):
        streams[op["engine"]].append(i)

    waited = {e: {} for e in ENGINES}

    def emit_stream(eng_obj, eng_name):
        wmap = waited[eng_name]
        for i in streams[eng_name]:
            op = ops[i]
            specs = []
            for d in sorted(op["deps"]):
                ws = wait_spec(d)
                if ws is not None and ws[1] is not None:
                    specs.append(ws)
            for ch, cnt in sorted(op["dma_waits"].items()):
                specs.append((ch_sem[ch], 16 * cnt))
            for sem, val in specs:
                key = id(sem)
                if wmap.get(key, -1) >= val:
                    continue
                eng_obj.wait_ge(sem, val)
                wmap[key] = val
            if op["fn"] is None:
                continue
            instr = op["fn"](eng_obj)
            if op["kind"] == "dma":
                instr.then_inc(ch_sem[op["ch"]], 16)
            elif op["kind"] == "coll":
                instr.then_inc(coll_sems[coll_idx[i]])
            elif i in milestone:
                instr.then_inc(eng_sem[op["engine"]], 1)
        if eng_name == "sp":
            for c, n in sorted(sched.channels.items()):
                sem = ch_sem[c]
                if wmap.get(id(sem), -1) < 16 * n:
                    eng_obj.wait_ge(sem, 16 * n)
            for s in coll_sems:
                eng_obj.wait_ge(s, 1)
            for e in ENGINES:
                if eng_count[e]:
                    eng_obj.wait_ge(eng_sem[e], eng_count[e])

    return eng_sem, ch_sem, coll_sems, emit_stream



def build_program(padt, debug=False):
    nc = bass.Bass("TRN2", target_bir_lowering=False, debug=debug)
    pad = padt * 128

    dri = {}
    for name, shape, dt in [
        ("x_shard", [NCH_A * 128, D], F32), ("x_own", [256, D], F32),
        ("x_region", [REG, D], F32), ("ag2_extra", [256, D], F32),
        ("a0_idx", [128, 1], I32), ("b0_idx", [128, 1], I32), ("b3_idx", [128, 1], I32),
        ("ffn_idx1", [128, padt], I32), ("ffn_idx3", [128, padt], I32),
        ("comb_idx", [128, 4], I32),
        ("meta_v", [128, 96], F32), ("vev", [128, 36], F32),
        ("rw_own", [128, 16], F32), ("idx_own", [128, 4], I32),
        ("consts", [128, 3], F32), ("band02", [128, 2048], F32),
        ("ident", [128, 128], F32), ("ones_row", [1, 256], F32),
        ("wq", [D, D], F32), ("wk", [D, 128], F32), ("wv", [D, 128], F32), ("wo", [D, D], F32),
        ("bq", [1, D], F32), ("bk", [1, 128], F32), ("bv", [1, 128], F32), ("bo", [1, D], F32),
        ("gu_e", [D, 2048], F32), ("gub_e", [1, 2048], F32),
        ("d_e", [D, D], F32), ("db_e", [1, D], F32),
    ]:
        dri[name] = nc.dram_tensor(name, shape, dt, kind="ExternalInput").ap()

    out_shard = nc.dram_tensor("out_shard", [4096, D], F32, kind="ExternalOutput").ap()

    ag1_in = nc.dram_tensor("ag1_in", [pad, D], BF16)
    ag1_out = nc.dram_tensor("ag1_out", [NC * pad, D], BF16, addr_space="Shared")
    ag2_in = nc.dram_tensor("ag2_in", [AG2_ROWS, D], F32)
    ag2_out = nc.dram_tensor("ag2_out", [NC * AG2_ROWS, D], F32, addr_space="Shared")
    ag3_in = nc.dram_tensor("ag3_in", [AG3_ROWS, D], F32)
    ag3_out = nc.dram_tensor("ag3_out", [NC * AG3_ROWS, D], F32, addr_space="Shared")
    ag4_in = nc.dram_tensor("ag4_in", [pad, D], BF16)
    ag4_out = nc.dram_tensor("ag4_out", [NC * pad, D], BF16, addr_space="Shared")

    es = ExitStack()
    E = es.enter_context

    sb = {}

    def SB(name, shape, dt):
        sb[name] = E(nc.sbuf_tensor("sb_" + name, shape, dt))
        return sb[name]

    SB("wq", [128, 8 * 1024], BF16)
    SB("wk", [128, 8 * 128], BF16)
    SB("wv", [128, 8 * 128], BF16)
    SB("wo", [128, 8 * 1024], BF16)
    SB("gu", [128, 8 * 2048], BF16)
    SB("dw", [128, 8 * 1024], BF16)
    SB("bq", [1, D], BF16)
    SB("bk", [1, 128], BF16)
    SB("bv", [1, 128], BF16)
    SB("bo", [1, D], BF16)
    SB("gub", [1, 2048], BF16)
    SB("db", [1, D], BF16)
    SB("ones", [1, 256], BF16)
    SB("ident", [128, 128], BF16)
    SB("band02", [128, 2048], BF16)
    SB("consts", [128, 3], F32)
    SB("meta_v", [128, 96], F32)
    SB("vev", [128, 36], F32)
    SB("ktA", [128, RING_KV * 128], BF16)
    SB("vaA", [128, RING_KV * 130], BF16)
    SB("ktB", [128, 4 * 128], BF16)
    SB("vaB", [128, 4 * 130], BF16)
    SB("xr", [128, RING_X * 1024], F32)
    SB("xb3", [128, 1024], F32)
    SB("xln", [128, 2 * 1024], BF16)
    SB("xlt", [128, 2 * 1024], BF16)
    SB("qt", [128, 3 * 1024], BF16)
    SB("pt", [128, 6 * 1024], BF16)
    SB("attnT", [128, 1024], BF16)
    SB("rsb", [64, 1024], BF16)
    SB("rcp", [1, 1024], BF16)
    SB("lnr", [1, 1024], F32)
    SB("bnt", [128, 48], F32)
    SB("xo", [128, 1024], F32)
    SB("x1own", [128, 2 * 1024], F32)
    SB("x2own", [128, 2 * 1024], F32)
    SB("ybuf", [128, 1024], BF16)
    SB("ybuf2", [128, 1024], BF16)
    SB("wcell", [128, 16], F32)
    SB("lnc", [128, 32], F32)
    SB("xg", [128, 1024], F32)
    SB("swz", [128, 3 * 1024], F32)
    SB("inter", [128, 1024], BF16)
    SB("itT", [128, 1024], BF16)
    SB("iffn1", [128, padt], I32)
    SB("iffn3", [128, padt], I32)
    SB("icomb", [128, 4], I32)
    SB("ia0", [128, 1], I32)
    SB("ib0", [128, 1], I32)
    SB("ib3", [128, 1], I32)
    SB("rw", [128, 16], F32)
    SB("idxo", [128, 4], I32)
    SB("idxf", [128, 4], F32)

    ps = {}
    ps["pAB"] = E(nc.psum_tensor("pAB", [128, 1024], F32))
    ps["pT"] = E(nc.psum_tensor("pT", [128, 1024], F32))
    ps["s0"] = E(nc.psum_tensor("s0", [128, 1024], F32))
    ps["s1"] = E(nc.psum_tensor("s1", [128, 1024], F32))

    S = Sched()

    def dma(engine, ch, out_ap, in_ap, reads, writes, tag=""):
        S.add(engine, lambda e, o=out_ap, i=in_ap: e.dma_start(out=o, in_=i),
              reads=reads, writes=writes, kind="dma", ch=ch, tag=tag)

    for name in ("vaA", "vaB"):
        S.add("pool", lambda e, t=sb[name]: e.memset(t[:, :], 1.0),
              writes=[name], tag="memset")
    S.add("pool", lambda e: e.memset(sb["xb3"][:, :], 0.0), writes=["xb3"])

    small = [
        ("consts", "consts"), ("meta_v", "meta_v"), ("vev", "vev"),
        ("band02", "band02"), ("ident", "ident"), ("ones", "ones_row"),
        ("bq", "bq"), ("bk", "bk"), ("bv", "bv"), ("bo", "bo"),
        ("gub", "gub_e"), ("db", "db_e"),
        ("iffn1", "ffn_idx1"), ("iffn3", "ffn_idx3"), ("icomb", "comb_idx"),
        ("ia0", "a0_idx"), ("ib0", "b0_idx"), ("ib3", "b3_idx"),
        ("rw", "rw_own"), ("idxo", "idx_own"),
    ]
    for sname, dname in small:
        dma("pool", "ci", sb[sname][:, :], dri[dname], reads=[], writes=[sname], tag=f"ld_{sname}")

    def wload(ch, sname, dname, n):
        out_ap = sb[sname][:, :].rearrange("p (dc n) -> p dc n", dc=8)
        in_ap = dri[dname].rearrange("(dc p) n -> p dc n", p=128)
        dma("pool", ch, out_ap, in_ap, reads=[], writes=[sname], tag=f"ld_{sname}")
    wload("cwf", "gu", "gu_e", 2048)
    wload("cwf", "dw", "d_e", 1024)
    wload("cwa", "wq", "wq", 1024)
    wload("cwa", "wk", "wk", 128)
    wload("cwa", "wv", "wv", 128)
    wload("cwa", "wo", "wo", 1024)

    def emit_ln_stats(src_ap, src_tok, ring, tag):
        cells = sb["lnc"]
        c0 = 8 * ring
        ct = f"lnc{ring}"
        bt = sb["bnt"][:, ring * 12:ring * 12 + 12]
        btok = f"bnt{ring}"
        S.add("dve", lambda e: e.bn_stats(bt[:, 0:6], src_ap[:, 0:512]),
              reads=[src_tok], writes=[btok, ct + "u"], tag=f"{tag}_bs0")
        S.add("dve", lambda e: e.bn_stats(bt[:, 6:12], src_ap[:, 512:1024]),
              reads=[src_tok, btok], writes=[btok], tag=f"{tag}_bs1")
        S.add("dve", lambda e: e.bn_aggr(cells[:, c0 + 2:c0 + 4], bt[:, 0:12]),
              reads=[btok], writes=[ct + "m", ct + "v"], tag=f"{tag}_agg")
        S.add("act", lambda e: e.activation(cells[:, c0 + 5:c0 + 6], cells[:, c0 + 3:c0 + 4],
                                            AF.Ln, bias=sb["consts"][:, 1:2], scale=1.0),
              reads=[ct + "v", "consts"], writes=[ct + "l"], tag=f"{tag}_ln")
        S.add("act", lambda e: e.activation(cells[:, c0 + 6:c0 + 7], cells[:, c0 + 5:c0 + 6],
                                            AF.Exp, scale=-0.5),
              reads=[ct + "l"], writes=[ct + "r"], tag=f"{tag}_rstd")
        S.add("dve", lambda e: e.scalar_tensor_tensor(cells[:, c0 + 7:c0 + 8], cells[:, c0 + 2:c0 + 3],
                                                      -1.0, cells[:, c0 + 6:c0 + 7],
                                                      op0=ALU.mult, op1=ALU.mult),
              reads=[ct + "m", ct + "r"], writes=[ct + "n"], tag=f"{tag}_nb")
    def emit_ln_apply(src_ap, src_tok, out_ap, out_tok, ring, tag):
        cells = sb["lnc"]
        c0 = 8 * ring
        ct = f"lnc{ring}"
        S.add("dve", lambda e: e.tensor_scalar(
            out_ap, src_ap, cells[:, c0 + 6:c0 + 7], cells[:, c0 + 7:c0 + 8],
            op0=ALU.mult, op1=ALU.add),
            reads=[src_tok, ct + "r", ct + "n"], writes=[out_tok, ct + "u"], tag=f"{tag}_app")

    def emit_ln(src_ap, src_tok, out_ap, out_tok, ring, tag):
        emit_ln_stats(src_ap, src_tok, ring, tag)
        emit_ln_apply(src_ap, src_tok, out_ap, out_tok, ring, tag)

    def emit_tr(src, src_tok, dst, dst_tok, tag):
        def f(e):
            ins = None
            for dc in range(8):
                bank_half = dc * 128
                ins = e.matmul(ps["pT"][:, bank_half:bank_half + 128],
                               src[:, dc * 128:(dc + 1) * 128], sb["ident"][:, :],
                               start=True, stop=True)
            return ins
        S.add("pe", f, reads=[src_tok, "ident"], writes=["pT"], tag=f"{tag}_tr")
        S.add("dve", lambda e: e.tensor_copy(dst, ps["pT"][:, :]),
              reads=["pT"], writes=[dst_tok], tag=f"{tag}_trev")

    def emit_kv_front(xsrc_ap, xsrc_tok, ln_ring, tag, prestats=False):
        bln = ln_ring % 2
        xln_ap = sb["xln"][:, bln * 1024:(bln + 1) * 1024]
        xlt_ap = sb["xlt"][:, bln * 1024:(bln + 1) * 1024]
        if not prestats:
            emit_ln_stats(xsrc_ap, xsrc_tok, ln_ring, tag + "_ln")
        emit_ln_apply(xsrc_ap, xsrc_tok, xln_ap, f"xln{bln}", ln_ring, tag + "_ln")
        emit_tr(xln_ap, f"xln{bln}", xlt_ap, f"xlt{bln}", tag)

    def emit_kv_back(ln_ring, kt_ap, kt_tok, va_ap, va_tok, vev_col,
                     with_q, qt_ring, tag, with_kv=True):
        bln = ln_ring % 2
        xlt_ap = sb["xlt"][:, bln * 1024:(bln + 1) * 1024]
        xlt_tok = f"xlt{bln}"

        if with_q == "single":
            def fq(e):
                ins = None
                for half in range(2):
                    for qcl in range(4):
                        qc = half * 4 + qcl
                        col = half * 512 + qcl * 128
                        ins = e.matmul(ps["pAB"][:, col:col + 128],
                                       sb["bq"][0:1, qc * 128:(qc + 1) * 128],
                                       sb["ones"][0:1, 0:128], start=True, stop=False)
                        for dc in range(8):
                            ins = e.matmul(ps["pAB"][:, col:col + 128],
                                           sb["wq"][:, dc * 1024 + qc * 128:dc * 1024 + (qc + 1) * 128],
                                           xlt_ap[:, dc * 128:(dc + 1) * 128],
                                           start=False, stop=(dc == 7))
                return ins
            S.add("pe", fq, reads=[xlt_tok, "wq", "bq", "ones"], writes=["pA", "pB"],
                  tag=f"{tag}_qt")
            qdst = sb["qt"][:, qt_ring * 1024: (qt_ring + 1) * 1024]
            S.add("dve", lambda e, d=qdst: e.tensor_copy(d, ps["pAB"][:, :]),
                  reads=["pA", "pB"], writes=[f"qt{qt_ring}_0", f"qt{qt_ring}_1"],
                  tag=f"{tag}_qtev")
        elif with_q == "pair":
            xlt2 = sb["xlt"][:, :].rearrange("p (b n) -> p b n", b=2)
            qr0, qr1 = (qt_ring + 2) % 3, qt_ring
            for half in range(2):
                def fqp(e, half=half):
                    ins = None
                    for qcl in range(4):
                        qc = half * 4 + qcl
                        col = qcl * 256
                        ins = e.matmul(ps["pAB"][:, col:col + 256],
                                       sb["bq"][0:1, qc * 128:(qc + 1) * 128],
                                       sb["ones"][0:1, 0:256], start=True, stop=False)
                        for dc in range(8):
                            ins = e.matmul(ps["pAB"][:, col:col + 256],
                                           sb["wq"][:, dc * 1024 + qc * 128:dc * 1024 + (qc + 1) * 128],
                                           xlt2[:, :, dc * 128:(dc + 1) * 128],
                                           start=False, stop=(dc == 7))
                    return ins
                S.add("pe", fqp, reads=["xlt0", "xlt1", "wq", "bq", "ones"],
                      writes=["pA", "pB"], tag=f"{tag}_qtp{half}")
                psrc = ps["pAB"][:, :].rearrange("p (q b n) -> p q b n", q=4, b=2)
                for bi, qr in ((0, qr0), (1, qr1)):
                    qdst = sb["qt"][:, qr * 1024 + half * 512: qr * 1024 + (half + 1) * 512]
                    S.add("dve", lambda e, d=qdst, s=psrc, bi=bi: e.tensor_copy(
                        d.rearrange("p (q n) -> p q n", q=4), s[:, :, bi, :]),
                        reads=["pA", "pB"], writes=[f"qt{qr}_{half}"],
                        tag=f"{tag}_qtpev{half}{bi}")

        if not with_kv:
            return

        def fk(e):
            ins = e.matmul(ps["pAB"][:, 0:128], sb["bk"][0:1, :], sb["ones"][0:1, 0:128],
                           start=True, stop=False)
            for dc in range(8):
                ins = e.matmul(ps["pAB"][:, 0:128],
                               sb["wk"][:, dc * 128:(dc + 1) * 128],
                               xlt_ap[:, dc * 128:(dc + 1) * 128],
                               start=False, stop=(dc == 7))
            return ins
        S.add("pe", fk, reads=[xlt_tok, "wk", "bk", "ones"], writes=["pA"], tag=f"{tag}_kt")
        S.add("dve", lambda e: e.tensor_copy(kt_ap, ps["pAB"][:, 0:128]),
              reads=["pA"], writes=[kt_tok], tag=f"{tag}_ktev")

        def fv(e):
            ins = e.matmul(ps["pAB"][:, 512:640], sb["ones"][0:1, 0:128], sb["bv"][0:1, :],
                           start=True, stop=False)
            for dc in range(8):
                ins = e.matmul(ps["pAB"][:, 512:640],
                               xlt_ap[:, dc * 128:(dc + 1) * 128],
                               sb["wv"][:, dc * 128:(dc + 1) * 128],
                               start=False, stop=(dc == 7))
            return ins
        S.add("pe", fv, reads=[xlt_tok, "wv", "bv", "ones"], writes=["pB"], tag=f"{tag}_v")
        for g in range(2):
            S.add("dve", lambda e, g=g: e.tensor_scalar(
                va_ap[:, g * 65:g * 65 + 64], ps["pAB"][:, 512 + g * 64:512 + (g + 1) * 64],
                sb["vev"][:, vev_col:vev_col + 1], None, op0=ALU.mult),
                reads=["pB", "vev"], writes=[va_tok], tag=f"{tag}_vev{g}")

    def emit_kv(xsrc_ap, xsrc_tok, ln_ring, kt_ap, kt_tok, va_ap, va_tok, vev_col,
                with_q, qt_ring, tag, with_kv=True, prestats=False):
        emit_kv_front(xsrc_ap, xsrc_tok, ln_ring, tag, prestats=prestats)
        emit_kv_back(ln_ring, kt_ap, kt_tok, va_ap, va_tok, vev_col,
                     "single" if with_q else False, qt_ring, tag, with_kv=with_kv)

    BLKMAP = {0: 0, 2: 1, 1: 2}

    def emit_sc_one(g, c, qt_ring, kt_list, meta_tile, tag):
        stn = {0: "s0", 2: "s1", 1: "s0"}
        for c in (c,):
                blk = 3 * g + BLKMAP[c]
                pn = stn[c]
                kt_ap, kt_tok = kt_list[c]

                def fs(e, g=g, c=c, pn=pn, kt_ap=kt_ap):
                    ins = None
                    for half in range(2):
                        rhs = sb["qt"][64 * g:64 * g + 64,
                                       qt_ring * 1024 + half * 512:qt_ring * 1024 + (half + 1) * 512]
                        ins = e.matmul(ps[pn][:, half * 512:(half + 1) * 512],
                                       kt_ap[64 * g:64 * g + 64, :], rhs,
                                       start=True, stop=True)
                    return ins
                S.add("pe", fs, reads=[kt_tok, f"qt{qt_ring}_0", f"qt{qt_ring}_1"],
                      writes=[pn], tag=f"{tag}_sc{g}{c}")
                ptb = sb["pt"][:, blk * 1024:(blk + 1) * 1024]
                mc = meta_tile * 3 + c
                S.add("act", lambda e, pn=pn, ptb=ptb, mc=mc: e.activation(
                    ptb, ps[pn][:, :], AF.Exp, bias=sb["consts"][:, 0:1],
                    scale=sb["meta_v"][:, mc:mc + 1]),
                    reads=[pn, "consts", "meta_v"], writes=[f"pt{blk}"],
                    tag=f"{tag}_exp{g}{c}")
                if c == 2:
                    ptm = sb["pt"][:, (3 * g) * 1024:(3 * g + 2) * 1024]
                    S.add("dve", lambda e, ptm=ptm: e.tensor_tensor(
                        ptm, ptm, sb["band02"][:, :], op=ALU.mult),
                        reads=[f"pt{3 * g}", f"pt{3 * g + 1}", "band02"],
                        writes=[f"pt{3 * g}", f"pt{3 * g + 1}"], tag=f"{tag}_mb{g}")

    def emit_sc_group(g, qt_ring, kt_list, meta_tile, tag):
        for c in (0, 2, 1):
            emit_sc_one(g, c, qt_ring, kt_list, meta_tile, tag)

    def emit_tile_sc(qt_ring, kt_list, meta_tile, tag):
        emit_sc_group(0, qt_ring, kt_list, meta_tile, tag)
        emit_sc_group(1, qt_ring, kt_list, meta_tile, tag)

    def emit_pv_group(g, va_list, tag):
        for par, pn in ((0, "pA"), (1, "pB")):
            def fpv(e, g=g, par=par):
                ins = None
                for c in range(3):
                    va_ap, _ = va_list[c]
                    ins = e.matmul(ps["pAB"][0:65, par * 512:(par + 1) * 512],
                                   va_ap[:, g * 65:(g + 1) * 65],
                                   sb["pt"][:, (3 * g + BLKMAP[c]) * 1024 + par * 512:
                                            (3 * g + BLKMAP[c]) * 1024 + (par + 1) * 512],
                                   start=(c == 0), stop=(c == 2))
                return ins
            S.add("pe", fpv,
                  reads=[va_list[0][1], va_list[1][1], va_list[2][1],
                         f"pt{3 * g + 0}", f"pt{3 * g + 1}", f"pt{3 * g + 2}"],
                  writes=[pn], tag=f"{tag}_pv{g}{par}")
        S.add("act", lambda e: e.activation(
            sb["lnr"][0:1, :], ps["pAB"][64:65, :], AF.Ln),
            reads=["pA", "pB"], writes=["lnr0", "lnr1"], tag=f"{tag}_ln{g}")

        def fex(e):
            with nc.allow_low_precision(reason="bf16 softmax-normalizer operand"):
                return e.activation(sb["rcp"][0:1, :], sb["lnr"][0:1, :],
                                    AF.Exp, scale=-1.0)
        S.add("act", fex, reads=["lnr0", "lnr1"], writes=["rcp0", "rcp1"],
              tag=f"{tag}_rc{g}")

    def emit_rb(g, tag):
        def frb(e):
            ins = None
            for par in range(2):
                ins = e.matmul(ps["pT"][0:64, par * 512:(par + 1) * 512],
                               sb["ones"][0:1, 0:64],
                               sb["rcp"][0:1, par * 512:(par + 1) * 512],
                               start=True, stop=True)
            return ins
        S.add("pe", frb, reads=["rcp0", "rcp1", "ones"], writes=["pT"], tag=f"{tag}_rb{g}")
        S.add("act", lambda e: e.activation(sb["rsb"][:, :], ps["pT"][0:64, :], AF.Identity),
              reads=["pT"], writes=["rsb"], tag=f"{tag}_rev{g}")

    def emit_tnorm(g, tag):
        for par, pn in ((0, "pA"), (1, "pB")):
            dst = sb["attnT"][par * 64:par * 64 + 64, g * 512:(g + 1) * 512]
            S.add("dve", lambda e, par=par, dst=dst: e.tensor_tensor(
                dst, ps["pAB"][0:64, par * 512:(par + 1) * 512],
                sb["rsb"][0:64, par * 512:(par + 1) * 512],
                op=ALU.mult),
                reads=[pn, "rsb"], writes=["attnT"], tag=f"{tag}_tn{g}{par}")

    def emit_norm_group(g, tag):
        emit_rb(g, tag)
        emit_tnorm(g, tag)

    def emit_wo_epi(x_ap, x_tok, dst_kind, dst_info, tag):
        def fwo(e):
            ins = None
            for half in range(2):
                ins = e.matmul(ps["pAB"][:, half * 512:(half + 1) * 512],
                               sb["ones"][0:1, 0:128],
                               sb["bo"][0:1, half * 512:(half + 1) * 512],
                               start=True, stop=False)
            for dc in range(8):
                for half in range(2):
                    ins = e.matmul(ps["pAB"][:, half * 512:(half + 1) * 512],
                                   sb["attnT"][:, dc * 128:(dc + 1) * 128],
                                   sb["wo"][:, dc * 1024 + half * 512:dc * 1024 + (half + 1) * 512],
                                   start=False, stop=(dc == 7))
            return ins
        S.add("pe", fwo, reads=["attnT", "wo", "bo", "ones"], writes=["pA", "pB"],
              tag=f"{tag}_wo")

        if dst_kind == "out":
            row0 = dst_info
            xo_ap = sb["xo"][:, :]
            S.add("dve", lambda e, xo_ap=xo_ap, x_ap=x_ap: e.tensor_tensor(
                xo_ap, ps["pAB"][:, :], x_ap, op=ALU.add),
                reads=["pA", "pB", x_tok], writes=["xo0"], tag=f"{tag}_epi")
            dma("sp", "ost", out_shard[row0:row0 + 128, :], xo_ap,
                reads=["xo0"], writes=[f"ost{row0}"], tag=f"{tag}_st")
        else:
            tt = dst_info
            x2_ap = sb["x2own"][:, tt * 1024:(tt + 1) * 1024]
            S.add("dve", lambda e, x2_ap=x2_ap, x_ap=x_ap: e.tensor_tensor(
                x2_ap, ps["pAB"][:, :], x_ap, op=ALU.add),
                reads=["pA", "pB", x_tok], writes=[f"x2own{tt}"], tag=f"{tag}_epi")
            dma("sp", "a3c", ag3_in.ap()[tt * 128:(tt + 1) * 128, :], x2_ap,
                reads=[f"x2own{tt}"], writes=[f"ag3in{tt}"], tag=f"{tag}_st")

    def emit_tile_pv(va_list, x_ap, x_tok, dst_kind, dst_info, tag):
        emit_pv_group(0, va_list, tag)
        emit_norm_group(0, tag)
        emit_pv_group(1, va_list, tag)
        emit_norm_group(1, tag)
        emit_wo_epi(x_ap, x_tok, dst_kind, dst_info, tag)

    def emit_tile(qt_ring, kt_list, va_list, meta_tile, x_ap, x_tok, dst_kind, dst_info, tag):
        emit_tile_sc(qt_ring, kt_list, meta_tile, tag)
        emit_tile_pv(va_list, x_ap, x_tok, dst_kind, dst_info, tag)

    def emit_ffn_head(call, t, src_dram, idx_sb_name, tag):
        ring = t % 2
        xg_ap = sb["xg"][:, :]
        xg_tok = "xg0"
        S.add("pool", lambda e: e.indirect_dma_start(
            out=xg_ap, out_offset=None, in_=src_dram,
            in_offset=IOff(ap=sb[idx_sb_name][:, t:t + 1], axis=0)),
            reads=[idx_sb_name, f"agsrc{call}"], writes=[xg_tok], kind="dma",
            ch="xg0", tag=f"{tag}_g")
        fln_ap = sb["xln"][:, ring * 1024:(ring + 1) * 1024]
        flt_ap = sb["xlt"][:, ring * 1024:(ring + 1) * 1024]
        emit_ln(xg_ap, xg_tok, fln_ap, f"xln{ring}", 2 + ring, tag + "_ln")
        emit_tr(fln_ap, f"xln{ring}", flt_ap, f"xlt{ring}", tag)
        flt_tok = f"xlt{ring}"

        for half, pn in ((0, "s0"), (1, "s1")):
            def fg(e, half=half, pn=pn, flt_ap=flt_ap):
                ins = None
                for q in range(2):
                    col = half * 1024 + q * 512
                    ins = e.matmul(ps[pn][:, q * 512:(q + 1) * 512],
                                   sb["ones"][0:1, 0:128],
                                   sb["gub"][0:1, col:col + 512], start=True, stop=False)
                    for dc in range(8):
                        ins = e.matmul(ps[pn][:, q * 512:(q + 1) * 512],
                                       flt_ap[:, dc * 128:(dc + 1) * 128],
                                       sb["gu"][:, dc * 2048 + col:dc * 2048 + col + 512],
                                       start=False, stop=(dc == 7))
                return ins
            S.add("pe", fg, reads=[flt_tok, "gu", "gub", "ones"], writes=[pn],
                  tag=f"{tag}_gu{half}")
    def emit_ffn_head2(call, t, tag):
        A = sb["swz"][:, 0:1024]
        C = sb["swz"][:, 2048:3072]
        S.add("dve", lambda e, A=A: e.tensor_scalar(A, ps["s0"][:, :], 7.0, None, op0=ALU.min),
              reads=["s0"], writes=["swzA"], tag=f"{tag}_gmin")
        S.add("dve", lambda e, C=C: e.tensor_scalar(C, ps["s1"][:, :], 7.0, -7.0,
                                                    op0=ALU.min, op1=ALU.max),
              reads=["s1"], writes=["swzC"], tag=f"{tag}_uclip")

    def emit_ffn_tail1(call, t, tag):
        A = sb["swz"][:, 0:1024]
        B = sb["swz"][:, 1024:2048]
        C = sb["swz"][:, 2048:3072]
        S.add("act", lambda e, A=A, B=B: e.activation(B, A, AF.Exp, scale=-1.702),
              reads=["swzA"], writes=["swzB"], tag=f"{tag}_exp")
        S.add("act", lambda e, B=B: e.activation(B, B, AF.Ln, bias=sb["consts"][:, 2:3]),
              reads=["swzB", "consts"], writes=["swzB"], tag=f"{tag}_ln1p")
        S.add("act", lambda e, B=B: e.activation(B, B, AF.Exp, scale=-1.0),
              reads=["swzB"], writes=["swzB"], tag=f"{tag}_sig")
        S.add("dve", lambda e, A=A, B=B: e.tensor_tensor(A, A, B, op=ALU.mult),
              reads=["swzA", "swzB"], writes=["swzA"], tag=f"{tag}_glu")
        S.add("dve", lambda e, A=A, C=C: e.scalar_tensor_tensor(
            sb["inter"][:, :], C, 1.0, A, op0=ALU.add, op1=ALU.mult),
            reads=["swzA", "swzC"], writes=["inter"], tag=f"{tag}_inter")

    def emit_ffn_tail2(call, t, contrib, tag):
        emit_tr(sb["inter"][:, :], "inter", sb["itT"][:, :], "itT", tag + "_it")

        def fd(e):
            ins = None
            for half in range(2):
                ins = e.matmul(ps["pAB"][:, half * 512:(half + 1) * 512],
                               sb["ones"][0:1, 0:128],
                               sb["db"][0:1, half * 512:(half + 1) * 512],
                               start=True, stop=False)
            for ic in range(8):
                for half in range(2):
                    ins = e.matmul(ps["pAB"][:, half * 512:(half + 1) * 512],
                                   sb["itT"][:, ic * 128:(ic + 1) * 128],
                                   sb["dw"][:, ic * 1024 + half * 512:ic * 1024 + (half + 1) * 512],
                                   start=False, stop=(ic == 7))
            return ins
        S.add("pe", fd, reads=["itT", "dw", "db", "ones"], writes=["pA", "pB"],
              tag=f"{tag}_dn")
        S.add("act", lambda e: e.activation(sb["inter"][:, :], ps["pAB"][:, :], AF.Identity),
              reads=["pA", "pB"], writes=["inter"], tag=f"{tag}_ycev")
        dma("sp", contrib[0], contrib[1][t * 128:(t + 1) * 128, :], sb["inter"][:, :],
            reads=["inter"], writes=[f"{contrib[2]}{t}"], tag=f"{tag}_yst")

    def emit_wcompute():
        S.add("dve", lambda e: e.tensor_copy(sb["idxf"][:, :], sb["idxo"][:, :]),
              reads=["idxo"], writes=["idxf"], tag="w_cast")
        for tt in range(2):
            for k in range(2):
                col = 2 * tt + k
                acc_a, acc_b = 4 + 2 * col, 5 + 2 * col
                for ei in range(8):
                    eq = 14
                    S.add("dve", lambda e, col=col, ei=ei, eq=eq: e.tensor_scalar(
                        sb["wcell"][:, eq:eq + 1], sb["idxf"][:, col:col + 1],
                        float(ei), None, op0=ALU.is_equal),
                        reads=["idxf"], writes=["weq"], tag=f"w_eq{col}_{ei}")
                    rcol = 8 * tt + ei
                    if ei == 0:
                        S.add("dve", lambda e, acc_a=acc_a, eq=eq, rcol=rcol: e.tensor_tensor(
                            sb["wcell"][:, acc_a:acc_a + 1], sb["wcell"][:, eq:eq + 1],
                            sb["rw"][:, rcol:rcol + 1], op=ALU.mult),
                            reads=["weq", "rw"], writes=[f"wacc{col}a"], tag=f"w_a{col}_{ei}")
                    else:
                        src_t, dst_t = ((f"wacc{col}a", f"wacc{col}b") if ei % 2 == 1
                                        else (f"wacc{col}b", f"wacc{col}a"))
                        src_c = acc_a if ei % 2 == 1 else acc_b
                        dst_c = acc_b if ei % 2 == 1 else acc_a
                        S.add("dve", lambda e, s=src_c, d=dst_c, eq=eq, rcol=rcol:
                              e.scalar_tensor_tensor(
                                  sb["wcell"][:, d:d + 1], sb["wcell"][:, eq:eq + 1],
                                  sb["rw"][:, rcol:rcol + 1], sb["wcell"][:, s:s + 1],
                                  op0=ALU.mult, op1=ALU.add),
                              reads=["weq", "rw", src_t], writes=[dst_t], tag=f"w_a{col}_{ei}")
                final_c = acc_b if 7 % 2 == 1 else acc_a
                final_t = f"wacc{col}b" if 7 % 2 == 1 else f"wacc{col}a"
                S.add("dve", lambda e, col=col, final_c=final_c: e.tensor_scalar(
                    sb["wcell"][:, col:col + 1], sb["wcell"][:, final_c:final_c + 1],
                    0.5, None, op0=ALU.mult),
                    reads=[final_t], writes=[f"w{col}"], tag=f"w_f{col}")

    def emit_combine(agout, xbase_sb, xbase_tok_fmt, dst_kind, ygroup, tag):
        for tt in range(2):
            A = sb["swz"][:, tt * 1024:(tt + 1) * 1024]
            At = "swzA" if tt == 0 else "swzB"
            y0 = sb["ybuf"][:, :]
            y1 = sb["ybuf2"][:, :]
            S.add("pool", lambda e, col=tt: e.indirect_dma_start(
                out=sb["ybuf"][:, :], out_offset=None,
                in_=agout, in_offset=IOff(ap=sb["icomb"][:, col:col + 1], axis=0)),
                reads=["icomb", f"agdone{tag}"], writes=["ybg0"], kind="dma",
                ch=f"y{ygroup}0", tag=f"{tag}_yg{tt}0")
            S.add("pool", lambda e, col=2 + tt: e.indirect_dma_start(
                out=sb["ybuf2"][:, :], out_offset=None,
                in_=agout, in_offset=IOff(ap=sb["icomb"][:, col:col + 1], axis=0)),
                reads=["icomb", f"agdone{tag}"], writes=["ybg1"], kind="dma",
                ch=f"y{ygroup}0", tag=f"{tag}_yg{tt}1")
            xb = xbase_sb[:, tt * 1024:(tt + 1) * 1024]
            S.add("dve", lambda e, A=A, y0=y0, xb=xb, tt=tt: e.scalar_tensor_tensor(
                A, y0, sb["wcell"][:, 2 * tt:2 * tt + 1], xb, op0=ALU.mult, op1=ALU.add),
                reads=["ybg0", f"w{2 * tt}", xbase_tok_fmt.format(tt)],
                writes=[At], tag=f"{tag}_c1_{tt}")
            if dst_kind == "x1":
                dst_ap = sb["x1own"][:, tt * 1024:(tt + 1) * 1024]
                dst_tok = f"x1own{tt}"
            else:
                dst_ap = sb["swz"][:, 2048:3072]
                dst_tok = "swzC"
            S.add("dve", lambda e, dst_ap=dst_ap, y1=y1, A=A, tt=tt: e.scalar_tensor_tensor(
                dst_ap, y1, sb["wcell"][:, 2 * tt + 1:2 * tt + 2], A,
                op0=ALU.mult, op1=ALU.add),
                reads=["ybg1", f"w{2 * tt + 1}", At], writes=[dst_tok],
                tag=f"{tag}_c2_{tt}")
            if dst_kind == "x1":
                dma("sp", "a2c", ag2_in.ap()[tt * 128:(tt + 1) * 128, :], dst_ap,
                    reads=[dst_tok], writes=[f"ag2in{tt}"], tag=f"{tag}_st{tt}")
            else:
                dma("sp", "ost", out_shard[3840 + tt * 128:3840 + (tt + 1) * 128, :], dst_ap,
                    reads=[dst_tok], writes=["outhbm"], tag=f"{tag}_st{tt}")

    S.add("pool", lambda e: e.engine_nop(), reads=[], writes=["agsrc1"], tag="src1")

    def ffn_h1(call, t):
        if call == 1:
            emit_ffn_head(1, t, dri["x_region"], "iffn1", f"f1t{t}")
        else:
            if t == 0:
                S.add("pool", lambda e: e.engine_nop(), reads=["agdoneC3"],
                      writes=["agsrc3"], tag="src3")
            emit_ffn_head(3, t, ag3_out.ap(), "iffn3", f"f3t{t}")

    def ffn_t2(call, t):
        if call == 1:
            emit_ffn_tail2(1, t, ("a1c", ag1_in.ap(), "ag1in"), f"f1t{t}")
            if t == padt - 1:
                S.add("pool", lambda e: e.collective_compute(
                    "AllGather", ALU.bypass, replica_groups=[list(range(NC))],
                    ins=[ag1_in.ap().opt()], outs=[ag1_out.ap().opt()]),
                    reads=[f"ag1in{tt}" for tt in range(padt)], writes=["agdoneC1"],
                    kind="coll", tag="AG1")
        else:
            emit_ffn_tail2(3, t, ("a4c", ag4_in.ap(), "ag4in"), f"f3t{t}")
            if t == padt - 1:
                S.add("pool", lambda e: e.collective_compute(
                    "AllGather", ALU.bypass, replica_groups=[list(range(NC))],
                    ins=[ag4_in.ap().opt()], outs=[ag4_out.ap().opt()]),
                    reads=[f"ag4in{tt}" for tt in range(padt)], writes=["agdoneC4"],
                    kind="coll", tag="AG4")

    def x_load(slot, ring, tag):
        dma("sp", f"x{ring}", sb["xr"][:, ring * 1024:(ring + 1) * 1024],
            dri["x_shard"][slot * 128:(slot + 1) * 128, :],
            reads=[], writes=[f"xr{ring}"], tag=tag)

    def kvA_front(slot, ring_x, tag):
        x_ap = sb["xr"][:, ring_x * 1024:(ring_x + 1) * 1024]
        emit_kv_front(x_ap, f"xr{ring_x}", slot % 2, tag, prestats=True)

    def kvA_back(slot, with_q, tag):
        pos = (slot + 1) % RING_KV
        kt_ap = sb["ktA"][:, pos * 128:(pos + 1) * 128]
        va_ap = sb["vaA"][:, pos * 130:pos * 130 + 130]
        q = "pair" if (with_q and slot % 2 == 1) else False
        emit_kv_back(slot % 2, kt_ap, f"ktA{pos}", va_ap, f"vaA{pos}",
                     1 + slot, q, (slot % 3), tag)

    def kvA(slot, ring_x, with_q, tag, x_ap=None, x_tok=None, prestats=False):
        pos = (slot + 1) % RING_KV
        kt_ap = sb["ktA"][:, pos * 128:(pos + 1) * 128]
        va_ap = sb["vaA"][:, pos * 130:pos * 130 + 130]
        if x_ap is None:
            x_ap = sb["xr"][:, ring_x * 1024:(ring_x + 1) * 1024]
            x_tok = f"xr{ring_x}"
        emit_kv(x_ap, x_tok, slot % 2, kt_ap, f"ktA{pos}", va_ap, f"vaA{pos}",
                1 + slot, bool(with_q), (slot % 3), tag, prestats=prestats)

    def ktva(slot):
        pos = (slot + 1) % RING_KV
        return ((sb["ktA"][:, pos * 128:(pos + 1) * 128], f"ktA{pos}"),
                (sb["vaA"][:, pos * 130:pos * 130 + 130], f"vaA{pos}"))

    def tileA(i, tag):
        kts, vas = [], []
        for c in range(3):
            kt, va = ktva(i - 1 + c)
            kts.append(kt)
            vas.append(va)
        x_ap = sb["xr"][:, (i % RING_X) * 1024:((i % RING_X) + 1) * 1024]
        emit_tile(i % 3, kts, vas, i, x_ap, f"xr{i % RING_X}", "out",
                  i * 128, tag)

    def tileA_sc(i, tag):
        kts = []
        for c in range(3):
            kt, va = ktva(i - 1 + c)
            kts.append(kt)
        emit_tile_sc(i % 3, kts, i, tag)

    def tile_kts(i):
        kts = []
        for c in range(3):
            kt, va = ktva(i - 1 + c)
            kts.append(kt)
        return kts

    def tile_vas(i):
        vas = []
        for c in range(3):
            kt, va = ktva(i - 1 + c)
            vas.append(va)
        return vas

    def tileA_pvmm(i, tag):
        x_ap = sb["xr"][:, (i % RING_X) * 1024:((i % RING_X) + 1) * 1024]
        S.add("dve", lambda e, x_ap=x_ap: e.tensor_copy(sb["xo"][:, :], x_ap),
              reads=[f"xr{i % RING_X}"], writes=["xo0"], tag=f"{tag}_xcp")
        emit_pv_group(0, tile_vas(i), tag)

    def tileA_tail(i, tag):
        vas = tile_vas(i)
        emit_norm_group(0, tag)
        emit_pv_group(1, vas, tag)
        emit_norm_group(1, tag)
        emit_wo_epi(sb["xo"][:, :], "xo0", "out", i * 128, tag)

    def statsA(slot):
        x_ap = sb["xr"][:, (slot % RING_X) * 1024:((slot % RING_X) + 1) * 1024]
        emit_ln_stats(x_ap, f"xr{slot % RING_X}", slot % 2, f"st{slot}")

    x_load(0, 0, "x0")
    statsA(0)
    x_load(1, 1, "x1")
    statsA(1)
    kvA_front(0, 0, "kv0")
    kvA_back(0, False, "kv0")
    x_load(2, 2, "x2")
    statsA(2)
    kvA_front(1, 1, "kv1")
    kvA_back(1, True, "kv1")
    kvA_front(2, 2, "kv2")
    kvA_back(2, True, "kv2")
    stats_done = 2

    after = {4: "comb1", 7: "segB", 10: "ffn3", 13: "comb3"}

    def ffn_block(call):
        ffn_h1(call, 0)
        emit_ffn_head2(call, 0, f"f{call}t0")
        for t in range(padt):
            if t + 1 < padt:
                ffn_h1(call, t + 1)
            emit_ffn_tail1(call, t, f"f{call}t{t}")
            if t + 1 < padt:
                emit_ffn_head2(call, t + 1, f"f{call}t{t + 1}")
            ffn_t2(call, t)

    ffn_block(1)

    kv_done = 2
    pending = None
    xlt_dirty = [True]
    load_done = 2

    def ensure_load(k):
        nonlocal load_done
        k = min(k, NCH_A - 1)
        while load_done < k:
            load_done += 1
            x_load(load_done, load_done % RING_X, f"x{load_done}")

    def refront(slot):
        x_ap = sb["xr"][:, (slot % RING_X) * 1024:((slot % RING_X) + 1) * 1024]
        x_load(slot, slot % RING_X, f"xr{slot}")
        emit_kv_front(x_ap, f"xr{slot % RING_X}", 2 + slot % 2, f"rf{slot}",
                      prestats=False)
    for i in range(1, NT_A):
        def advance_kv():
            nonlocal kv_done, stats_done
            k = kv_done + 1
            if stats_done < k:
                ensure_load(k)
                statsA(k)
                stats_done = k
            kvA_front(k, k % RING_X, f"kv{k}")
            if k % 2 == 1 and xlt_dirty[0]:
                refront(k - 1)
                xlt_dirty[0] = False
            kvA_back(k, k <= 29, f"kv{k}")
            kv_done = k
        while kv_done < i + 1:
            advance_kv()
        do_kv = (i != 7 and kv_done < min(i + 2, NCH_A - 1))
        if do_kv:
            k = i + 2
            if stats_done < k:
                ensure_load(k)
                statsA(k)
                stats_done = k
            kvA_front(k, k % RING_X, f"kv{k}")
            if k % 2 == 1 and xlt_dirty[0]:
                refront(k - 1)
                xlt_dirty[0] = False
        k3 = stats_done + 1
        if k3 == i + 3 and load_done >= k3 and k3 <= NCH_A - 1:
            statsA(k3)
            stats_done = k3
        kts_i = tile_kts(i)
        emit_sc_group(0, i % 3, kts_i, i, f"tA{i}")
        if pending is not None:
            tileA_tail(pending, f"tA{pending}")
            pending = None
        emit_sc_group(1, i % 3, kts_i, i, f"tA{i}")
        if do_kv:
            k = i + 2
            kvA_back(k, k <= 29, f"kv{k}")
            kv_done = k
        tileA_pvmm(i, f"tA{i}")
        pending = i
        if stats_done < min(kv_done + 1, NCH_A - 1):
            k2 = stats_done + 1
            ensure_load(k2)
            statsA(k2)
            stats_done = k2
        ensure_load(stats_done + 1)
        ph = after.get(i)
        if ph == "ffn3":
            if pending is not None:
                tileA_tail(pending, f"tA{pending}")
                pending = None
            ffn_block(3)
            xlt_dirty[0] = True
        elif ph == "comb1":
            emit_wcompute()
            S.add("pool", lambda e: e.engine_nop(), reads=["agdoneC1"],
                  writes=["agdonecomb1"], tag="agw1")
            for tt in range(2):
                dma("sp", "xoc", sb["x1own"][:, tt * 1024:(tt + 1) * 1024],
                    dri["x_own"][tt * 128:(tt + 1) * 128, :],
                    reads=[], writes=[f"x1own{tt}"], tag=f"xown{tt}")
            emit_combine(ag1_out.ap(), sb["x1own"], "x1own{}", "x1", 0, "comb1")
            dma("sp", "a2c", ag2_in.ap()[256:512, :], dri["ag2_extra"],
                reads=[], writes=["ag2inx"], tag="ag2extra")
            S.add("pool", lambda e: e.collective_compute(
                "AllGather", ALU.bypass, replica_groups=[list(range(NC))],
                ins=[ag2_in.ap().opt()], outs=[ag2_out.ap().opt()]),
                reads=["ag2in0", "ag2in1", "ag2inx"], writes=["agdoneC2"],
                kind="coll", tag="AG2")
        elif ph == "segB":
            S.add("pool", lambda e: e.memset(sb["xg"][:, :], 0.0),
                  reads=[], writes=["xg0"], tag="zb0")
            if pending is not None:
                tileA_tail(pending, f"tA{pending}")
                pending = None
            sring = i % 3
            S.add("pool", lambda e: e.indirect_dma_start(
                out=sb["xg"][:, :], out_offset=None, in_=ag2_out.ap(),
                in_offset=IOff(ap=sb["ib0"][:, 0:1], axis=0)),
                reads=["ib0", "agdoneC2", "xg0"], writes=["xg0"], kind="dma",
                ch="gb0", tag="gb0")
            S.add("pool", lambda e: e.indirect_dma_start(
                out=sb["xb3"][:, :], out_offset=None, in_=ag2_out.ap(),
                in_offset=IOff(ap=sb["ib3"][:, 0:1], axis=0)),
                reads=["ib3", "agdoneC2", "xb3"], writes=["xb3"], kind="dma",
                ch="gb3", tag="gb3")
            bx = [(sb["xg"][:, :], "xg0"),
                  (sb["x1own"][:, 0:1024], "x1own0"),
                  (sb["x1own"][:, 1024:2048], "x1own1"),
                  (sb["xb3"][:, :], "xb3")]
            for j in range(4):
                kt_ap = sb["ktB"][:, j * 128:(j + 1) * 128]
                va_ap = sb["vaB"][:, j * 130:j * 130 + 130]
                emit_kv(bx[j][0], bx[j][1], 2 + j % 2, kt_ap, f"ktB{j}", va_ap, f"vaB{j}",
                        32 + j, j == 1, sring, f"kvB{j}")
            for bt in range(2):
                if bt == 1:
                    emit_kv(bx[2][0], bx[2][1], 2, None, None, None, None,
                            34, True, sring, "kvB2q", with_kv=False)
                kts = [(sb["ktB"][:, (bt + c) * 128:(bt + c + 1) * 128], f"ktB{bt + c}")
                       for c in range(3)]
                vas = [(sb["vaB"][:, (bt + c) * 130:(bt + c) * 130 + 130], f"vaB{bt + c}")
                       for c in range(3)]
                emit_tile(sring, kts, vas, 30 + bt,
                          sb["x1own"][:, bt * 1024:(bt + 1) * 1024], f"x1own{bt}",
                          "x2", bt, f"tB{bt}")
            S.add("pool", lambda e: e.collective_compute(
                "AllGather", ALU.bypass, replica_groups=[list(range(NC))],
                ins=[ag3_in.ap().opt()], outs=[ag3_out.ap().opt()]),
                reads=["ag3in0", "ag3in1"], writes=["agdoneC3"],
                kind="coll", tag="AG3")
            xlt_dirty[0] = True
        elif ph == "comb3":
            S.add("pool", lambda e: e.engine_nop(), reads=["agdoneC4"],
                  writes=["agdonecomb3"], tag="agw4")
            emit_combine(ag4_out.ap(), sb["x2own"], "x2own{}", "out", 1, "comb3")

    if pending is not None:
        tileA_tail(pending, f"tA{pending}")
        pending = None

    S.add("pool", lambda e: e.indirect_dma_start(
        out=sb["xg"][:, :], out_offset=None, in_=ag2_out.ap(),
        in_offset=IOff(ap=sb["ia0"][:, 0:1], axis=0)),
        reads=["ia0", "agdoneC2", "xg0"], writes=["xg0"], kind="dma", ch="ga0", tag="ga0")
    emit_kv(sb["xg"][:, :], "xg0", 2, sb["ktA"][:, 0:128], "ktA0",
            sb["vaA"][:, 0:130], "vaA0", 0, False, 0, "kvm1")
    x_load(0, 0, "x0r")
    emit_kv(sb["xr"][:, 0:1024], "xr0", 3, sb["ktA"][:, 128:256], "ktA1",
            sb["vaA"][:, 130:260], "vaA1", 1, True, 0, "kv0r")
    x_load(1, 1, "x1r")
    emit_kv(sb["xr"][:, 1024:2048], "xr1", 2, sb["ktA"][:, 256:384], "ktA2",
            sb["vaA"][:, 260:390], "vaA2", 2, False, 0, "kv1r")
    tileA(0, "tA0")

    S.add("sp", None, reads=[], writes=[], tag="fence")

    with es:
        with nc.Block() as block:
            eng_sem, ch_sem, coll_sems, emit_stream = _materialize(nc, S, es)

            @block.tensor
            def _(t):
                emit_stream(t, "pe")

            @block.scalar
            def _(s):
                emit_stream(s, "act")

            @block.vector
            def _(v):
                emit_stream(v, "dve")

            @block.gpsimd
            def _(g):
                emit_stream(g, "pool")

            @block.sync
            def _(sy):
                emit_stream(sy, "sp")

    return nc



def kernel(**inputs):
    in_maps, padt, base_ids = _prep(inputs)
    nc = build_program(padt)
    res = run_bass_kernel_spmd(nc, in_maps, core_ids=list(range(NC)))
    out = np.empty((TOK, D), dtype=np.float32)
    for c in range(NC):
        shard = res.results[c]["out_shard"]
        base = base_ids[c]
        for i in range(NT_A):
            a, b = _chunk_rows(base + i)
            out[a:b] = shard[i * 128:(i + 1) * 128]
        out[256 * c:256 * c + 256] = shard[3840:4096]
    return out.reshape(BATCH, SEQ, D)


if __name__ == "__main__":
    import sys
    if "--build" in sys.argv:
        nc = build_program(5)
        n = sum(1 for _ in nc.all_instructions())
        print(f"build OK, {n} instructions")
